# version 06
# baseline (speedup 1.0000x reference)
import numpy as np
import ml_dtypes
from contextlib import ExitStack

import concourse.bass as bass
import concourse.mybir as mybir
from concourse.bass import IndirectOffsetOnAxis
from concourse.bass_utils import run_bass_kernel_spmd

F32 = mybir.dt.float32
BF16 = mybir.dt.bfloat16
I32 = mybir.dt.int32
U32 = mybir.dt.uint32
AF = mybir.ActivationFunctionType
ALU = mybir.AluOpType
AXX = mybir.AxisListType.X
BF = ml_dtypes.bfloat16

NCORES = 8
T = 4096
DM = 1024
NSTRIP = T // 128
EPS = 1e-5
NEGBIG = -3.402823466e38
OOB = 1 << 30

ENGINES = ("sync", "tensor", "vector", "scalar", "gpsimd")
ts = bass.ts


class Sem:
    __slots__ = ("h", "n", "name", "sw")
    def __init__(self, h, name, sw=False):
        self.h, self.n, self.name, self.sw = h, 0, name, sw
    def inc(self, k=1):
        self.n += k
        return self.n


class Prog:
    def __init__(self, nc, ctx):
        self.nc, self.ctx = nc, ctx
        self.ops = {e: [] for e in ENGINES}
        self.labels = {e: [] for e in ENGINES}
        self.stage_label = None
        self._sems = {}
        self._alias = None
        self.barsc = None

    def _pool(self, name, sw=False):
        if name not in self._sems:
            h = self.ctx.enter_context(self.nc.semaphore(name))
            self._sems[name] = Sem(h, name, sw)
        assert self._sems[name].sw == sw, name
        return self._sems[name]

    def S(self, name, sw=False):
        if self._alias is None:
            return self._pool(name, sw)
        if name not in self._alias:
            k = sum(1 for s in self._alias.values() if s.sw == sw)
            self._alias[name] = self._pool(f"gs{k}" if sw else f"g{k}", sw)
        s = self._alias[name]
        assert s.sw == sw, f"sem {name} class mismatch"
        return s

    def new_stage(self):
        self._alias = {}

    def barrier(self):
        bs = self._pool("_bar")
        def ft(e):
            return e.nop()
        def fv(e):
            e.drain()
            return e.nop()
        def fs(e):
            e.drain()
            return e.activation(self.barsc[:, :], self.barsc[:, :], AF.Relu)
        def fg(e):
            e.drain()
            return e.engine_nop()
        self.op("tensor", ft, sem=bs, drain=False)
        self.op("vector", fv, sem=bs, drain=False)
        self.op("scalar", fs, sem=bs, drain=False)
        self.op("gpsimd", fg, sem=bs, drain=False)
        tgt = bs.n
        self.waits(ENGINES, [(bs, tgt)])

    def add(self, eng, fn):
        self.ops[eng].append(fn)
        self.labels[eng].append(self.stage_label)

    def wait(self, eng, sem, val):
        if val > 0:
            self.add(eng, lambda e, _s=sem.h, _v=val: e.wait_ge(_s, _v))

    def waits(self, engs, pairs):
        for eng in engs:
            for sem, val in pairs:
                self.wait(eng, sem, val)

    def dma(self, eng, sem, out, in_, **kw):
        if self._alias is not None:
            assert sem.sw == (eng == "gpsimd"), f"sem {sem.name} dma class"
        v = sem.inc(16)
        def f(e, _o=out, _i=in_, _s=sem.h, _kw=kw):
            e.dma_start(out=_o, in_=_i, **_kw).then_inc(_s, 16)
        self.add(eng, f)
        return v

    def gather(self, sem, out, in_dram, idx_ap, bound):
        if self._alias is not None:
            assert sem.sw, f"sem {sem.name} gather class"
        v = sem.inc(16)
        def f(e, _o=out, _i=in_dram, _x=idx_ap, _b=bound, _s=sem.h):
            if _b is None:
                e.indirect_dma_start(_o, None, _i, IndirectOffsetOnAxis(_x, axis=0)
                                     ).then_inc(_s, 16)
            else:
                e.indirect_dma_start(_o, None, _i, IndirectOffsetOnAxis(_x, axis=0),
                                     bounds_check=_b, oob_is_err=False).then_inc(_s, 16)
        self.add("gpsimd", f)
        return v

    def scatter(self, sem, out_dram, idx_ap, in_, bound, op=ALU.bypass):
        if self._alias is not None:
            assert sem.sw, f"sem {sem.name} scatter class"
        v = sem.inc(16)
        def f(e, _o=out_dram, _i=in_, _x=idx_ap, _b=bound, _s=sem.h, _op=op):
            if _b is None:
                e.indirect_dma_start(_o, IndirectOffsetOnAxis(_x, axis=0), _i, None,
                                     compute_op=_op).then_inc(_s, 16)
            else:
                e.indirect_dma_start(_o, IndirectOffsetOnAxis(_x, axis=0), _i, None,
                                     bounds_check=_b, oob_is_err=False,
                                     compute_op=_op).then_inc(_s, 16)
        self.add("gpsimd", f)
        return v

    def mm(self, out, lhsT, rhs, start, stop, sem=None):
        v = sem.inc(1) if sem is not None else None
        def f(e, _o=out, _l=lhsT, _r=rhs, _st=start, _sp=stop, _s=sem):
            i = e.matmul(_o, _l, _r, start=_st, stop=_sp)
            if _s is not None:
                i.then_inc(_s.h, 1)
        self.add("tensor", f)
        return v

    def tp(self, out, in_, ident, sem=None):
        v = sem.inc(1) if sem is not None else None
        def f(e, _o=out, _i=in_, _id=ident, _s=sem):
            i = e.transpose(_o, _i, _id)
            if _s is not None:
                i.then_inc(_s.h, 1)
        self.add("tensor", f)
        return v

    def tnop(self, sem):
        v = sem.inc(1)
        self.add("tensor", lambda e, _s=sem.h: e.nop().then_inc(_s, 1))
        return v

    def opt(self, eng, fn, drain=True):
        s = self.S("_tk_" + eng)
        v = self.op(eng, fn, sem=s, drain=drain)
        return (s, v)

    def mmt(self, out, lhsT, rhs, start, stop):
        s = self.S("_tk_mm")
        v = self.mm(out, lhsT, rhs, start, stop, sem=s)
        return (s, v)

    def op(self, eng, fn, sem=None, drain=True):
        v = sem.inc(1) if sem is not None else None
        def f(e, _fn=fn, _s=sem, _d=drain):
            i = _fn(e)
            if _s is not None:
                i.then_inc(_s.h, 1)
            if _d:
                e.drain()
        self.add(eng, f)
        return v

    def _replay(self, eng, e):
        fns, labs = self.ops[eng], self.labels[eng]
        i, n = 0, len(fns)
        while i < n:
            lab = labs[i]
            j = i
            while j < n and labs[j] == lab:
                j += 1
            if lab is None:
                for k in range(i, j):
                    fns[k](e)
            else:
                with self.nc.named_scope(lab):
                    for k in range(i, j):
                        fns[k](e)
            i = j

    def build(self):
        with self.nc.Block() as block:
            @block.sync
            def _(e):
                self._replay("sync", e)
            @block.tensor
            def _(e):
                self._replay("tensor", e)
            @block.vector
            def _(e):
                self._replay("vector", e)
            @block.scalar
            def _(e):
                self._replay("scalar", e)
            @block.gpsimd
            def _(e):
                self._replay("gpsimd", e)


class PBank:
    def __init__(self, P, ctx, nc, name, shape, dt=F32):
        self.P = P
        self.ap = ctx.enter_context(nc.psum_tensor(shape, dt))
        self.full = P.S(name + "_fu")
        self.free = P.S(name + "_fr")
        self.fills = self.full.n
        self._free_base = self.free.n
        self.freed_target = self.free.n

    def mm_group(self, parts, extra_sem=None):
        P = self.P
        if self.freed_target > self._free_base:
            P.wait("tensor", self.free, self.freed_target)
        n = len(parts)
        for i, (osl, lhsT, rhs) in enumerate(parts):
            out = self.ap[:, :] if osl is None else osl
            P.mm(out, lhsT, rhs, i == 0, i == n - 1,
                 sem=self.full if i == n - 1 else None)
        if extra_sem is not None:
            self.P.tnop(extra_sem)
        self.fills += 1

    def tp_into(self, in_, ident):
        P = self.P
        if self.freed_target > self._free_base:
            P.wait("tensor", self.free, self.freed_target)
        P.tp(self.ap[0:in_.shape[-1], 0:in_.shape[0]], in_, ident, sem=self.full)
        self.fills += 1

    def acc_start(self):
        if self.freed_target > self._free_base:
            self.P.wait("tensor", self.free, self.freed_target)

    def acc_mm(self, out_slice, lhsT, rhs, first, last):
        self.P.mm(self.ap[:, :] if out_slice is None else out_slice, lhsT, rhs,
                  first, last, sem=self.full if last else None)
        if last:
            self.fills += 1

    def consume(self, eng, fn):
        P = self.P
        P.wait(eng, self.full, self.fills)
        self.freed_target += 1
        P.op(eng, lambda e, _f=fn, _a=self.ap: _f(e, _a), sem=self.free)
        return self.free, self.freed_target




class DStream:
    def __init__(self, P, name, eng, depth=2):
        self.P, self.eng, self.depth = P, eng, depth
        self.sems = [P.S(f"{name}_{i}", sw=(eng == "gpsimd")) for i in range(depth)]
        self.last = [None] * depth
        self.k = 0

    def _next(self):
        i = self.k % self.depth
        s = self.sems[i]
        if self.last[i] is not None:
            self.P.waits((self.eng,), [self.last[i]])
        self.k += 1
        return s, i

    def issue(self, out, in_, **kw):
        s, i = self._next()
        v = self.P.dma(self.eng, s, out, in_, **kw)
        self.last[i] = (s, v)
        return (s, v)

    def issue_gather(self, out, in_dram, idx_ap, bound):
        s, i = self._next()
        v = self.P.gather(s, out, in_dram, idx_ap, bound)
        self.last[i] = (s, v)
        return (s, v)

    def issue_scatter(self, out_dram, idx_ap, in_, bound, op=ALU.bypass):
        s, i = self._next()
        v = self.P.scatter(s, out_dram, idx_ap, in_, bound, op=op)
        self.last[i] = (s, v)
        return (s, v)

    def tokens(self):
        return [(s, s.n) for s in self.sems]


class LNKit:
    def __init__(self, P, sb, tag=""):
        self.P = P
        self.tag = tag
        self.sum = sb(f"ln_sum{tag}", [128, 1], F32)
        self.ssq = sb(f"ln_ssq{tag}", [128, 1], F32)
        self.mu = sb(f"ln_mu{tag}", [128, 1], F32)
        self.mu2 = sb(f"ln_mu2{tag}", [128, 1], F32)
        self.var = sb(f"ln_var{tag}", [128, 1], F32)
        self.rstd = sb(f"ln_rstd{tag}", [128, 1], F32)
        self.junk = sb(f"ln_junk{tag}", [128, DM], BF16)
        self.s_a = P.S(f"ln_a{tag}")
        self.s_v = P.S(f"ln_v{tag}")
        self.s_fr = P.S(f"ln_fr{tag}")
        self.s_done = P.S(f"ln_dn{tag}")
        self.ncall = 0
        self._frb = self.s_fr.n

    def emit(self, xs, out_bf, n_free, g_rep=None, in_tok=None):
        P = self.P
        if in_tok is not None:
            P.waits(("scalar", "vector"), [in_tok])
        if self.ncall > 0:
            P.wait("scalar", self.s_fr, self._frb + self.ncall)
        self.ncall += 1
        P.op("scalar", lambda e, _x=xs, _n=n_free: e.activation(
            self.junk[:, 0:_n], _x, AF.Square, accum_out=self.ssq[:, :]), sem=self.s_a)
        P.op("vector", lambda e, _x=xs: e.reduce_sum(self.sum[:, :], _x, axis=AXX))
        P.wait("vector", self.s_a, self.s_a.n)
        inv = 1.0 / n_free
        P.op("vector", lambda e, _i=inv: e.tensor_scalar_mul(self.mu[:, :], self.sum[:, :], _i))
        P.op("vector", lambda e, _i=inv: e.tensor_scalar_mul(self.var[:, :], self.ssq[:, :], _i))
        P.op("vector", lambda e: e.scalar_tensor_tensor(self.mu2[:, :], self.mu[:, :], -1.0,
                                                        self.mu[:, :], ALU.mult, ALU.mult))
        P.op("vector", lambda e: e.scalar_tensor_tensor(self.var[:, :], self.var[:, :], float(EPS),
                                                        self.mu2[:, :], ALU.add, ALU.add),
             sem=self.s_v)
        P.wait("scalar", self.s_v, self.s_v.n)
        P.op("scalar", lambda e: e.activation(self.rstd[:, :], self.var[:, :], AF.Sqrt), sem=self.s_a)
        P.wait("vector", self.s_a, self.s_a.n)
        P.op("vector", lambda e: e.reciprocal(self.rstd[:, :], self.rstd[:, :]))
        if g_rep is None:
            P.op("vector", lambda e, _x=xs, _o=out_bf: e.tensor_scalar(
                _o, _x, self.mu[:, :], self.rstd[:, :], ALU.subtract, ALU.mult),
                sem=self.s_done)
        else:
            P.op("vector", lambda e, _x=xs, _n=n_free: e.tensor_scalar(
                self.junk[:, 0:_n], _x, self.mu[:, :], self.rstd[:, :],
                ALU.subtract, ALU.mult))
            P.op("vector", lambda e, _o=out_bf, _g=g_rep, _n=n_free: e.tensor_mul(
                _o, self.junk[:, 0:_n], _g), sem=self.s_done)
        P.op("vector", lambda e: e.nop(), sem=self.s_fr, drain=False)
        self.sq_tok = (self.s_a, self.s_a.n - 1)
        return self.s_done, self.s_done.n


def build_l0ffn(C, stage="full"):
    nc = bass.Bass("TRN2", target_bir_lowering=False)
    dr = lambda n, s, d, k="ExternalInput": nc.dram_tensor(n, s, d, kind=k).ap()
    a = dict(
        x=dr("x", [T, DM], F32),
        y=dr("y", [T, DM], F32, "ExternalOutput"),
        sw1T=dr("sw1T", [DM, 512], F32), sw2T=dr("sw2T", [DM, 512], F32),
        sw3T=dr("sw3T", [512, DM], F32),
        sb1=dr("sb1", [1, 512], F32), sb2=dr("sb2", [1, 512], F32), sb3=dr("sb3", [1, DM], F32),
        slng=dr("slng", [1, 512], F32), slnb=dr("slnb", [128, 4], F32),
        sgeg=dr("sgeg", [DM, 16], F32), sgegb=dr("sgegb", [1, 16], F32),
        expb=dr("expb", [1, 8], F32),
        ew1T=dr("ew1T", [8, DM, 512], F32), ew2T=dr("ew2T", [8, DM, 512], F32),
        ew3T=dr("ew3T", [8, 512, DM], F32),
        eb1=dr("eb1", [8, 1, 512], F32), eb2=dr("eb2", [8, 1, 512], F32),
        eb3=dr("eb3", [8, 1, DM], F32),
        selg=dr("selg", [1, 512], F32), selb=dr("selb", [128, 4], F32),
        midx=dr("midx", [128, C // 128], I32),
        mmask=dr("mmask", [128, C // 128], F32),
        onesf=dr("onesf", [1, 128], F32), identbf=dr("identbf", [128, 128], BF16),
        iota8=dr("iota8", [128, 8], F32),
        xnat_d=dr("xnat_d", [T, DM], BF16, "Internal"),
        shout_d=dr("shout_d", [T, DM], BF16, "Internal"),
        C=C, scale=0.5,
    )
    ctx = ExitStack()
    P = Prog(nc, ctx)
    emit_l0ffn(nc, ctx, P, a, stage=stage)
    P.build()
    ctx.close()
    return nc


def emit_l0ffn(nc, ctx, P, a, stage="full"):
    a = dict(a); a["stage"] = stage
    C = a["C"]
    ngrp = C // 128
    sb = lambda n, s, d: ctx.enter_context(nc.sbuf_tensor(n, s, d))

    onesf = sb("onesf_sb", [1, 128], F32)
    ident = sb("ident_sb", [128, 128], BF16)
    iota8 = sb("iota8_sb", [128, 8], F32)
    s_cst = P.S("cst")
    P.dma("sync", s_cst, onesf[:, :], a["onesf"])
    P.dma("sync", s_cst, ident[:, :], a["identbf"])
    P.dma("sync", s_cst, iota8[:, :], a["iota8"])
    v_cst = s_cst.n

    ab = ExitStack()
    sbA = lambda n, s, d: ab.enter_context(nc.sbuf_tensor(n, s, d))

    xlnT = sbA("xlnT", [128, 8, T], BF16)
    w1 = sbA("w1", [128, 8, 512], BF16)
    w2 = sbA("w2", [128, 8, 512], BF16)
    w3 = sbA("w3", [128, 4, DM], BF16)
    b1 = sbA("b1", [1, 512], F32)
    b2 = sbA("b2", [1, 512], F32)
    b3 = sbA("b3", [1, DM], F32)
    slnbbf = sbA("slnbbf", [128, 4], BF16)
    grow = sbA("grow", [1, 512], F32)
    G = sbA("G", [128, 512], F32)
    beff = sbA("beff", [1, DM], F32)
    xs = [sbA(f"xs{i}", [128, DM], F32) for i in range(2)]
    xln = sbA("xln", [128, DM], BF16)
    s1 = sbA("s1", [128, 512], F32)
    prod = sbA("prod", [128, 512], F32)
    shbf = sbA("shbf", [128, 512], BF16)
    shT = sbA("shT", [128, 4, 128], BF16)
    sobf = [sbA(f"sobf{i}", [128, DM], BF16) for i in range(2)]
    ysb = [sbA(f"ysb{i}", [128, DM], F32) for i in range(2)]

    pb1 = PBank(P, ab, nc, "pb1", [128, 512])
    pb2 = PBank(P, ab, nc, "pb2", [128, 512])
    pb3 = PBank(P, ab, nc, "pb3", [128, 512])
    pb4 = PBank(P, ab, nc, "pb4", [128, 512])
    ptA = PBank(P, ab, nc, "ptA", [128, 128], BF16)
    ptB = PBank(P, ab, nc, "ptB", [128, 128], BF16)

    lnk = LNKit(P, sbA)

    s_ldwg = P.S("ldwg", sw=True)
    s_ldws = P.S("ldws")
    P.dma("gpsimd", s_ldwg, w1[:, :, :], a["sw1T"].rearrange("(c p) h -> p c h", p=128))
    P.dma("gpsimd", s_ldwg, w2[:, :, :], a["sw2T"].rearrange("(c p) h -> p c h", p=128))
    P.dma("gpsimd", s_ldwg, w3[:, :, :], a["sw3T"].rearrange("(c p) h -> p c h", p=128))
    P.dma("gpsimd", s_ldwg, slnbbf[:, :], a["slnb"])
    P.dma("sync", s_ldws, b1[:, :], a["sb1"])
    P.dma("sync", s_ldws, b2[:, :], a["sb2"])
    P.dma("sync", s_ldws, b3[:, :], a["sb3"])
    P.dma("sync", s_ldws, grow[:, :], a["slng"])
    v_ldwg, v_ldws = s_ldwg.n, s_ldws.n

    P.waits(("tensor",), [(s_cst, v_cst), (s_ldwg, v_ldwg), (s_ldws, v_ldws)])
    pb1.mm_group([(None, onesf[:, :], grow[:, :])])
    pb1.consume("vector", lambda e, p: e.tensor_copy(G[:, :], p[:, :]))
    P.wait("vector", s_ldws, v_ldws)
    for dt_ in range(2):
        bk = pb3 if dt_ == 0 else pb4
        bk.mm_group([(bk.ap[0:1, :], slnbbf[:, cc:cc+1], w3[:, cc, ts(dt_, 512)]) for cc in range(4)])
        bk.consume("vector", lambda e, p, _d=dt_: e.tensor_add(
            beff[:, ts(_d, 512)], p[0:1, :], b3[:, ts(_d, 512)]))
    sem_, val_ = pb4.free, pb4.freed_target
    P.waits(("tensor", "scalar"), [(sem_, val_)])

    ld_x = DStream(P, "ldx", "sync", 2)
    st_y = DStream(P, "sty", "gpsimd", 2)
    st_sh = DStream(P, "stsh", "gpsimd", 2)
    st_xn = DStream(P, "stxn", "gpsimd", 2)
    s_xfree = P.S("xfree")
    sxf_base = s_xfree.n

    tp_end = [0, 0]
    prev_mul = [None]
    prev_sq = [None]
    xn_tok = None
    y_tok = [None, None]
    sh_tok = [None, None]

    for s in range(NSTRIP):
        def strip(s=s):
            nonlocal xn_tok
            xb, yb, sob = xs[s % 2], ysb[s % 2], sobf[s % 2]
            if s >= 2:
                P.wait("sync", s_xfree, sxf_base + s - 1)
            tok = ld_x.issue(xb[:, :], a["x"][ts(s, 128), :])
            P.waits(("vector", "scalar"), [tok])
            if s >= 1:
                P.wait("vector", ptA.full, tp_end[0])
                P.wait("vector", ptB.full, tp_end[1])
                P.waits(("vector",), [xn_tok])
            sd, vd = lnk.emit(xb[:, :], xln[:, :], DM)
            P.wait("gpsimd", sd, vd)
            xn_tok = st_xn.issue(a["xnat_d"][ts(s, 128), :], xln[:, :])
            P.wait("tensor", sd, vd)
            for c in range(8):
                pt = ptA if c % 2 == 0 else ptB
                pt.tp_into(xln[:, ts(c, 128)], ident[:, :])
                pt.consume("vector", lambda e, p, _c=c: e.tensor_copy(
                    xlnT[:, _c, ts(s, 128)], p[:, 0:128]))
            P.wait("tensor", ptA.free, ptA.freed_target)
            P.wait("tensor", ptB.free, ptB.freed_target)
            if stage == "ln":
                if y_tok[s % 2] is not None:
                    P.waits(("gpsimd",), [y_tok[s % 2]])
                y_tok[s % 2] = st_y.issue(a["y"][ts(s, 128), :], xb[:, :])
                P.op("vector", lambda e: e.nop(), sem=s_xfree, drain=False)
                tp_end[0], tp_end[1] = ptA.fills, ptB.fills
                return
            pb1.mm_group([(None, xlnT[:, cc, ts(s, 128)], w1[:, cc, :]) for cc in range(8)]
                         + [(None, onesf[:, :], b1[:, :])])
            pb2.mm_group([(None, xlnT[:, cc, ts(s, 128)], w2[:, cc, :]) for cc in range(8)]
                         + [(None, onesf[:, :], b2[:, :])])
            if prev_mul[0] is not None:
                P.waits(("scalar",), [prev_mul[0]])
            fsem, fval = pb1.consume("scalar", lambda e, p: e.activation(s1[:, :], p[:, :], AF.Silu))
            P.wait("vector", fsem, fval)
            if prev_sq[0] is not None:
                P.waits(("vector",), [prev_sq[0]])
            ptok = pb2.consume("vector", lambda e, p: e.tensor_mul(prod[:, :], s1[:, :], p[:, :]))
            prev_mul[0] = ptok
            if stage == "x12":
                if y_tok[s % 2] is not None:
                    P.waits(("gpsimd",), [y_tok[s % 2]])
                P.waits(("gpsimd",), [ptok])
                y_tok[s % 2] = st_y.issue(a["y"][ts(s, 128), :], xb[:, :])
                P.op("vector", lambda e: e.nop(), sem=s_xfree, drain=False)
                tp_end[0], tp_end[1] = ptA.fills, ptB.fills
                return
            sd2, vd2 = lnk.emit(prod[:, :], shbf[:, :], 512, g_rep=G[:, :], in_tok=ptok)
            prev_sq[0] = lnk.sq_tok
            P.wait("tensor", sd2, vd2)
            for c in range(4):
                pt = ptA if c % 2 == 0 else ptB
                pt.tp_into(shbf[:, ts(c, 128)], ident[:, :])
                pt.consume("vector", lambda e, p, _c=c: e.tensor_copy(
                    shT[:, _c, :], p[:, 0:128]))
            P.wait("tensor", ptA.free, ptA.freed_target)
            P.wait("tensor", ptB.free, ptB.freed_target)
            if stage == "shln":
                if y_tok[s % 2] is not None:
                    P.waits(("gpsimd",), [y_tok[s % 2]])
                y_tok[s % 2] = st_y.issue(a["y"][ts(s, 128), :], xb[:, :])
                P.op("vector", lambda e: e.nop(), sem=s_xfree, drain=False)
                tp_end[0], tp_end[1] = ptA.fills, ptB.fills
                return
            if y_tok[s % 2] is not None:
                P.waits(("vector",), [y_tok[s % 2]])
                if sh_tok[s % 2] is not None:
                    P.waits(("scalar",), [sh_tok[s % 2]])
            for dt_ in range(2):
                bk = pb3 if dt_ == 0 else pb4
                bk.mm_group([(None, shT[:, cc, :], w3[:, cc, ts(dt_, 512)]) for cc in range(4)]
                            + [(None, onesf[:, :], beff[:, ts(dt_, 512)])])
                ytok_c = bk.consume("vector", lambda e, p, _d=dt_, _y=yb, _x=xb: e.scalar_tensor_tensor(
                    _y[:, ts(_d, 512)], p[:, :], a["scale"], _x[:, ts(_d, 512)],
                    ALU.mult, ALU.add))
                if stage not in ("ab_nosh", "ab_nocopy"):
                    P.waits(("scalar",), [ytok_c])
                    bk.consume("scalar", lambda e, p, _d=dt_, _o=sob: e.copy(_o[:, ts(_d, 512)], p[:, :]))
            P.op("vector", lambda e: e.nop(), sem=s_xfree, drain=False)
            P.wait("gpsimd", pb3.free, pb3.freed_target)
            P.wait("gpsimd", pb4.free, pb4.freed_target)
            y_tok[s % 2] = st_y.issue(a["y"][ts(s, 128), :], yb[:, :])
            if stage != "ab_nosh":
                sh_tok[s % 2] = st_sh.issue(a["shout_d"][ts(s, 128), :], sob[:, :])
            tp_end[0], tp_end[1] = ptA.fills, ptB.fills
        strip(s)

    ab_tokens = st_y.tokens() + st_sh.tokens() + st_xn.tokens()
    ab.close()
    if stage in ("ab", "ln", "x12", "shln", "ab_nosh", "ab_nocopy", "ab_shsync"):
        P.waits(("gpsimd",), ab_tokens)
        return

    mo = ExitStack()
    sbC = lambda n, s, d: mo.enter_context(nc.sbuf_tensor(n, s, d))

    P.waits(ENGINES, ab_tokens)

    idx_sb = sbC("idx_sb", [128, ngrp], I32)
    mmask = sbC("mmask", [128, ngrp], F32)
    xmoe = sbC("xmoe", [128, DM], BF16)
    xmoeT = sbC("xmoeT", [128, 8, C], BF16)
    wsg = sbC("wsg", [128, 8, 16], BF16)
    bsg = sbC("bsg", [1, 16], F32)
    expb_sb = sbC("expb_sb", [1, 8], F32)
    expb_r = sbC("expb_r", [128, 8], F32)
    scs = sbC("scs", [128, ngrp, 16], F32)
    ss = sbC("ss", [128, ngrp], F32)
    es = sbC("es", [128, ngrp, 8], F32)
    top8 = sbC("top8", [128, 8], F32)
    tidx = sbC("tidx", [128, 8], U32)
    t1f = sbC("t1f", [128, 1], F32)
    oh = sbC("oh", [128, ngrp, 8], F32)
    tsq = sbC("tsq", [128, 8], F32)
    tscore = sbC("tscore", [128, 1], F32)
    shw = sbC("shw", [128, ngrp], F32)
    expw = sbC("expw", [128, ngrp], F32)
    dtmp = sbC("dtmp", [128, 1], F32)
    ew1 = [sbC(f"ew1_{i}", [128, 8, 512], BF16) for i in range(2)]
    ew2 = [sbC(f"ew2_{i}", [128, 8, 512], BF16) for i in range(2)]
    ew3 = [sbC(f"ew3_{i}", [128, 4, DM], BF16) for i in range(2)]
    be1 = [sbC(f"be1_{i}", [1, 512], F32) for i in range(2)]
    be2 = [sbC(f"be2_{i}", [1, 512], F32) for i in range(2)]
    be3 = [sbC(f"be3_{i}", [1, DM], F32) for i in range(2)]
    be3eff = [sbC(f"be3eff_{i}", [1, DM], F32) for i in range(2)]
    selbbf = sbC("selbbf", [128, 4], BF16)
    selgrow = sbC("selgrow", [1, 512], F32)
    selG = sbC("selG", [128, 512], F32)
    sl1 = sbC("sl1", [128, 512], F32)
    prodC = sbC("prodC", [128, 512], F32)
    zbf = sbC("zbf", [128, 512], BF16)
    zT = sbC("zT", [128, 4, 128], BF16)
    acc = [sbC(f"acc{i}", [128, ngrp, DM], F32) for i in range(2)]
    shm = sbC("shm", [128, DM], BF16)
    corr = sbC("corr", [128, DM], F32)
    w01 = sbC("w01", [128, 1], F32)
    w02 = sbC("w02", [128, 1], F32)

    qb1 = PBank(P, mo, nc, "qb1", [128, 512])
    qb2 = PBank(P, mo, nc, "qb2", [128, 512])
    qb3 = PBank(P, mo, nc, "qb3", [128, 512])
    qb4 = PBank(P, mo, nc, "qb4", [128, 512])
    qtA = PBank(P, mo, nc, "qtA", [128, 128], BF16)
    qtB = PBank(P, mo, nc, "qtB", [128, 128], BF16)

    lnkC = LNKit(P, sbC, tag="C")

    s_ld = P.S("mld")
    s_ldg = P.S("mldg", sw=True)
    P.dma("sync", s_ld, idx_sb[:, :], a["midx"])
    P.dma("sync", s_ld, mmask[:, :], a["mmask"])
    P.dma("gpsimd", s_ldg, wsg[:, :, :], a["sgeg"].rearrange("(c p) h -> p c h", p=128))
    P.dma("gpsimd", s_ldg, selbbf[:, :], a["selb"])
    P.dma("sync", s_ld, bsg[:, :], a["sgegb"])
    P.dma("sync", s_ld, selgrow[:, :], a["selg"])
    P.dma("sync", s_ld, expb_sb[:, :], a["expb"])
    v_ld0, v_ldg0 = s_ld.n, s_ldg.n
    P.waits(("tensor", "gpsimd"), [(s_ld, v_ld0), (s_ldg, v_ldg0)])
    qb1.mm_group([(qb1.ap[:, 0:8], onesf[:, :], expb_sb[:, :])])
    qb1.consume("vector", lambda e, p: e.tensor_copy(expb_r[:, :], p[:, 0:8]))
    qb2.mm_group([(None, onesf[:, :], selgrow[:, :])])
    qb2.consume("vector", lambda e, p: e.tensor_copy(selG[:, :], p[:, :]))

    s_mz = P.S("mz")
    P.op("vector", lambda e: e.memset(xmoe[:, :], 0.0))
    P.op("vector", lambda e: e.memset(shm[:, :], 0.0), sem=s_mz)
    P.wait("gpsimd", s_mz, s_mz.n)
    gth = DStream(P, "gth", "gpsimd", 2)
    gat_tp_end = [qtA.fills, qtB.fills]
    for g in range(ngrp):
        def grp(g=g):
            if g >= 1:
                P.wait("gpsimd", qtA.full, gat_tp_end[0])
                P.wait("gpsimd", qtB.full, gat_tp_end[1])
            tok = gth.issue_gather(xmoe[:, :], a["xnat_d"], idx_sb[:, g:g+1], None)
            P.waits(("tensor",), [tok])
            for c in range(8):
                pt = qtA if c % 2 == 0 else qtB
                pt.tp_into(xmoe[:, ts(c, 128)], ident[:, :])
                pt.consume("vector", lambda e, p, _c=c, _g=g: e.tensor_copy(
                    xmoeT[:, _c, ts(_g, 128)], p[:, 0:128]))
            gat_tp_end[0], gat_tp_end[1] = qtA.fills, qtB.fills
            P.wait("tensor", qtA.free, qtA.freed_target)
            P.wait("tensor", qtB.free, qtB.freed_target)
            qb3.mm_group([(qb3.ap[:, 0:16], xmoeT[:, cc, ts(g, 128)], wsg[:, cc, :])
                          for cc in range(8)]
                         + [(qb3.ap[:, 0:16], onesf[:, :], bsg[:, :])])
            qb3.consume("vector", lambda e, p, _g=g: e.tensor_copy(scs[:, _g, :], p[:, 0:16]))
        grp(g)

    s_sc = P.S("msc")
    s_gv = P.S("mgv")
    P.wait("scalar", qb3.free, qb3.freed_target)
    for g in range(ngrp):
        def gate(g=g):
            P.op("scalar", lambda e, _g=g: e.activation(ss[:, _g:_g+1], scs[:, _g, 0:1], AF.Sigmoid))
            P.op("scalar", lambda e, _g=g: e.activation(es[:, _g, :], scs[:, _g, 1:9], AF.Sigmoid),
                 sem=s_sc)
            P.wait("vector", s_sc, s_sc.n)
            P.op("vector", lambda e, _g=g: e.tensor_add(tsq[:, :], es[:, _g, :], expb_r[:, :]))
            P.op("vector", lambda e: e.max(out=top8[:, :], in_=tsq[:, :]))
            P.op("vector", lambda e: e.max_index(tidx[:, :], top8[:, :], tsq[:, :]))
            P.op("vector", lambda e: e.tensor_copy(t1f[:, :], tidx[:, 0:1]))
            P.op("vector", lambda e, _g=g: e.tensor_tensor(
                oh[:, _g, :], iota8[:, :], t1f[:, :].to_broadcast([128, 8]), ALU.is_equal))
            P.op("vector", lambda e, _g=g: e.tensor_mul(tsq[:, :], oh[:, _g, :], es[:, _g, :]))
            P.op("vector", lambda e: e.reduce_sum(tscore[:, :], tsq[:, :], axis=AXX))
            P.op("vector", lambda e, _g=g: e.tensor_sub(dtmp[:, :], ss[:, _g:_g+1], tscore[:, :]),
                 sem=s_gv)
            P.wait("scalar", s_gv, s_gv.n)
            P.op("scalar", lambda e, _g=g: e.activation(shw[:, _g:_g+1], dtmp[:, :], AF.Sigmoid),
                 sem=s_sc)
            P.wait("vector", s_sc, s_sc.n)
            P.op("vector", lambda e, _g=g: e.tensor_scalar(
                expw[:, _g:_g+1], shw[:, _g:_g+1], -1.0, 1.0, ALU.mult, ALU.add))
        gate(g)
    s_gates = P.S("gdone")
    P.op("vector", lambda e: e.nop(), sem=s_gates, drain=False)
    v_gates = s_gates.n
    P.wait("scalar", s_gates, v_gates)
    if stage == "abg":
        P.waits(("gpsimd",), [(s_gates, v_gates)])
        mo.close()
        return

    s_wl = [P.S("wl0", sw=True), P.S("wl1", sw=True)]
    s_wls = [P.S("wls0"), P.S("wls1")]
    wu_tok = [None, None]

    def load_expert(e_, b):
        if wu_tok[b] is not None:
            P.waits(("gpsimd", "sync"), [wu_tok[b]])
        P.dma("gpsimd", s_wl[b], ew1[b][:, :, :], a["ew1T"][e_].rearrange("(c p) h -> p c h", p=128))
        P.dma("gpsimd", s_wl[b], ew2[b][:, :, :], a["ew2T"][e_].rearrange("(c p) h -> p c h", p=128))
        P.dma("gpsimd", s_wl[b], ew3[b][:, :, :], a["ew3T"][e_].rearrange("(c p) h -> p c h", p=128))
        P.dma("sync", s_wls[b], be1[b][:, :], a["eb1"][e_])
        P.dma("sync", s_wls[b], be2[b][:, :], a["eb2"][e_])
        P.dma("sync", s_wls[b], be3[b][:, :], a["eb3"][e_])
        return (s_wl[b].n, s_wls[b].n)

    wv = [load_expert(0, 0), load_expert(1, 1)]
    prev_mulC = [None]
    prev_sqC = [None]
    for e_ in range(8):
        def expert(e_=e_):
            b = e_ % 2
            P.waits(("tensor", "vector"), [(s_wl[b], wv[b][0]), (s_wls[b], wv[b][1])])
            for dt_ in range(2):
                bk = qb3 if dt_ == 0 else qb4
                bk.mm_group([(bk.ap[0:1, :], selbbf[:, cc:cc+1], ew3[b][:, cc, ts(dt_, 512)])
                             for cc in range(4)])
                bk.consume("vector", lambda e, p, _d=dt_, _b=b: e.tensor_add(
                    be3eff[_b][:, ts(_d, 512)], p[0:1, :], be3[_b][:, ts(_d, 512)]))
            sem_b3, val_b3 = qb4.free, qb4.freed_target
            P.wait("tensor", sem_b3, val_b3)
            for g in range(ngrp):
                def egrp(g=g):
                    qb1.mm_group([(None, xmoeT[:, cc, ts(g, 128)], ew1[b][:, cc, :])
                                  for cc in range(8)] + [(None, onesf[:, :], be1[b][:, :])])
                    qb2.mm_group([(None, xmoeT[:, cc, ts(g, 128)], ew2[b][:, cc, :])
                                  for cc in range(8)] + [(None, onesf[:, :], be2[b][:, :])])
                    if prev_mulC[0] is not None:
                        P.waits(("scalar",), [prev_mulC[0]])
                    fsem, fval = qb1.consume("scalar", lambda e, p: e.activation(
                        sl1[:, :], p[:, :], AF.Silu))
                    P.wait("vector", fsem, fval)
                    if prev_sqC[0] is not None:
                        P.waits(("vector",), [prev_sqC[0]])
                    ptok = qb2.consume("vector", lambda e, p: e.tensor_mul(prodC[:, :], sl1[:, :], p[:, :]))
                    prev_mulC[0] = ptok
                    sd, vd = lnkC.emit(prodC[:, :], zbf[:, :], 512, g_rep=selG[:, :], in_tok=ptok)
                    prev_sqC[0] = lnkC.sq_tok
                    P.wait("tensor", sd, vd)
                    for c in range(4):
                        pt = qtA if c % 2 == 0 else qtB
                        pt.tp_into(zbf[:, ts(c, 128)], ident[:, :])
                        pt.consume("vector", lambda e, p, _c=c: e.tensor_copy(
                            zT[:, _c, :], p[:, 0:128]))
                    P.wait("tensor", qtA.free, qtA.freed_target)
                    P.wait("tensor", qtB.free, qtB.freed_target)
                    for dt_ in range(2):
                        bk = qb3 if dt_ == 0 else qb4
                        bk.mm_group([(None, zT[:, cc, :], ew3[b][:, cc, ts(dt_, 512)])
                                     for cc in range(4)]
                                    + [(None, onesf[:, :], be3eff[b][:, ts(dt_, 512)])])
                        if e_ == 0:
                            bk.consume("vector", lambda e, p, _d=dt_, _g=g: e.tensor_scalar_mul(
                                acc[0][:, _g, ts(_d, 512)], p[:, :], oh[:, _g, 0:1]))
                        else:
                            src_, dst_ = acc[(e_ + 1) % 2], acc[e_ % 2]
                            bk.consume("vector", lambda e, p, _d=dt_, _g=g, _s=src_, _t=dst_:
                                       e.scalar_tensor_tensor(
                                           _t[:, _g, ts(_d, 512)], p[:, :], oh[:, _g, e_:e_+1],
                                           _s[:, _g, ts(_d, 512)], ALU.mult, ALU.add))
                egrp(g)
            wu_tok[b] = (qb4.free, qb4.freed_target)
            if e_ + 2 < 8:
                wv[b] = load_expert(e_ + 2, b)
        expert(e_)

    if stage == "abe":
        sem_e, val_e = qb4.free, qb4.freed_target
        P.waits(("gpsimd",), [(sem_e, val_e)])
        mo.close()
        return
    accF = acc[7 % 2]
    s_corr = P.S("corr")
    corr_base = s_corr.n
    sct = DStream(P, "sct", "gpsimd", 2)
    gth2 = DStream(P, "gth2", "gpsimd", 2)
    shm_tok = None
    sc_tok = None
    for g in range(ngrp):
        def fix(g=g):
            nonlocal shm_tok, sc_tok
            if g >= 1:
                P.wait("gpsimd", s_corr, corr_base + 3 * (g - 1) + 3)
            if sc_tok is not None:
                P.waits(("vector",), [sc_tok])
            shm_tok = gth2.issue_gather(shm[:, :], a["shout_d"], idx_sb[:, g:g+1], None)
            P.waits(("vector",), [shm_tok])
            P.op("vector", lambda e, _g=g: e.tensor_scalar(
                w01[:, :], shw[:, _g:_g+1], -1.0, 0.5, ALU.add, ALU.mult))
            P.op("vector", lambda e, _g=g: e.tensor_scalar_mul(w02[:, :], expw[:, _g:_g+1], 0.5))
            P.op("vector", lambda e: e.tensor_scalar_mul(corr[:, :], shm[:, :], w01[:, :]),
                 sem=s_corr)
            P.op("vector", lambda e, _g=g: e.scalar_tensor_tensor(
                corr[:, :], accF[:, _g, :], w02[:, :], corr[:, :], ALU.mult, ALU.add),
                 sem=s_corr)
            P.op("vector", lambda e, _g=g: e.tensor_scalar_mul(
                corr[:, :], corr[:, :], mmask[:, _g:_g + 1]), sem=s_corr)
            P.wait("gpsimd", s_corr, s_corr.n)
            sc_tok = sct.issue_scatter(a["y"], idx_sb[:, g:g+1], corr[:, :], None, op=ALU.add)
        fix(g)
    P.waits(("gpsimd",), sct.tokens())
    mo.close()


def emit_ln_xlnT(nc, ctx, P, sb, a, xlnT, ident, lnk, tag, xnat_d=None, banks=None):
    xs = [sb(f"lnx{tag}{i}", [128, DM], F32) for i in range(2)]
    xln = sb(f"lnxo{tag}", [128, DM], BF16)
    if banks is None:
        ptA = PBank(P, ctx, nc, f"lptA{tag}", [128, 128], BF16)
        ptB = PBank(P, ctx, nc, f"lptB{tag}", [128, 128], BF16)
    else:
        ptA, ptB = banks
    ld_x = DStream(P, f"lldx{tag}", "sync", 2)
    st_xn = DStream(P, f"lstxn{tag}", "gpsimd", 2)
    s_xfree = P.S(f"lxf{tag}")
    sxf_base = s_xfree.n
    tp_end = [0, 0]
    xn_tok = None
    for s in range(NSTRIP):
        def strip(s=s):
            nonlocal xn_tok
            xb = xs[s % 2]
            if s >= 2:
                P.wait("sync", s_xfree, sxf_base + s - 1)
            tok = ld_x.issue(xb[:, :], a["x"][ts(s, 128), :])
            P.waits(("vector", "scalar"), [tok])
            if s >= 1:
                P.wait("vector", ptA.full, tp_end[0])
                P.wait("vector", ptB.full, tp_end[1])
                if xn_tok is not None:
                    P.waits(("vector",), [xn_tok])
            sd, vd = lnk.emit(xb[:, :], xln[:, :], DM)
            if xnat_d is not None:
                P.wait("gpsimd", sd, vd)
                xn_tok = st_xn.issue(xnat_d[ts(s, 128), :], xln[:, :])
            P.wait("tensor", sd, vd)
            for c in range(8):
                pt = ptA if c % 2 == 0 else ptB
                pt.tp_into(xln[:, ts(c, 128)], ident[:, :])
                pt.consume("vector", lambda e, p, _c=c: e.tensor_copy(
                    xlnT[:, _c, ts(s, 128)], p[:, 0:128]))
            P.op("vector", lambda e: e.nop(), sem=s_xfree, drain=False)
            tp_end[0], tp_end[1] = ptA.fills, ptB.fills
        strip(s)
    return ([(ptA.free, ptA.freed_target), (ptB.free, ptB.freed_target)]
            + st_xn.tokens())


def build_l1ffn():
    nc = bass.Bass("TRN2", target_bir_lowering=False)
    dr = lambda n, s, d, k="ExternalInput": nc.dram_tensor(n, s, d, kind=k).ap()
    a = dict(
        x=dr("x", [T, DM], F32),
        y=dr("y", [T, DM], F32, "ExternalOutput"),
        wr=dr("wr", [DM, 8], F32),
        wg=dr("wg", [8, DM, 2048], F32),
        wu=dr("wu", [8, DM, 2048], F32),
        wd=dr("wd", [8, 2048, DM], F32),
        onesf=dr("onesf", [1, 128], F32),
        onesbf=dr("onesbf", [1, 128], BF16),
        identbf=dr("identbf", [128, 128], BF16),
        identf=dr("identf", [128, 128], F32),
        iota8=dr("iota8", [128, 8], F32),
        accd=dr("accd", [T, DM], F32, "Internal"),
        fwd=dr("fwd", [8, T], BF16, "Internal"),
        scale=0.5,
    )
    ctx = ExitStack()
    P = Prog(nc, ctx)
    emit_l1ffn(nc, ctx, P, a)
    P.build()
    ctx.close()
    return nc


def emit_l1ffn(nc, ctx, P, a):
    sb = lambda n, s, d: ctx.enter_context(nc.sbuf_tensor(n, s, d))
    onesf = sb("onesf_sb", [1, 128], F32)
    onesbf = sb("onesbf_sb", [1, 128], BF16)
    ident = sb("ident_sb", [128, 128], BF16)
    identf = sb("identf_sb", [128, 128], F32)
    iota8 = sb("iota8_sb", [128, 8], F32)
    s_cst = P.S("cst")
    P.dma("sync", s_cst, onesf[:, :], a["onesf"])
    P.dma("sync", s_cst, onesbf[:, :], a["onesbf"])
    P.dma("sync", s_cst, ident[:, :], a["identbf"])
    P.dma("sync", s_cst, identf[:, :], a["identf"])
    P.dma("sync", s_cst, iota8[:, :], a["iota8"])
    P.waits(("tensor",), [(s_cst, s_cst.n)])

    xlnT = sb("xlnT", [128, 8, T], BF16)
    fwsl = [sb(f"fwsl{i}", [1, 512], BF16) for i in range(2)]
    lnscope = ExitStack()
    sbL = lambda n, s, d: lnscope.enter_context(nc.sbuf_tensor(n, s, d))
    lnk = LNKit(P, sbL)
    ln_toks = emit_ln_xlnT(nc, lnscope, P, sbL, a, xlnT, ident, lnk, tag="F")

    wrsb = sbL("wrsb", [128, 8, 8], BF16)
    probs = sbL("probs", [128, 8], F32)
    nmax = sbL("nmax", [128, 1], F32)
    sacc = sbL("sacc", [128, 1], F32)
    top8 = sbL("top8", [128, 8], F32)
    tidx = sbL("tidx", [128, 8], U32)
    fw8 = sbL("fw8", [8, T], BF16)
    i1f = sbL("i1f", [128, 1], F32)
    i2f = sbL("i2f", [128, 1], F32)
    nw1 = sbL("nw1", [128, 1], F32)
    oh1 = sbL("oh1", [128, 8], F32)
    fw = sbL("fw", [128, 8], F32)
    s_ldr = P.S("ldr", sw=True)
    P.dma("gpsimd", s_ldr, wrsb[:, :, :], a["wr"].rearrange("(c p) h -> p c h", p=128))
    P.waits(("tensor",), [(s_ldr, s_ldr.n)])
    P.waits(("tensor",), ln_toks[0:2])
    rb = PBank(P, lnscope, nc, "rb", [128, 512])
    rt = PBank(P, lnscope, nc, "rt", [128, 128], F32)
    s_rv = P.S("rv")
    s_ra = P.S("ra")
    for s in range(NSTRIP):
        def rstrip(s=s):
            rb.mm_group([(rb.ap[:, 0:8], xlnT[:, cc, ts(s, 128)], wrsb[:, cc, :])
                         for cc in range(8)])
            tok1 = rb.consume("vector", lambda e, p: e.reduce_max(nmax[:, :], p[:, 0:8], axis=AXX))
            P.op("vector", lambda e: e.tensor_scalar_mul(nmax[:, :], nmax[:, :], -1.0), sem=s_rv)
            P.waits(("scalar",), [tok1, (s_rv, s_rv.n)])
            etok = rb.consume("scalar", lambda e, p: e.activation(
                probs[:, :], p[0:128, 0:8], AF.Exp, bias=nmax[:, :], accum_out=sacc[:, :]))
            P.waits(("vector",), [etok])
            P.op("vector", lambda e: e.reciprocal(sacc[:, :], sacc[:, :]))
            P.op("vector", lambda e: e.tensor_scalar_mul(probs[:, :], probs[:, :], sacc[:, :]))
            P.op("vector", lambda e: e.max(out=top8[:, :], in_=probs[:, :]))
            P.op("vector", lambda e: e.max_index(tidx[:, :], top8[:, :], probs[:, :]))
            P.op("vector", lambda e: e.tensor_copy(i1f[:, :], tidx[:, 0:1]))
            P.op("vector", lambda e: e.tensor_copy(i2f[:, :], tidx[:, 1:2]))
            P.op("vector", lambda e: e.tensor_sub(nmax[:, :], top8[:, 0:1], top8[:, 1:2]), sem=s_rv)
            P.wait("scalar", s_rv, s_rv.n)
            P.op("scalar", lambda e: e.activation(nw1[:, :], nmax[:, :], AF.Sigmoid), sem=s_ra)
            P.wait("vector", s_ra, s_ra.n)
            P.op("vector", lambda e: e.tensor_tensor(
                oh1[:, :], iota8[:, :], i1f[:, :].to_broadcast([128, 8]), ALU.is_equal))
            P.op("vector", lambda e: e.tensor_scalar_mul(fw[:, :], oh1[:, :], nw1[:, :]))
            P.op("vector", lambda e: e.tensor_tensor(
                oh1[:, :], iota8[:, :], i2f[:, :].to_broadcast([128, 8]), ALU.is_equal))
            P.op("vector", lambda e: e.tensor_scalar(nw1[:, :], nw1[:, :], -1.0, 1.0,
                                                     ALU.mult, ALU.add))
            P.op("vector", lambda e: e.scalar_tensor_tensor(
                fw[:, :], oh1[:, :], nw1[:, :], fw[:, :], ALU.mult, ALU.add), sem=s_rv)
            P.wait("tensor", s_rv, s_rv.n)
            rt.tp_into(fw[:, 0:8], identf[:, :])
            rt.consume("vector", lambda e, p, _s=s: e.tensor_copy(
                fw8[:, ts(_s, 128)], p[0:8, 0:128]))
        rstrip(s)
    s_rdone = P.S("rdone")
    P.op("vector", lambda e: e.nop(), sem=s_rdone, drain=False)
    P.waits(("tensor", "scalar", "gpsimd", "sync"), [(s_rdone, s_rdone.n)])
    s_fwb = P.S("fwb", sw=True)
    P.dma("gpsimd", s_fwb, a["fwd"], fw8[:, :])
    P.waits(("gpsimd", "sync"), [(s_fwb, s_fwb.n)])

    lnscope.close()
    wg_sb = sb("wg_sb", [128, 8, 2048], BF16)
    wu_sb = sb("wu_sb", [128, 8, 2048], BF16)
    wd_sb = sb("wd_sb", [128, 16, DM], BF16)
    wrep = sb("wrep", [128, 512], F32)
    sg = sb("sg", [128, 512], BF16)
    usc = sb("usc", [128, 512], BF16)
    fuseT = sb("fuseT", [128, 16, 512], BF16)
    accs = [sb(f"accs{i}", [128, DM], F32) for i in range(2)]
    xts = [sb(f"xts{i}", [128, DM], F32) for i in range(2)]

    pg = PBank(P, ctx, nc, "pg", [128, 512])
    pu = PBank(P, ctx, nc, "pu", [128, 512])
    pg2 = PBank(P, ctx, nc, "pg2", [128, 512])
    pu2 = PBank(P, ctx, nc, "pu2", [128, 512])
    pd1 = PBank(P, ctx, nc, "pd1", [128, 512])
    pd2 = PBank(P, ctx, nc, "pd2", [128, 512])
    prp = PBank(P, ctx, nc, "prp", [128, 512])

    s_wle = P.S("wle", sw=True)
    ld_acc = DStream(P, "ldacc", "sync", 2)
    st_acc = DStream(P, "stacc", "gpsimd", 2)
    ld_xf = DStream(P, "ldxf", "sync", 2)
    ld_fw = DStream(P, "ldfw", "sync", 2)
    NT = 8
    acc_st_tok = {}
    buf_st_tok = [None, None]
    xbuf_free = [None, None]
    prev_fuse = [None]
    wfree_tok = [None]
    fsl_free = [None, None]
    s_av = P.S("av")
    subctr = [0]
    for e_ in range(8):
        def expert(e_=e_):
            if wfree_tok[0] is not None:
                P.waits(("gpsimd",), [wfree_tok[0]])
            P.dma("gpsimd", s_wle, wg_sb[:, :, :], a["wg"][e_].rearrange("(c p) f -> p c f", p=128))
            P.dma("gpsimd", s_wle, wu_sb[:, :, :], a["wu"][e_].rearrange("(c p) f -> p c f", p=128))
            P.dma("gpsimd", s_wle, wd_sb[:, :, :], a["wd"][e_].rearrange("(c p) f -> p c f", p=128))
            vw = s_wle.n
            P.waits(("tensor",), [(s_wle, vw)])
            for tt in range(NT):
                def ttile(tt=tt):
                    fsl = fwsl[tt % 2]
                    if fsl_free[tt % 2] is not None:
                        P.waits(("sync",), [fsl_free[tt % 2]])
                    ftok = ld_fw.issue(fsl[:, :], a["fwd"][e_:e_+1, ts(tt, 512)])
                    P.waits(("tensor",), [ftok])
                    prp.mm_group([(None, onesbf[:, :], fsl[:, :])])
                    fsl_free[tt % 2] = (prp.full, prp.fills)
                    prp.consume("vector", lambda e, p: e.tensor_copy(wrep[:, :], p[:, :]))
                    P.wait("tensor", prp.free, prp.freed_target)
                    for ft in range(16):
                        def ftile(ft=ft):
                            bg = pg if ft % 2 == 0 else pg2
                            bu = pu if ft % 2 == 0 else pu2
                            bg.mm_group([(None, wg_sb[:, cc, ts(ft, 128)], xlnT[:, cc, ts(tt, 512)])
                                         for cc in range(8)])
                            bu.mm_group([(None, wu_sb[:, cc, ts(ft, 128)], xlnT[:, cc, ts(tt, 512)])
                                         for cc in range(8)])
                            if prev_fuse[0] is not None:
                                P.waits(("scalar",), [prev_fuse[0]])
                            bg.consume("scalar", lambda e, p: e.activation(sg[:, :], p[:, :], AF.Silu))
                            bu.consume("vector", lambda e, p: e.tensor_mul(usc[:, :], p[:, :], wrep[:, :]))
                            P.waits(("vector",), [(bg.free, bg.freed_target)])
                            s_fu = P.S("fu")
                            prev_fuse[0] = (s_fu, P.op("vector", lambda e, _f=ft: e.tensor_tensor(
                                fuseT[:, _f, :], sg[:, :], usc[:, :], ALU.mult), sem=s_fu))
                        ftile(ft)
                    P.wait("tensor", prev_fuse[0][0], prev_fuse[0][1])
                    for sub in range(4):
                        def dsub(sub=sub):
                            row0 = tt * 512 + sub * 128
                            bi = subctr[0] % 2
                            subctr[0] += 1
                            ab_ = accs[bi]
                            atok = None
                            xtok = None
                            if e_ > 0:
                                if acc_st_tok.get((tt, sub)) is not None:
                                    P.waits(("sync",), [acc_st_tok[(tt, sub)]])
                                if buf_st_tok[bi] is not None:
                                    P.waits(("sync",), [buf_st_tok[bi]])
                                atok = ld_acc.issue(ab_[:, :], a["accd"][row0:row0+128, :])
                            if e_ == 7:
                                if xbuf_free[bi] is not None:
                                    P.waits(("sync",), [xbuf_free[bi]])
                                xtok = ld_xf.issue(xts[bi][:, :], a["x"][row0:row0+128, :])
                            if atok is not None:
                                P.waits(("vector",), [atok])
                            if e_ == 0 and buf_st_tok[bi] is not None:
                                P.waits(("vector",), [buf_st_tok[bi]])
                            if xtok is not None:
                                P.waits(("vector",), [xtok])
                            for dh in range(2):
                                bk = pd1 if dh == 0 else pd2
                                bk.mm_group([(None, fuseT[:, ft2, ts(sub, 128)],
                                              wd_sb[:, ft2, ts(dh, 512)]) for ft2 in range(16)])
                                if e_ == 0:
                                    bk.consume("vector", lambda e, p, _d=dh, _a=ab_:
                                               e.tensor_copy(_a[:, ts(_d, 512)], p[:, :]))
                                elif e_ < 7:
                                    bk.consume("vector", lambda e, p, _d=dh, _a=ab_:
                                               e.tensor_add(_a[:, ts(_d, 512)], p[:, :],
                                                            _a[:, ts(_d, 512)]))
                                else:
                                    bk.consume("vector", lambda e, p, _d=dh, _a=ab_:
                                               e.tensor_add(_a[:, ts(_d, 512)], p[:, :],
                                                            _a[:, ts(_d, 512)]))
                                    P.op("vector", lambda e, _d=dh, _a=ab_, _x=xts[bi]:
                                         e.scalar_tensor_tensor(
                                             _a[:, ts(_d, 512)], _a[:, ts(_d, 512)], a["scale"],
                                             _x[:, ts(_d, 512)], ALU.mult, ALU.add), sem=s_av)
                            P.wait("gpsimd", pd1.free, pd1.freed_target)
                            P.wait("gpsimd", pd2.free, pd2.freed_target)
                            if e_ == 7:
                                P.wait("gpsimd", s_av, s_av.n)
                                xbuf_free[bi] = (s_av, s_av.n)
                                tok = st_acc.issue(a["y"][row0:row0+128, :], ab_[:, :])
                            else:
                                tok = st_acc.issue(a["accd"][row0:row0+128, :], ab_[:, :])
                            acc_st_tok[(tt, sub)] = tok
                            buf_st_tok[bi] = tok
                        dsub(sub)
                ttile(tt)
            wfree_tok[0] = (pd2.free, pd2.freed_target)
        expert(e_)
    P.waits(("gpsimd",), st_acc.tokens())


CE = 1536


def build_l1ffn_sparse():
    nc = bass.Bass("TRN2", target_bir_lowering=False)
    dr = lambda n, s, d, k="ExternalInput": nc.dram_tensor(n, s, d, kind=k).ap()
    a = dict(
        x=dr("x", [T, DM], F32),
        y=dr("y", [T, DM], F32, "ExternalOutput"),
        wr=dr("wr", [DM, 8], F32),
        wg=dr("wg", [8, DM, 2048], F32),
        wu=dr("wu", [8, DM, 2048], F32),
        wd=dr("wd", [8, 2048, DM], F32),
        onesf=dr("onesf", [1, 128], F32),
        onesbf=dr("onesbf", [1, 128], BF16),
        identbf=dr("identbf", [128, 128], BF16),
        identf=dr("identf", [128, 128], F32),
        iota8=dr("iota8", [128, 8], F32),
        trilow=dr("trilow", [128, 128], BF16),
        pidx=dr("pidx", [128, 1], F32),
        xnat_d=dr("xnat_d", [T, DM], BF16, "Internal"),
        tix_d=dr("tix_d", [8 * CE, 2], F32, "Internal"),
        out_d=dr("out_d", [8 * CE, DM], F32, "Internal"),
        ptd=dr("ptd", [T, 4], F32, "Internal"),
        scale=0.5,
    )
    ctx = ExitStack()
    P = Prog(nc, ctx)
    emit_l1ffn_sparse(nc, ctx, P, a)
    P.build()
    ctx.close()
    return nc


def emit_l1ffn_sparse(nc, ctx, P, a):
    sb = lambda n, s, d: ctx.enter_context(nc.sbuf_tensor(n, s, d))
    onesbf = sb("onesbf_sb", [1, 128], BF16)
    ident = sb("ident_sb", [128, 128], BF16)
    identf = sb("identf_sb", [128, 128], F32)
    iota8 = sb("iota8_sb", [128, 8], F32)
    trilow = sb("trilow_sb", [128, 128], BF16)
    pidx = sb("pidx_sb", [128, 1], F32)
    fullon = sb("fullon_sb", [128, 128], BF16)
    zini = sb("zini_sb", [128, 192], F32)
    s_cst = P.S("cst")
    P.dma("sync", s_cst, onesbf[:, :], a["onesbf"])
    P.dma("sync", s_cst, ident[:, :], a["identbf"])
    P.dma("sync", s_cst, identf[:, :], a["identf"])
    P.dma("sync", s_cst, iota8[:, :], a["iota8"])
    P.dma("sync", s_cst, trilow[:, :], a["trilow"])
    P.dma("sync", s_cst, pidx[:, :], a["pidx"])
    v_cst = s_cst.n
    P.waits(("tensor", "vector"), [(s_cst, v_cst)])
    s_mz = P.S("mz2")
    P.op("vector", lambda e: e.memset(fullon[:, :], 1.0))
    P.op("vector", lambda e: e.memset(zini[:, :], 0.0), sem=s_mz)
    P.waits(("gpsimd",), [(s_mz, s_mz.n)])
    s_tixi = P.S("tixi", sw=True)
    P.dma("gpsimd", s_tixi, a["tix_d"].rearrange("(g p) c -> p g c", p=128),
          zini[:, :].rearrange("p (g c) -> p g c", c=2))
    v_tixi = s_tixi.n

    xlnT = sb("xlnT", [128, 8, T], BF16)
    lnscope = ExitStack()
    sbL = lambda n, s, d: lnscope.enter_context(nc.sbuf_tensor(n, s, d))
    lnk = LNKit(P, sbL)
    ln_toks = emit_ln_xlnT(nc, lnscope, P, sbL, a, xlnT, ident, lnk, tag="F",
                           xnat_d=a["xnat_d"])

    wrsb = sbL("wrsb", [128, 8, 8], BF16)
    probs = sbL("probs", [128, 8], F32)
    nmax = sbL("nmax", [128, 1], F32)
    sacc = sbL("sacc", [128, 1], F32)
    top8 = sbL("top8", [128, 8], F32)
    tidx = sbL("tidx", [128, 8], U32)
    i1f = sbL("i1f", [128, 1], F32)
    i2f = sbL("i2f", [128, 1], F32)
    nw1 = sbL("nw1", [128, 1], F32)
    nw2 = sbL("nw2", [128, 1], F32)
    oh1f = sbL("oh1f", [128, 8], F32)
    oh2f = sbL("oh2f", [128, 8], F32)
    ohcb = sbL("ohcb", [128, 8], BF16)
    offs = sbL("offs", [128, 8], F32)
    rk = sbL("rk", [128, 8], F32)
    base8 = sbL("base8", [128, 8], F32)
    t8 = sbL("t8", [128, 8], F32)
    pos1 = sbL("pos1", [128, 1], F32)
    pos2 = sbL("pos2", [128, 1], F32)
    lim1 = sbL("lim1", [128, 1], F32)
    cemo = sbL("cemo", [128, 1], F32)
    tval = sbL("tval", [128, 1], F32)
    val1 = sbL("val1", [128, 2], F32)
    val2 = sbL("val2", [128, 2], F32)
    pri1 = sbL("pri1", [128, 1], I32)
    pri2 = sbL("pri2", [128, 1], I32)
    ptrow = sbL("ptrow", [128, 4], F32)
    s_ldr = P.S("ldr", sw=True)
    P.dma("gpsimd", s_ldr, wrsb[:, :, :], a["wr"].rearrange("(c p) h -> p c h", p=128))
    P.waits(("tensor",), [(s_ldr, s_ldr.n)])
    P.waits(("tensor",), ln_toks[0:2])
    P.op("vector", lambda e: e.memset(offs[:, :], 0.0))
    P.op("vector", lambda e: e.memset(cemo[:, :], float(CE - 1)))
    rb = PBank(P, lnscope, nc, "rb", [128, 512])
    pRank = PBank(P, lnscope, nc, "pRank", [128, 8], F32)
    pCnt = PBank(P, lnscope, nc, "pCnt", [128, 8], F32)
    cnt8 = sbL("cnt8", [128, 8], F32)
    s_rv = P.S("rv")
    s_ra = P.S("ra")
    sct1 = DStream(P, "sct1", "gpsimd", 2)
    st_pt = DStream(P, "stpt", "gpsimd", 2)
    ev_tok = [None]
    sc_guard = [None, None]
    pt_guard = [None]
    for s in range(NSTRIP):
        def rstrip(s=s):
            rb.mm_group([(rb.ap[:, 0:8], xlnT[:, cc, ts(s, 128)], wrsb[:, cc, :])
                         for cc in range(8)])
            tok1 = rb.consume("vector", lambda e, p: e.reduce_max(nmax[:, :], p[:, 0:8], axis=AXX))
            P.op("vector", lambda e: e.tensor_scalar_mul(nmax[:, :], nmax[:, :], -1.0), sem=s_rv)
            P.waits(("scalar",), [tok1, (s_rv, s_rv.n)])
            etok = rb.consume("scalar", lambda e, p: e.activation(
                probs[:, :], p[0:128, 0:8], AF.Exp, bias=nmax[:, :], accum_out=sacc[:, :]))
            P.waits(("vector",), [etok])
            P.op("vector", lambda e: e.reciprocal(sacc[:, :], sacc[:, :]))
            P.op("vector", lambda e: e.tensor_scalar_mul(probs[:, :], probs[:, :], sacc[:, :]))
            P.op("vector", lambda e: e.max(out=top8[:, :], in_=probs[:, :]))
            P.op("vector", lambda e: e.max_index(tidx[:, :], top8[:, :], probs[:, :]))
            P.op("vector", lambda e: e.tensor_copy(i1f[:, :], tidx[:, 0:1]))
            P.op("vector", lambda e: e.tensor_copy(i2f[:, :], tidx[:, 1:2]))
            P.op("vector", lambda e: e.tensor_sub(nmax[:, :], top8[:, 0:1], top8[:, 1:2]), sem=s_rv)
            P.wait("scalar", s_rv, s_rv.n)
            P.op("scalar", lambda e: e.activation(nw1[:, :], nmax[:, :], AF.Sigmoid), sem=s_ra)
            P.wait("vector", s_ra, s_ra.n)
            P.op("vector", lambda e: e.tensor_scalar(nw2[:, :], nw1[:, :], -1.0, 1.0,
                                                     ALU.mult, ALU.add))
            P.op("vector", lambda e: e.tensor_tensor(
                oh1f[:, :], iota8[:, :], i1f[:, :].to_broadcast([128, 8]), ALU.is_equal))
            P.op("vector", lambda e: e.tensor_tensor(
                oh2f[:, :], iota8[:, :], i2f[:, :].to_broadcast([128, 8]), ALU.is_equal))
            otok = P.opt("vector", lambda e: e.tensor_add(ohcb[:, :], oh1f[:, :], oh2f[:, :]))
            P.waits(("tensor",), [otok])
            pRank.mm_group([(pRank.ap[:, 0:8], trilow[:, :], ohcb[:, :])])
            pCnt.mm_group([(pCnt.ap[:, 0:8], fullon[:, :], ohcb[:, :])])
            pRank.consume("vector", lambda e, p: e.tensor_copy(rk[:, :], p[:, 0:8]))
            P.op("vector", lambda e: e.tensor_add(base8[:, :], rk[:, :], offs[:, :]))
            pCnt.consume("vector", lambda e, p: e.tensor_copy(cnt8[:, :], p[:, 0:8]))
            P.op("vector", lambda e: e.tensor_add(offs[:, :], offs[:, :], cnt8[:, :]))
            P.op("vector", lambda e, _s=s: e.tensor_scalar(
                tval[:, :], pidx[:, :], 1.0, float(128 * _s), ALU.mult, ALU.add))
            for sl, (ohf, nw, pos, pri, val) in enumerate(
                    [(oh1f, nw1, pos1, pri1, val1), (oh2f, nw2, pos2, pri2, val2)]):
                def slot(ohf=ohf, nw=nw, pos=pos, pri=pri, val=val, sl=sl):
                    if_ = i1f if sl == 0 else i2f
                    P.op("vector", lambda e: e.tensor_mul(t8[:, :], ohf[:, :], base8[:, :]))
                    P.op("vector", lambda e: e.reduce_sum(pos[:, :], t8[:, :], axis=AXX))
                    P.op("vector", lambda e: e.scalar_tensor_tensor(
                        pos[:, :], if_[:, :], float(CE), pos[:, :], ALU.mult, ALU.add))
                    P.op("vector", lambda e: e.scalar_tensor_tensor(
                        lim1[:, :], if_[:, :], float(CE), cemo[:, :], ALU.mult, ALU.add))
                    P.op("vector", lambda e: e.tensor_tensor(
                        pos[:, :], pos[:, :], lim1[:, :], ALU.min))
                    if sc_guard[sl] is not None:
                        P.waits(("vector",), [sc_guard[sl]])
                    P.op("vector", lambda e: e.tensor_copy(pri[:, :], pos[:, :]))
                    P.op("vector", lambda e: e.tensor_copy(val[:, 0:1], tval[:, :]))
                    vtok = P.opt("vector", lambda e: e.tensor_copy(val[:, 1:2], nw[:, :]))
                    P.waits(("gpsimd",), [vtok])
                    if s == 0 and sl == 0:
                        P.waits(("gpsimd",), [(s_tixi, v_tixi)])
                    sc_guard[sl] = sct1.issue_scatter(a["tix_d"], pri[:, :], val[:, :], None)
                slot()
            if pt_guard[0] is not None:
                P.waits(("vector",), [pt_guard[0]])
            P.op("vector", lambda e: e.tensor_copy(ptrow[:, 0:1], pos1[:, :]))
            P.op("vector", lambda e: e.tensor_copy(ptrow[:, 1:2], pos2[:, :]))
            P.op("vector", lambda e: e.tensor_copy(ptrow[:, 2:3], nw1[:, :]))
            ptok = P.opt("vector", lambda e: e.tensor_copy(ptrow[:, 3:4], nw2[:, :]))
            P.waits(("gpsimd",), [ptok])
            pt_guard[0] = st_pt.issue(a["ptd"][ts(s, 128), :], ptrow[:, :])
        rstrip(s)
    P.waits(("sync", "gpsimd", "vector", "tensor", "scalar"),
            [sc_guard[0], sc_guard[1], pt_guard[0]])
    P.waits(("gpsimd", "sync"), ln_toks[2:])
    lnscope.close()

    exsc = ExitStack()
    sbE = lambda n, s, d: exsc.enter_context(nc.sbuf_tensor(n, s, d))
    wg_sb = sbE("wg_sb", [128, 8, 2048], BF16)
    wu_sb = sbE("wu_sb", [128, 8, 2048], BF16)
    wd_sb = sbE("wd_sb", [128, 16, DM], BF16)
    tixs = [sbE(f"tixs{i}", [128, 2], F32) for i in range(2)]
    ixq = [sbE(f"ixq{i}", [128, 1], I32) for i in range(2)]
    xg = [sbE(f"xg{i}", [128, DM], BF16) for i in range(2)]
    xgT = sbE("xgT", [128, 8, 512], BF16)
    sg2 = [sbE(f"sg2_{i}", [128, 512], BF16) for i in range(2)]
    fuseT = sbE("fuseT", [128, 16, 512], BF16)
    ob = [sbE(f"ob{i}", [128, DM], F32) for i in range(2)]

    pg = PBank(P, exsc, nc, "pg", [128, 512])
    pu = PBank(P, exsc, nc, "pu", [128, 512])
    pg2 = PBank(P, exsc, nc, "pg2", [128, 512])
    pu2 = PBank(P, exsc, nc, "pu2", [128, 512])
    pd1 = PBank(P, exsc, nc, "pd1", [128, 512])
    pd2 = PBank(P, exsc, nc, "pd2", [128, 512])
    pTx = [PBank(P, exsc, nc, f"pTx{i}", [128, 128], BF16) for i in range(2)]

    s_wle = P.S("wle", sw=True)
    ld_tix = DStream(P, "ldtix", "sync", 2)
    gthx = DStream(P, "gthx", "gpsimd", 2)
    st_out = DStream(P, "stout", "gpsimd", 2)
    wfree_tok = [None]
    tix_free = [None, None]
    xg_free = [None, None]
    sg_free = [None, None]
    ob_free = [None, None]
    xgT_guard = [None]
    fuse_guard = [None]
    prev_fuse = [None]
    NTT = CE // 512
    for e_ in range(8):
        def expert(e_=e_):
            if wfree_tok[0] is not None:
                P.waits(("gpsimd",), [wfree_tok[0]])
            P.dma("gpsimd", s_wle, wg_sb[:, :, :], a["wg"][e_].rearrange("(c p) f -> p c f", p=128))
            P.dma("gpsimd", s_wle, wu_sb[:, :, :], a["wu"][e_].rearrange("(c p) f -> p c f", p=128))
            P.dma("gpsimd", s_wle, wd_sb[:, :, :], a["wd"][e_].rearrange("(c p) f -> p c f", p=128))
            vw = s_wle.n
            P.waits(("tensor",), [(s_wle, vw)])
            for tt in range(NTT):
                def ttile(tt=tt):
                    if xgT_guard[0] is not None:
                        P.waits(("vector",), [xgT_guard[0]])
                    for q in range(4):
                        def chunk(q=q):
                            bi = q % 2
                            row0 = e_ * CE + tt * 512 + q * 128
                            if tix_free[bi] is not None:
                                P.waits(("sync",), [tix_free[bi]])
                            ttok = ld_tix.issue(tixs[bi][:, :], a["tix_d"][row0:row0 + 128, :])
                            P.waits(("vector",), [ttok])
                            ctok = P.opt("vector", lambda e: e.tensor_copy(
                                ixq[bi][:, :], tixs[bi][:, 0:1]))
                            tix_free[bi] = ctok
                            P.waits(("gpsimd",), [ctok])
                            if xg_free[bi] is not None:
                                P.waits(("gpsimd",), [xg_free[bi]])
                            gtok = gthx.issue_gather(xg[bi][:, :], a["xnat_d"],
                                                     ixq[bi][:, :], None)
                            P.waits(("tensor",), [gtok])
                            for c in range(8):
                                pt_ = pTx[c % 2]
                                pt_.tp_into(xg[bi][:, ts(c, 128)], ident[:, :])
                                pt_.consume("vector", lambda e, p, _c=c, _q=q: e.tensor_copy(
                                    xgT[:, _c, ts(_q, 128)], p[:, 0:128]))
                            xg_free[bi] = (pTx[1].full, pTx[1].fills)
                        chunk(q)
                    xv = P.opt("vector", lambda e: e.nop(), drain=False)
                    P.waits(("tensor",), [xv])
                    if fuse_guard[0] is not None:
                        P.waits(("vector", "scalar"), [fuse_guard[0]])
                    for fc in range(16):
                        def ftile(fc=fc):
                            bg = pg if fc % 2 == 0 else pg2
                            bu = pu if fc % 2 == 0 else pu2
                            sgb = sg2[fc % 2]
                            bg.mm_group([(None, wg_sb[:, cc, ts(fc, 128)], xgT[:, cc, :])
                                         for cc in range(8)])
                            bu.mm_group([(None, wu_sb[:, cc, ts(fc, 128)], xgT[:, cc, :])
                                         for cc in range(8)])
                            if sg_free[fc % 2] is not None:
                                P.waits(("scalar",), [sg_free[fc % 2]])
                            stok = bg.consume("scalar", lambda e, p, _b=sgb: e.activation(
                                _b[:, :], p[:, :], AF.Silu))
                            P.waits(("vector",), [stok])
                            mtok = bu.consume("vector", lambda e, p, _f=fc, _b=sgb: e.tensor_mul(
                                fuseT[:, _f, :], p[:, :], _b[:, :]))
                            sg_free[fc % 2] = mtok
                            prev_fuse[0] = mtok
                        ftile(fc)
                    P.waits(("tensor",), [prev_fuse[0]])
                    for q2 in range(4):
                        def dchunk(q2=q2):
                            bi = q2 % 2
                            row0 = e_ * CE + tt * 512 + q2 * 128
                            if ob_free[bi] is not None:
                                P.waits(("vector",), [ob_free[bi]])
                            for dh in range(2):
                                bk = pd1 if dh == 0 else pd2
                                bk.mm_group([(None, fuseT[:, ft2, ts(q2, 128)],
                                              wd_sb[:, ft2, ts(dh, 512)]) for ft2 in range(16)])
                                bk.consume("vector", lambda e, p, _d=dh, _o=ob[bi]:
                                           e.tensor_copy(_o[:, ts(_d, 512)], p[:, :]))
                            ov = P.opt("vector", lambda e: e.nop(), drain=False)
                            P.waits(("gpsimd",), [ov])
                            ob_free[bi] = st_out.issue(a["out_d"][row0:row0 + 128, :],
                                                       ob[bi][:, :])
                        dchunk(q2)
                    fuse_guard[0] = (pd2.free, pd2.freed_target)
                    xgT_guard[0] = (pu2.full, pu2.fills)
                ttile(tt)
            wfree_tok[0] = (pd2.free, pd2.freed_target)
        expert(e_)
    P.waits(("sync", "vector", "gpsimd"), st_out.tokens())
    exsc.close()

    fsc = ExitStack()
    sbF = lambda n, s, d: fsc.enter_context(nc.sbuf_tensor(n, s, d))
    xs2 = [sbF(f"xs2_{i}", [128, DM], F32) for i in range(2)]
    g1b = [sbF(f"g1b{i}", [128, DM], F32) for i in range(2)]
    g2b = [sbF(f"g2b{i}", [128, DM], F32) for i in range(2)]
    ptl = [sbF(f"ptl{i}", [128, 4], F32) for i in range(2)]
    pi1 = [sbF(f"pi1_{i}", [128, 1], I32) for i in range(2)]
    pi2 = [sbF(f"pi2_{i}", [128, 1], I32) for i in range(2)]
    ld_x2 = DStream(P, "ldx2", "sync", 2)
    ld_pt = DStream(P, "ldpt", "sync", 2)
    gth1 = DStream(P, "gth1", "gpsimd", 2)
    gth2f = DStream(P, "gth2f", "gpsimd", 2)
    st_y = DStream(P, "sty", "gpsimd", 2)
    y_tok = [None, None]
    for s in range(NSTRIP):
        def fstrip(s=s):
            bi = s % 2
            if y_tok[bi] is not None:
                P.waits(("sync", "vector", "gpsimd"), [y_tok[bi]])
            xtok = ld_x2.issue(xs2[bi][:, :], a["x"][ts(s, 128), :])
            ptok = ld_pt.issue(ptl[bi][:, :], a["ptd"][ts(s, 128), :])
            P.waits(("vector",), [ptok])
            c1 = P.opt("vector", lambda e: e.tensor_copy(pi1[bi][:, :], ptl[bi][:, 0:1]))
            c2 = P.opt("vector", lambda e: e.tensor_copy(pi2[bi][:, :], ptl[bi][:, 1:2]))
            P.waits(("gpsimd",), [c1, c2])
            g1t = gth1.issue_gather(g1b[bi][:, :], a["out_d"], pi1[bi][:, :], None)
            g2t = gth2f.issue_gather(g2b[bi][:, :], a["out_d"], pi2[bi][:, :], None)
            P.waits(("vector",), [xtok, g1t, g2t])
            P.op("vector", lambda e: e.tensor_scalar_mul(
                g1b[bi][:, :], g1b[bi][:, :], ptl[bi][:, 2:3]))
            P.op("vector", lambda e: e.scalar_tensor_tensor(
                g1b[bi][:, :], g2b[bi][:, :], ptl[bi][:, 3:4], g1b[bi][:, :],
                ALU.mult, ALU.add))
            ytok = P.opt("vector", lambda e: e.scalar_tensor_tensor(
                g1b[bi][:, :], g1b[bi][:, :], a["scale"], xs2[bi][:, :],
                ALU.mult, ALU.add))
            P.waits(("gpsimd",), [ytok])
            y_tok[bi] = st_y.issue(a["y"][ts(s, 128), :], g1b[bi][:, :])
        fstrip(s)
    P.waits(("gpsimd",), st_y.tokens())
    fsc.close()


LAT_SLOTS = [(J, K) for K in range(8) for J in range(4 * K, 32)]
LAT_SLOT = {jk: i for i, jk in enumerate(LAT_SLOTS)}


def build_l0attn():
    nc = bass.Bass("TRN2", target_bir_lowering=False)
    dr = lambda n, s, d, k="ExternalInput": nc.dram_tensor(n, s, d, kind=k).ap()
    a = dict(
        x=dr("x", [T, DM], F32),
        y=dr("y", [T, DM], F32, "ExternalOutput"),
        wqkvh=dr("wqkvh", [8, DM, 320], F32),
        wvc=dr("wvc", [8, DM, 64], F32),
        woutT=dr("woutT", [512, DM], F32),
        identbf=dr("identbf", [128, 128], BF16),
        t1t_mask=dr("t1t_mask", [128, 2, 256], BF16),
        lat_mask=dr("lat_mask", [128, 4, 512], BF16),
        otd=nc.dram_tensor("otd", [4, 128, T], BF16, kind="Internal").ap(),
        causal_mask=dr("causal_mask", [128, 4, 512], BF16),
        lat_d=dr("lat_d", [len(LAT_SLOTS) * 128, 512], BF16, "Internal"),
        scale=1.0,
    )
    ctx = ExitStack()
    P = Prog(nc, ctx)
    emit_l0attn(nc, ctx, P, a)
    P.build()
    ctx.close()
    return nc


def emit_l0attn(nc, ctx, P, a):
    sb = lambda n, s, d: ctx.enter_context(nc.sbuf_tensor(n, s, d))
    ident = sb("ident_sb", [128, 128], BF16)
    t1m = sb("t1m_sb", [128, 2, 256], BF16)
    latm = sb("latm_sb", [128, 4, 512], BF16)
    caum = sb("caum_sb", [128, 4, 512], BF16)
    s_cst = P.S("cst")
    P.dma("sync", s_cst, ident[:, :], a["identbf"])
    P.dma("sync", s_cst, t1m[:, :, :], a["t1t_mask"])
    P.dma("sync", s_cst, latm[:, :, :], a["lat_mask"])
    P.dma("sync", s_cst, caum[:, :, :], a["causal_mask"])
    P.waits(("tensor", "vector", "scalar"), [(s_cst, s_cst.n)])

    xlnT = sb("xlnT", [128, 8, T], BF16)
    lnscope = ExitStack()
    sbL = lambda n, s, d: lnscope.enter_context(nc.sbuf_tensor(n, s, d))
    lnk = LNKit(P, sbL)
    ln_toks = emit_ln_xlnT(nc, lnscope, P, sbL, a, xlnT, ident, lnk, tag="A")
    P.waits(("tensor",), ln_toks[0:2])
    lnscope.close()

    vuT = sb("vuT", [64, T], BF16)
    qcsT = sb("qcsT", [64, T], BF16)
    kcT = sb("kcT", [64, T], BF16)
    vc_sb = sb("vc_sb", [128, 32, 64], BF16)
    otsb = [sb(f"otsb{i}", [64, 128], BF16) for i in range(2)]
    t1tb = sb("t1tb", [128, 32, 256], BF16)
    scoreb = [sb(f"scoreb{i}", [128, T], F32) for i in range(2)]
    pbuf = sb("pbuf", [128, T], BF16)
    latc = [sb(f"latc{i}", [128, 4, 512], BF16) for i in range(2)]
    latw = sb("latw", [128, 512], BF16)
    silu_s = [sb(f"silu_s{i}", [128, 512], F32) for i in range(2)]
    negmax = sb("negmaxA", [128, 1], F32)
    rl = sb("rlA", [128, 1], F32)
    lsum = sb("lsumA", [128, 1], F32)
    osb = sb("osbA", [128, 64], BF16)
    ptb = [sb(f"ptbA{i}", [128, 128], BF16) for i in range(2)]

    pA = PBank(P, ctx, nc, "pA", [128, 512])
    pB = PBank(P, ctx, nc, "pB", [128, 512])
    pS0 = PBank(P, ctx, nc, "pS0", [128, 512])
    pS1 = PBank(P, ctx, nc, "pS1", [128, 512])
    pT0 = PBank(P, ctx, nc, "pT0", [128, 128], BF16)
    pT1 = PBank(P, ctx, nc, "pT1", [128, 128], BF16)
    pO0 = PBank(P, ctx, nc, "pO0", [128, 64])
    pO1 = PBank(P, ctx, nc, "pO1", [128, 64])

    ld_lat2 = [DStream(P, "ldlatA", "sync", 4), DStream(P, "ldlatB", "sync", 4)]
    chunk_ctr = [0]
    chunk_guard = [None, None]
    st_lat = DStream(P, "stlat", "gpsimd", 2)
    ld_w = DStream(P, "ldw", "gpsimd", 2)
    st_ot = DStream(P, "stot", "gpsimd", 2)
    ot_ctr = [0]
    ot_st_tok = [None, None]
    s_latuse = P.S("latuse")

    sc_free = [None, None]
    p_free = [None]
    silu_free = [None, None]
    osb_free = [None]
    t1_guard = [None]
    lat_st_toks = {}

    for h in range(8):
        def head(h=h):
            hs = ExitStack()
            sbH = lambda n, s, d: hs.enter_context(nc.sbuf_tensor(n, s, d))
            qusT = sbH(f"qusT{h}", [64, T], BF16)
            kuT = sbH(f"kuT{h}", [64, T], BF16)
            wq = sbH(f"wq{h}", [128, 8, 320], BF16)
            wv = sbH(f"wv{h}", [128, 8, 64], BF16)
            if t1_guard[0] is not None:
                P.waits(("gpsimd",), [t1_guard[0]])
                P.waits(("gpsimd",), ld_lat2[0].tokens() + ld_lat2[1].tokens())
            tokw = ld_w.issue(wq[:, :, :], a["wqkvh"][h].rearrange("(c p) m -> p c m", p=128))
            tokv = ld_w.issue(wv[:, :, :], a["wvc"][h].rearrange("(c p) m -> p c m", p=128))
            P.waits(("tensor",), [tokw, tokv])
            for tt in range(8):
                def qkvtile(tt=tt):
                    for mt in range(3):
                        bk = pA if mt % 2 == 0 else pB
                        mwid = 128 if mt < 2 else 64
                        bk.mm_group([(bk.ap[0:mwid, :], wq[:, cc, mt*128:mt*128+mwid],
                                      xlnT[:, cc, ts(tt, 512)]) for cc in range(8)])
                        if mt == 0:
                            bk.consume("vector", lambda e, p, _t=tt: e.tensor_scalar_mul(
                                qusT[:, ts(_t, 512)], p[0:64, :], 0.125))
                            bk.consume("vector", lambda e, p, _t=tt: e.tensor_copy(
                                kuT[:, ts(_t, 512)], p[64:128, :]))
                        elif mt == 1:
                            bk.consume("vector", lambda e, p, _t=tt: e.tensor_copy(
                                vuT[:, ts(_t, 512)], p[0:64, :]))
                            bk.consume("vector", lambda e, p, _t=tt: e.tensor_scalar_mul(
                                qcsT[:, ts(_t, 512)], p[64:128, :], 0.125))
                        else:
                            bk.consume("vector", lambda e, p, _t=tt: e.tensor_copy(
                                kcT[:, ts(_t, 512)], p[0:64, :]))
                    for q in range(4):
                        st_i = tt * 4 + q
                        bk = pO0 if q % 2 == 0 else pO1
                        bk.mm_group([(None, xlnT[:, cc, ts(st_i, 128)], wv[:, cc, :])
                                     for cc in range(8)])
                        bk.consume("vector", lambda e, p, _s=st_i: e.tensor_copy(
                            vc_sb[:, _s, :], p[:, :]))
                qkvtile(tt)
            for J in range(32):
                for K in range(min(J // 4, 7) + 1):
                    def lat_tile(J=J, K=K):
                        slot = LAT_SLOT[(J, K)]
                        bk = pA if (J + K) % 2 == 0 else pB
                        prev = lat_st_toks.get("last")
                        if prev is not None:
                            P.waits(("scalar", "vector"), [prev])
                        if h > 0:
                            pass
                        bk.mm_group([(None, kuT[:, ts(J, 128)], qusT[:, ts(K, 512)])])
                        if J < 4 * K + 4:
                            stok = bk.consume("scalar", lambda e, p: e.activation(
                                silu_s[0][:, :], p[:, :], AF.Sigmoid))
                            P.waits(("vector",), [stok])
                            vtok = P.opt("vector", lambda e, _m=J-4*K: e.tensor_tensor(
                                latw[:, :], silu_s[0][:, :], latm[:, _m, :], ALU.mult))
                            P.waits(("gpsimd",), [vtok])
                            silu_free[0] = vtok
                        else:
                            ctok = bk.consume("scalar", lambda e, p: e.activation(
                                latw[:, :], p[:, :], AF.Sigmoid))
                            P.waits(("gpsimd",), [ctok])
                        t = st_lat.issue(a["lat_d"][slot*128:(slot+1)*128, :], latw[:, :])
                        lat_st_toks[slot] = t
                        lat_st_toks["last"] = t
                    lat_tile(J, K)
            hs.close()
            for g in range(16):
                def group(g=g):
                    s0, s1 = 2 * g, 2 * g + 1
                    for J in range(s1 + 1):
                        def t1tile(J=J):
                            bk = pA if J % 2 == 0 else pB
                            bk.mm_group([(bk.ap[:, 0:256], vuT[:, ts(J, 128)],
                                          qcsT[:, g*256:g*256+256])])
                            if t1_guard[0] is not None and J <= 1:
                                P.waits(("vector",), [t1_guard[0]])
                            if J >= s0:
                                bk.consume("vector", lambda e, p, _J=J, _m=J-s0: e.tensor_tensor(
                                    t1tb[:, _J, :], p[:, 0:256], t1m[:, _m, :], ALU.mult))
                            else:
                                bk.consume("vector", lambda e, p, _J=J: e.tensor_copy(
                                    t1tb[:, _J, :], p[:, 0:256]))
                        t1tile(J)
                    P.waits(("tensor",), [(pA.free, pA.freed_target),
                                          (pB.free, pB.freed_target)])
                    km0, km1 = s0 // 4, s1 // 4
                    for K in range(km1 + 1):
                        def ktile(K=K):
                            do0 = K <= km0
                            if do0:
                                pS0.acc_start()
                            pS1.acc_start()
                            Js = list(range(4 * K, s1 + 1))
                            chunks = [Js[i:i+4] for i in range(0, len(Js), 4)]
                            for ci, ch in enumerate(chunks):
                                def chunk(ci=ci, ch=ch):
                                    par = chunk_ctr[0] % 2
                                    cb = latc[par]
                                    strm = ld_lat2[par]
                                    g2 = chunk_guard[par]
                                    if g2 is not None:
                                        P.waits(("sync",), [g2])
                                    chunk_ctr[0] += 1
                                    toks = []
                                    for q, J in enumerate(ch):
                                        slot = LAT_SLOT[(J, K)]
                                        P.waits(("sync",), [lat_st_toks[slot]])
                                        toks.append(strm.issue(
                                            cb[:, q, :], a["lat_d"][slot*128:(slot+1)*128, :]))
                                    P.waits(("tensor",), toks)
                                    lastc = (ci == len(chunks) - 1)
                                    for q, J in enumerate(ch):
                                        first = (ci == 0 and q == 0)
                                        last = (lastc and q == len(ch) - 1)
                                        if do0:
                                            pS0.acc_mm(None, t1tb[:, J, 0:128], cb[:, q, :],
                                                       first, last)
                                        if last:
                                            pS1.acc_mm(None, t1tb[:, J, 128:256], cb[:, q, :],
                                                       first, True)
                                            chunk_guard[(chunk_ctr[0] - 1) % 2] = (pS1.full, pS1.fills)
                                        elif q == len(ch) - 1:
                                            v = P.mm(pS1.ap[:, :], t1tb[:, J, 128:256], cb[:, q, :],
                                                     first, False, sem=s_latuse)
                                            chunk_guard[(chunk_ctr[0] - 1) % 2] = (s_latuse, v)
                                        else:
                                            pS1.acc_mm(None, t1tb[:, J, 128:256], cb[:, q, :],
                                                       first, False)
                                chunk(ci, ch)
                            for sidx, bk, s_ in ((0, pS0, s0), (1, pS1, s1)):
                                if sidx == 0 and not do0:
                                    continue
                                def evac(bk=bk, s_=s_, sidx=sidx):
                                    if silu_free[sidx] is not None:
                                        P.waits(("scalar",), [silu_free[sidx]])
                                    stok = bk.consume("scalar", lambda e, p, _i=sidx: e.activation(
                                        silu_s[_i][:, :], p[:, :], AF.Silu))
                                    ck = pA if sidx == 0 else pB
                                    ck.mm_group([(None, qcsT[:, ts(s_, 128)], kcT[:, ts(K, 512)])])
                                    P.waits(("vector",), [stok])
                                    if K == 0 and sc_free[s_ % 2] is not None:
                                        P.waits(("vector",), [sc_free[s_ % 2]])
                                    silu_free[sidx] = ck.consume(
                                        "vector", lambda e, p, _s=s_, _K=K, _i=sidx:
                                        e.scalar_tensor_tensor(
                                            scoreb[_s % 2][:, ts(_K, 512)], p[:, :], 1.0,
                                            silu_s[_i][:, :], ALU.mult, ALU.subtract))
                                    if K == s_ // 4:
                                        P.op("vector", lambda e, _s=s_, _K=K: e.tensor_add(
                                            scoreb[_s % 2][:, ts(_K, 512)],
                                            scoreb[_s % 2][:, ts(_K, 512)],
                                            caum[:, _s % 4, :]))
                                evac()
                        ktile(K)
                    for s_ in (s0, s1):
                        def smax(s_=s_):
                            scb = scoreb[s_ % 2]
                            Ls = 512 * (s_ // 4 + 1)
                            P.op("vector", lambda e, _L=Ls, _b=scb: e.reduce_max(
                                negmax[:, :], _b[:, 0:_L], axis=AXX))
                            ntok = P.opt("vector", lambda e: e.tensor_scalar_mul(
                                negmax[:, :], negmax[:, :], -1.0))
                            P.waits(("scalar",), [ntok])
                            if p_free[0] is not None:
                                P.waits(("scalar",), [p_free[0]])
                            etok = P.opt("scalar", lambda e, _L=Ls, _b=scb: e.activation(
                                pbuf[:, 0:_L], _b[:, 0:_L], AF.Exp, bias=negmax[:, :],
                                accum_out=lsum[:, :]))
                            sc_free[s_ % 2] = etok
                            P.waits(("vector",), [etok])
                            P.op("vector", lambda e: e.reciprocal(rl[:, :], lsum[:, :]))
                            pstok = P.opt("vector", lambda e, _L=Ls: e.tensor_scalar_mul(
                                pbuf[:, 0:_L], pbuf[:, 0:_L], rl[:, :]))
                            P.waits(("tensor",), [pstok])
                            ob = pO0 if s_ % 2 == 0 else pO1
                            ob.acc_start()
                            nk = Ls // 128
                            for kt in range(nk):
                                def pv(kt=kt):
                                    pt = pT0 if kt % 2 == 0 else pT1
                                    pt.tp_into(pbuf[:, ts(kt, 128)], ident[:, :])
                                    pt.consume("vector", lambda e, p, _k=kt: e.tensor_copy(
                                        ptb[_k % 2][:, :], p[:, 0:128]))
                                    P.wait("tensor", pt.free, pt.freed_target)
                                    ob.acc_mm(None if False else ob.ap[:, 0:64],
                                              ptb[kt % 2][:, :], vc_sb[:, kt, :],
                                              kt == 0, kt == nk - 1)
                                pv(kt)
                            if osb_free[0] is not None:
                                P.waits(("scalar",), [osb_free[0]])
                            ob.consume("scalar", lambda e, p: e.copy(osb[:, :], p[:, 0:64]))
                            P.wait("tensor", ob.free, ob.freed_target)
                            pt = pT0
                            pt.tp_into(osb[:, 0:64], ident[:, :])
                            osb_free[0] = (pt.full, pt.fills)
                            bi = ot_ctr[0] % 2
                            ot_ctr[0] += 1
                            if ot_st_tok[bi] is not None:
                                P.waits(("vector",), [ot_st_tok[bi]])
                            etok = pt.consume("vector", lambda e, p, _b=bi: e.tensor_copy(
                                otsb[_b][:, :], p[0:64, 0:128]))
                            P.waits(("gpsimd",), [etok])
                            ot_st_tok[bi] = st_ot.issue(
                                a["otd"][h // 2, 64*(h % 2):64*(h % 2)+64, ts(s_, 128)],
                                otsb[bi][:, :])
                            p_free[0] = (pt.full, pt.fills)
                        smax(s_)
                    t1_guard[0] = (s_latuse, s_latuse.n)
                group(g)
        head(h)

    P.waits(("sync",), st_ot.tokens())
    woutT_sb = sb("woutT_sb", [128, 4, DM], BF16)
    otl = [sb(f"otl{i}", [128, 4, 128], BF16) for i in range(2)]
    s_cstg = P.S("cstg", sw=True)
    P.dma("gpsimd", s_cstg, woutT_sb[:, :, :], a["woutT"].rearrange("(c p) h -> p c h", p=128))
    P.waits(("tensor",), [(s_cstg, s_cstg.n)])
    ld_x2 = DStream(P, "ldx2", "sync", 2)
    ld_ot = DStream(P, "ldot", "sync", 2)
    st_y2 = DStream(P, "sty2", "gpsimd", 2)
    xs2 = [sb(f"xs2_{i}", [128, DM], F32) for i in range(2)]
    ys2 = [sb(f"ys2_{i}", [128, DM], F32) for i in range(2)]
    y_tok2 = [None, None]
    x_free2 = [None, None]
    ot_free2 = [None, None]
    for s in range(NSTRIP):
        def ostrip(s=s):
            xb, yb, ob_ = xs2[s % 2], ys2[s % 2], otl[s % 2]
            if x_free2[s % 2] is not None:
                P.waits(("sync",), [x_free2[s % 2]])
            tok = ld_x2.issue(xb[:, :], a["x"][ts(s, 128), :])
            if ot_free2[s % 2] is not None:
                P.waits(("sync",), [ot_free2[s % 2]])
            tok2 = ld_ot.issue(ob_[:, :, :],
                               a["otd"][:, :, ts(s, 128)].rearrange("c p t -> p c t"))
            P.waits(("tensor",), [tok2])
            if y_tok2[s % 2] is not None:
                P.waits(("vector",), [y_tok2[s % 2]])
            last_v = None
            for dh in range(2):
                bk = pA if dh == 0 else pB
                bk.mm_group([(None, ob_[:, cc, :], woutT_sb[:, cc, ts(dh, 512)])
                             for cc in range(4)])
                if dh == 0:
                    P.waits(("vector",), [tok])
                last_v = bk.consume("vector", lambda e, p, _d=dh, _y=yb, _x=xb:
                                    e.scalar_tensor_tensor(
                                        _y[:, ts(_d, 512)], p[:, :], a["scale"],
                                        _x[:, ts(_d, 512)], ALU.mult, ALU.add))
            x_free2[s % 2] = last_v
            ot_free2[s % 2] = (pB.full, pB.fills)
            P.waits(("gpsimd",), [last_v])
            y_tok2[s % 2] = st_y2.issue(a["y"][ts(s, 128), :], yb[:, :])
        ostrip(s)
    P.waits(("gpsimd",), st_y2.tokens())


def build_l1attn():
    nc = bass.Bass("TRN2", target_bir_lowering=False)
    dr = lambda n, s, d, k="ExternalInput": nc.dram_tensor(n, s, d, kind=k).ap()
    a = dict(
        x=dr("x", [T, DM], F32),
        y=dr("y", [T, DM], F32, "ExternalOutput"),
        wdkvkr=dr("wdkvkr", [DM, 288], F32),
        wdq=dr("wdq", [DM, 512], F32),
        wqr=dr("wqr", [512, 256], F32),
        widxq=dr("widxq", [DM, 256], F32),
        widxk=dr("widxk", [DM, 64], F32),
        wuqT=dr("wuqT", [512, 512], F32),
        wukT=dr("wukT", [512, 256], F32),
        wuv=dr("wuv", [256, 512], F32),
        wout=dr("wout", [512, DM], F32),
        idxw=dr("idxw", [1, 4], F32),
        rawdelta=dr("rawdelta", [1, 16], F32),
        cosT=dr("cosT", [128, 32, 16], F32), sinT=dr("sinT", [128, 32, 16], F32),
        cosB=dr("cosB", [128, 32, 16], F32), sinB=dr("sinB", [128, 32, 16], F32),
        cosS=dr("cosS", [128, 32, 16], F32), sinS=dr("sinS", [128, 32, 16], F32),
        cosR=dr("cosR", [128, 3, 32, 16], F32), sinR=dr("sinR", [128, 3, 32, 16], F32),
        identbf=dr("identbf", [128, 128], BF16),
        onesf=dr("onesf", [1, 128], F32),
        bandmask=dr("bandmask", [128, 256], BF16),
        imaskd=dr("imaskd", [128, 4, 512], BF16),
        imaskp=dr("imaskp", [128, 512], BF16),
        s0add=dr("s0add", [128, 192], F32),
        qiT_d=dr("qiT_d", [NSTRIP, 128, 2, 128], BF16, "Internal"),
        cqd=dr("cqd", [NSTRIP, 128, 4, 128], BF16, "Internal"),
        ckd=dr("ckd", [T, 288], BF16, "Internal"),
        otd=dr("otd", [4, 128, T], BF16, "Internal"),
        scale=1.0,
    )
    ctx = ExitStack()
    P = Prog(nc, ctx)
    emit_l1attn(nc, ctx, P, a)
    P.build()
    ctx.close()
    return nc


def _rot(P, out_re, out_im, re, im, cos, sin, tmp1, tmp2):
    P.op("vector", lambda e: e.tensor_tensor(tmp1, re, cos, ALU.mult))
    P.op("vector", lambda e: e.tensor_tensor(tmp2, im, sin, ALU.mult))
    P.op("vector", lambda e: e.tensor_sub(out_re, tmp1, tmp2))
    P.op("vector", lambda e: e.tensor_tensor(tmp1, re, sin, ALU.mult))
    P.op("vector", lambda e: e.tensor_tensor(tmp2, im, cos, ALU.mult))
    return P.opt("vector", lambda e: e.tensor_add(out_im, tmp1, tmp2))


def _softplus(P, dst, src, tmp):
    P.op("scalar", lambda e: e.activation(tmp, src, AF.Abs))
    P.op("scalar", lambda e: e.activation(tmp, tmp, AF.Exp, scale=-1.0))
    P.op("scalar", lambda e: e.activation(tmp, tmp, AF.Ln, bias=1.0))
    tok = P.opt("scalar", lambda e: e.activation(dst, src, AF.Relu))
    P.waits(("vector",), [tok])
    return P.opt("vector", lambda e: e.tensor_add(dst, dst, tmp))


def emit_l1attn(nc, ctx, P, a):
    sb = lambda n, s, d: ctx.enter_context(nc.sbuf_tensor(n, s, d))
    SCL = float((64 + 32) ** -0.5)

    ident = sb("ident_sb", [128, 128], BF16)
    onesf = sb("onesf_sb", [1, 128], F32)
    bandm = sb("bandm_sb", [128, 256], BF16)
    imaskd = sb("imaskd_sb", [128, 4, 512], BF16)
    imaskp = sb("imaskp_sb", [128, 512], BF16)
    tabs = {}
    s_cst = P.S("cst")
    P.dma("sync", s_cst, ident[:, :], a["identbf"])
    P.dma("sync", s_cst, onesf[:, :], a["onesf"])
    P.dma("sync", s_cst, bandm[:, :], a["bandmask"])
    P.dma("sync", s_cst, imaskd[:, :, :], a["imaskd"])
    P.dma("sync", s_cst, imaskp[:, :], a["imaskp"])
    for nm in ("cosT", "sinT", "cosB", "sinB", "cosS", "sinS"):
        tabs[nm] = sb(nm + "_sb", [128, 32, 16], F32)
        P.dma("sync", s_cst, tabs[nm][:, :, :], a[nm])
    tabRc = sb("tabRc", [128, 3, 32, 16], F32)
    tabRs = sb("tabRs", [128, 3, 32, 16], F32)
    P.dma("sync", s_cst, tabRc[:, :, :, :], a["cosR"])
    P.dma("sync", s_cst, tabRs[:, :, :, :], a["sinR"])
    idxw_sb = sb("idxw_sb", [1, 4], F32)
    rdel = sb("rdel_sb", [1, 16], F32)
    s0add = sb("s0add_sb", [128, 192], F32)
    P.dma("sync", s_cst, idxw_sb[:, :], a["idxw"])
    P.dma("sync", s_cst, rdel[:, :], a["rawdelta"])
    P.dma("sync", s_cst, s0add[:, :], a["s0add"])
    v_cst = s_cst.n
    P.waits(("tensor", "vector", "scalar"), [(s_cst, v_cst)])

    pA = PBank(P, ctx, nc, "pA", [128, 512])
    pB = PBank(P, ctx, nc, "pB", [128, 512])
    pC = PBank(P, ctx, nc, "pC", [128, 512])
    pD = PBank(P, ctx, nc, "pD", [128, 512])
    pT0 = PBank(P, ctx, nc, "pT0", [128, 128], BF16)
    pT1 = PBank(P, ctx, nc, "pT1", [128, 128], BF16)
    pO0 = PBank(P, ctx, nc, "pO0", [128, 64])
    pRep = PBank(P, ctx, nc, "pRep", [128, 64])

    dl = sb("dl", [1, 16], F32)
    shh = sb("shh", [1, 16], F32)
    chh = sb("chh", [1, 16], F32)
    cosd = sb("cosd", [1, 16], F32)
    sind = sb("sind", [1, 16], F32)
    cosd_r = sb("cosd_r", [128, 16], F32)
    sind_r = sb("sind_r", [128, 16], F32)
    wrep4 = sb("wrep4", [128, 4], F32)
    dsc = ExitStack()
    sbD = lambda n, s, d: dsc.enter_context(nc.sbuf_tensor(n, s, d))
    tt1 = sbD("tt1", [128, 32, 16], F32)
    tt2 = sbD("tt2", [128, 32, 16], F32)
    tt3 = sbD("tt3", [128, 32, 16], F32)

    t1 = P.opt("scalar", lambda e: e.activation(dl[:, :], rdel[:, :], AF.Sigmoid))
    P.waits(("vector",), [t1])
    t2 = P.opt("vector", lambda e: e.tensor_scalar_mul(dl[:, :], dl[:, :], -2.0 * float(np.pi)))
    P.waits(("scalar",), [t2])
    P.op("scalar", lambda e: e.activation(shh[:, :], dl[:, :], AF.Sin, scale=0.5))
    t2b = P.opt("scalar", lambda e: e.mul(chh[:, :], dl[:, :], 0.5))
    P.waits(("vector",), [t2b])
    t2c = P.opt("vector", lambda e: e.tensor_scalar_add(chh[:, :], chh[:, :],
                                                        float(np.pi / 2)))
    P.waits(("scalar",), [t2c])
    t3 = P.opt("scalar", lambda e: e.activation(chh[:, :], chh[:, :], AF.Sin))
    P.waits(("vector",), [t3])
    P.op("vector", lambda e: e.scalar_tensor_tensor(cosd[:, :], shh[:, :], -2.0, shh[:, :],
                                                    ALU.mult, ALU.mult))
    P.op("vector", lambda e: e.tensor_scalar_add(cosd[:, :], cosd[:, :], 1.0))
    t4 = P.opt("vector", lambda e: e.scalar_tensor_tensor(sind[:, :], shh[:, :], 2.0,
                                                          chh[:, :], ALU.mult, ALU.mult))
    P.waits(("tensor",), [t4])
    pRep.mm_group([(pRep.ap[0:128, 0:16], onesf[:, :], cosd[:, :])])
    pRep.consume("vector", lambda e, p: e.tensor_copy(cosd_r[:, :], p[:, 0:16]))
    pRep.mm_group([(pRep.ap[0:128, 0:16], onesf[:, :], sind[:, :])])
    pRep.consume("vector", lambda e, p: e.tensor_copy(sind_r[:, :], p[:, 0:16]))
    pRep.mm_group([(pRep.ap[0:128, 0:4], onesf[:, :], idxw_sb[:, :])])
    pRep.consume("vector", lambda e, p: e.tensor_copy(wrep4[:, :], p[:, 0:4]))

    cb = cosd_r[:, :].unsqueeze(1).to_broadcast([128, 32, 16])
    sb_ = sind_r[:, :].unsqueeze(1).to_broadcast([128, 32, 16])
    s_tb = P.S("tb")

    def delta_combine2(ct, st):
        P.op("vector", lambda e: e.tensor_tensor(tt1[:, :, :], ct, cb, ALU.mult))
        P.op("vector", lambda e: e.tensor_tensor(tt2[:, :, :], st, sb_, ALU.mult))
        P.op("vector", lambda e: e.tensor_sub(tt3[:, :, :], tt1[:, :, :], tt2[:, :, :]))
        P.op("vector", lambda e: e.tensor_tensor(tt1[:, :, :], st, cb, ALU.mult))
        P.op("vector", lambda e: e.tensor_tensor(tt2[:, :, :], ct, sb_, ALU.mult))
        P.op("vector", lambda e: e.tensor_add(st, tt1[:, :, :], tt2[:, :, :]))
        P.op("vector", lambda e: e.tensor_copy(ct, tt3[:, :, :]), sem=s_tb)

    delta_combine2(tabs["cosT"][:, :, :], tabs["sinT"][:, :, :])
    for b in range(3):
        delta_combine2(tabRc[:, b, :, :], tabRs[:, b, :, :])
    v_tb = s_tb.n
    dsc.close()

    ckvT = sb("ckvT", [128, 2, T], BF16)
    cknat = sb("cknat", [128, 32, 288], BF16)
    vT = sb("vT", [32, T], BF16)
    kiT = sb("kiT", [128, T], BF16)
    wth = sb("wth", [128, 32, 256], BF16)
    wuv_sb = sb("wuv_sb", [128, 2, 512], BF16)
    wqr_sb = sb("wqr_sb", [128, 4, 256], BF16)

    ln1 = ExitStack()
    sbL = lambda n, s, d: ln1.enter_context(nc.sbuf_tensor(n, s, d))
    xlnT = sbL("xlnT", [128, 8, T], BF16)
    lnk = LNKit(P, sbL)
    ln_toks = emit_ln_xlnT(nc, ln1, P, sbL, a, xlnT, ident, lnk, tag="M", banks=(pT0, pT1))
    P.waits(("tensor",), ln_toks[0:2])

    wsc = ExitStack()
    sbW = lambda n, s, d: wsc.enter_context(nc.sbuf_tensor(n, s, d))
    wdkv_sb = sbW("wdkv_sb", [128, 8, 288], BF16)
    wdq_sb = sbW("wdq_sb", [128, 8, 512], BF16)
    widxq_sb = sbW("widxq_sb", [128, 8, 256], BF16)
    widxk_sb = sbW("widxk_sb", [128, 8, 64], BF16)
    wuqT_sb = sbW("wuqT_sb", [128, 4, 512], BF16)
    wukT_sb = sbW("wukT_sb", [128, 4, 256], BF16)
    lnsq = sbW("lnsq", [128, 288], F32)
    sp1t = sbW("sp1t", [128, 32], F32)
    vband = sbW("vband", [128, 32], BF16)
    qi_sb2 = [sbW(f"qi_sb{i}", [128, 2, 128], BF16) for i in range(2)]
    cqb2 = [sbW(f"cqb{i}", [128, 4, 128], BF16) for i in range(2)]
    s_ldp = P.S("ldp", sw=True)
    P.dma("gpsimd", s_ldp, wdkv_sb[:, :, :], a["wdkvkr"].rearrange("(c p) h -> p c h", p=128))
    P.dma("gpsimd", s_ldp, wdq_sb[:, :, :], a["wdq"].rearrange("(c p) h -> p c h", p=128))
    P.dma("gpsimd", s_ldp, widxq_sb[:, :, :], a["widxq"].rearrange("(c p) h -> p c h", p=128))
    P.dma("gpsimd", s_ldp, widxk_sb[:, :, :], a["widxk"].rearrange("(c p) h -> p c h", p=128))
    P.dma("gpsimd", s_ldp, wuqT_sb[:, :, :], a["wuqT"].rearrange("(c p) h -> p c h", p=128))
    P.dma("gpsimd", s_ldp, wukT_sb[:, :, :], a["wukT"].rearrange("(c p) h -> p c h", p=128))
    P.dma("gpsimd", s_ldp, wuv_sb[:, :, :], a["wuv"].rearrange("(c p) h -> p c h", p=128))
    P.dma("gpsimd", s_ldp, wqr_sb[:, :, :], a["wqr"].rearrange("(c p) h -> p c h", p=128))
    v_ldp = s_ldp.n
    P.waits(("tensor",), [(s_ldp, v_ldp), (s_tb, v_tb)])

    for h in range(8):
        for mt in range(4):
            def wt_tile(h=h, mt=mt):
                bk = pA if mt % 2 == 0 else pB
                hb = (h % 2) * 64
                bk.mm_group([(bk.ap[:, 0:256],
                              wuqT_sb[hb:hb + 64, h // 2, ts(mt, 128)],
                              wukT_sb[hb:hb + 64, h // 2, :])])
                bk.consume("vector", lambda e, p, _h=h, _m=mt: e.tensor_scalar_mul(
                    wth[:, _h * 4 + _m, :], p[:, 0:256], SCL))
            wt_tile(h, mt)

    st_ck = DStream(P, "stck", "gpsimd", 2)
    st_qi = DStream(P, "stqi", "gpsimd", 2)
    st_cq = DStream(P, "stcq", "gpsimd", 2)
    cq_sb = wsc_cq = None
    ck_tok = None
    qi_tok = [None, None]
    cq_tok = [None, None]
    for s in range(NSTRIP):
        def pstrip(s=s):
            nonlocal ck_tok
            pA.mm_group([(pA.ap[:, 0:288], xlnT[:, cc, ts(s, 128)], wdkv_sb[:, cc, :])
                         for cc in range(8)])
            etok = pA.consume("vector", lambda e, p: e.tensor_copy(lnsq[:, :], p[:, 0:288]))
            P.waits(("scalar",), [etok])
            sp_tok = _softplus(P, lnsq[:, 256:288], lnsq[:, 256:288], sp1t[:, :])
            if ck_tok is not None:
                P.waits(("vector",), [ck_tok])
            c1 = P.opt("vector", lambda e, _s=s: e.tensor_copy(cknat[:, _s, :], lnsq[:, :]))
            P.waits(("gpsimd",), [c1])
            ck_tok = st_ck.issue(a["ckd"][ts(s, 128), :], cknat[:, s, :])
            P.waits(("tensor",), [c1])
            rtok = _rot(P, vband[:, 0:16], vband[:, 16:32],
                        lnsq[:, 256:272], lnsq[:, 272:288],
                        tabs["cosS"][:, s, :], tabs["sinS"][:, s, :],
                        sp1t[:, 0:16], sp1t[:, 16:32])
            P.waits(("tensor",), [rtok])
            pT0.tp_into(vband[:, 0:32], ident[:, :])
            pT0.consume("vector", lambda e, p, _s=s: e.tensor_copy(
                vT[:, ts(_s, 128)], p[0:32, 0:128]))
            for c2 in range(2):
                pt = pT1 if c2 == 0 else pT0
                pt.tp_into(cknat[:, s, ts(c2, 128)], ident[:, :])
                pt.consume("vector", lambda e, p, _c=c2, _s=s: e.tensor_copy(
                    ckvT[:, _c, ts(_s, 128)], p[:, 0:128]))
            for mt in range(4):
                bk = pB if mt % 2 == 0 else pC
                bk.mm_group([(bk.ap[:, 0:128], wdq_sb[:, cc, ts(mt, 128)],
                              xlnT[:, cc, ts(s, 128)]) for cc in range(8)])
                if mt == 0 and cq_tok[s % 2] is not None:
                    P.waits(("vector",), [cq_tok[s % 2]])
                bk.consume("vector", lambda e, p, _m=mt, _s=s: e.tensor_copy(
                    cqb2[_s % 2][:, _m, :], p[:, 0:128]))
            vcq = P.opt("vector", lambda e: e.nop(), drain=False)
            P.waits(("gpsimd",), [vcq])
            cq_tok[s % 2] = st_cq.issue(a["cqd"][s], cqb2[s % 2][:, :, :])
            pD.mm_group([(pD.ap[0:64, 0:128], widxk_sb[:, cc, :],
                          xlnT[:, cc, ts(s, 128)]) for cc in range(8)])
            pD.consume("vector", lambda e, p, _s=s: e.tensor_copy(
                kiT[0:64, ts(_s, 128)], p[0:64, 0:128]))
            pD.consume("vector", lambda e, p, _s=s: e.tensor_copy(
                kiT[64:128, ts(_s, 128)], p[0:64, 0:128]))
            for mt in range(2):
                bk = pB if mt == 0 else pC
                bk.mm_group([(bk.ap[:, 0:128], widxq_sb[:, cc, ts(mt, 128)],
                              xlnT[:, cc, ts(s, 128)]) for cc in range(8)])
                if mt == 0 and qi_tok[s % 2] is not None:
                    P.waits(("vector",), [qi_tok[s % 2]])
                bk.consume("vector", lambda e, p, _m=mt, _s=s: e.tensor_copy(
                    qi_sb2[_s % 2][:, _m, :], p[:, 0:128]))
            vq = P.opt("vector", lambda e: e.nop(), drain=False)
            P.waits(("gpsimd",), [vq])
            qi_tok[s % 2] = st_qi.issue(a["qiT_d"][s], qi_sb2[s % 2][:, :, :])
        pstrip(s)
    proj_toks = [t for t in [ck_tok, qi_tok[0], qi_tok[1], cq_tok[0], cq_tok[1]]
                 if t is not None]
    P.waits(("sync", "gpsimd"), proj_toks)
    wsc.close()
    ln1.close()

    at = ExitStack()
    sbA_ = lambda n, s, d: at.enter_context(nc.sbuf_tensor(n, s, d))
    ksel = sbA_("ksel", [128, 32, 288], BF16)
    rotb = sbA_("rotb", [128, 32, 16], F32)
    rotb2 = sbA_("rotb2", [128, 32, 16], F32)
    Ibuf = sbA_("Ibuf", [128, T], F32)
    relu_t = sbA_("relu_t", [128, 512], F32)
    mx8 = sbA_("mx8", [128, 8], F32)
    tidx = sbA_("tidx", [128, 96], U32)
    tidxi = sbA_("tidxi", [128, 96], I32)
    qi_l = sbA_("qi_l", [128, 2, 128], BF16)
    cq_l = sbA_("cq_l", [128, 4, 128], BF16)
    qr_raw = sbA_("qr_raw", [128, 256], F32)
    sptmp = sbA_("sptmp", [128, 256], F32)
    usel = sbA_("usel", [128, 8, 32], BF16)
    uband = sbA_("uband", [128, 8, 32], BF16)
    qhat = sbA_("qhat", [128, 256], BF16)
    qext = sbA_("qext", [128, 288], BF16)
    qhatT = sbA_("qhatT", [128, 2, 128], BF16)
    ubT = sbA_("ubT", [32, 128], BF16)
    sbband = sbA_("sbband", [128, 256], F32)
    selS = sbA_("selS", [128, 32], F32)
    ttrj = sbA_("ttrj", [128, 288], BF16)
    ttp = sbA_("ttp", [128, 32, 288], BF16)
    ttq = sbA_("ttq", [128, 32, 256], BF16)
    negm = sbA_("negm", [128, 1], F32)
    m2t = sbA_("m2t", [128, 1], F32)
    l1t = sbA_("l1t", [128, 1], F32)
    l2t = sbA_("l2t", [128, 1], F32)
    rlt = sbA_("rlt", [128, 1], F32)
    Pb = sbA_("Pb", [128, 256], BF16)
    Ps = sbA_("Ps", [128, 32], F32)
    ptb = [sbA_(f"ptb{i}", [128, 128], BF16) for i in range(2)]
    asel = sbA_("asel", [128, 256], F32)
    atot = sbA_("atot", [128, 256], F32)
    atbf = sbA_("atbf", [128, 256], BF16)
    osb = sbA_("osb", [128, 64], BF16)
    otsb = [sbA_(f"otsb{i}", [64, 128], BF16) for i in range(2)]
    s96 = sbA_("s96", [128, 8, 96], F32)
    P96 = sbA_("P96", [128, 8, 96], F32)
    a8 = sbA_("a8", [128, 8, 256], F32)
    sent = sbA_("sent", [128, 192], F32)

    gsel = DStream(P, "gsel", "gpsimd", 4)
    ld_qi = DStream(P, "ldqi", "sync", 2)
    st_ot = DStream(P, "stot", "gpsimd", 2)
    ot_ctr = [0]
    ot_tok = [None, None]
    ksel_guard = [None]
    ksel_guard2 = [None]

    ri_tok = [None]

    def emit_I(s, nK):
        for K in range(nK):
            for hh in range(4):
                def ih(K=K, hh=hh):
                    bk = pC if hh % 2 == 0 else pD
                    hb = (hh % 2) * 64
                    bk.mm_group([(None, qi_l[hb:hb + 64, hh // 2, :],
                                  kiT[hb:hb + 64, ts(K, 512)])])
                    if ri_tok[0] is not None:
                        P.waits(("scalar",), [ri_tok[0]])
                    stok = bk.consume("scalar", lambda e, p: e.activation(
                        relu_t[:, :], p[:, :], AF.Relu))
                    P.waits(("vector",), [stok])
                    if hh == 0:
                        ri_tok[0] = P.opt("vector", lambda e, _K=K: e.tensor_scalar_mul(
                            Ibuf[:, ts(_K, 512)], relu_t[:, :], wrep4[:, 0:1]))
                    else:
                        ri_tok[0] = P.opt("vector", lambda e, _K=K, _h=hh: e.scalar_tensor_tensor(
                            Ibuf[:, ts(_K, 512)], relu_t[:, :], wrep4[:, _h:_h + 1],
                            Ibuf[:, ts(_K, 512)], ALU.mult, ALU.add))
                ih(K, hh)
        m = s % 4
        Kd = s // 4
        P.op("vector", lambda e, _m=m, _K=Kd: e.tensor_add(
            Ibuf[:, ts(_K, 512)], Ibuf[:, ts(_K, 512)], imaskd[:, _m, :]))
        if m == 0 and Kd >= 1:
            P.op("vector", lambda e, _K=Kd - 1: e.tensor_add(
                Ibuf[:, ts(_K, 512)], Ibuf[:, ts(_K, 512)], imaskp[:, :]))

    def emit_topk(region, rounds, out_off):
        for r in range(rounds):
            P.op("vector", lambda e, _r=r: e.max(out=mx8[:, :], in_=region))
            P.op("vector", lambda e, _r=r: e.max_index(
                tidx[:, out_off + 8 * _r: out_off + 8 * _r + 8], mx8[:, :], region))
            P.op("vector", lambda e: e.match_replace(
                out=region, in_to_replace=mx8[:, :], in_values=region, imm_value=-2e38))

    def emit_qside(s, with_band):
        pD.mm_group([(pD.ap[:, 0:256], cq_l[:, cc, :], wqr_sb[:, cc, :])
                     for cc in range(4)])
        etok = pD.consume("vector", lambda e, p: e.tensor_copy(qr_raw[:, :], p[:, 0:256]))
        P.waits(("scalar",), [etok])
        _softplus(P, qr_raw[:, :], qr_raw[:, :], sptmp[:, :])
        wqv = qr_raw[:, :].rearrange("p (h c) -> p h c", h=8)
        re = wqv[:, :, 0:16]
        im = wqv[:, :, 16:32]
        cT = tabs["cosT"][:, s, :].unsqueeze(1).to_broadcast([128, 8, 16])
        sT = tabs["sinT"][:, s, :].unsqueeze(1).to_broadcast([128, 8, 16])
        _rot(P, usel[:, :, 0:16], usel[:, :, 16:32], re, im, cT, sT,
             rotb[:, 0:8, :], rotb2[:, 0:8, :])
        if with_band:
            cB = tabs["cosB"][:, s, :].unsqueeze(1).to_broadcast([128, 8, 16])
            sB = tabs["sinB"][:, s, :].unsqueeze(1).to_broadcast([128, 8, 16])
            return _rot(P, uband[:, :, 0:16], uband[:, :, 16:32], re, im, cB, sB,
                        rotb[:, 0:8, :], rotb2[:, 0:8, :])
        return P.opt("vector", lambda e: e.nop(), drain=False)

    def emit_gather_rotate(idx_col_base, nslots, rank_block):
        toks = []
        for j in range(nslots):
            toks.append(gsel.issue_gather(
                ksel[:, j, :], a["ckd"], tidxi[:, idx_col_base + j:idx_col_base + j + 1],
                None))
        P.waits(("vector",), toks)
        kv = ksel[:, :, 256:288]
        re = kv[:, :, 0:16]
        im = kv[:, :, 16:32]
        cR = tabRc[:, rank_block, :, :]
        sR = tabRs[:, rank_block, :, :]
        P.op("vector", lambda e: e.tensor_tensor(rotb[:, :, :], re, cR, ALU.mult))
        P.op("vector", lambda e: e.tensor_tensor(rotb2[:, :, :], im, sR, ALU.mult))
        P.op("vector", lambda e: e.tensor_sub(rotb[:, :, :], rotb[:, :, :], rotb2[:, :, :]))
        P.op("vector", lambda e: e.tensor_tensor(rotb2[:, :, :], re, sR, ALU.mult))
        P.op("vector", lambda e: e.tensor_tensor(sptmp_g[:, :, :], im, cR, ALU.mult))
        P.op("vector", lambda e: e.tensor_add(im, rotb2[:, :, :], sptmp_g[:, :, :]))
        return P.opt("vector", lambda e: e.tensor_copy(re, rotb[:, :, :]))

    sptmp_g = sbA_("sptmp_g", [128, 32, 16], F32)

    hs_tok = [None]

    def emit_headsel_scores(h, out_ap, nslots):
        qb = qext[:, :].unsqueeze(1).to_broadcast([128, nslots, 288])
        if hs_tok[0] is not None:
            P.waits(("vector",), [hs_tok[0]])
        ttok = P.opt("vector", lambda e, _n=nslots, _q=qb: e.tensor_tensor(
            ttp[:, 0:_n, :], ksel[:, 0:_n, :], _q, ALU.mult))
        P.waits(("scalar",), [ttok])
        for j in range(nslots - 1):
            P.op("scalar", lambda e, _j=j, _o=out_ap: e.activation(
                ttp[:, _j, :], ttp[:, _j, :], AF.Identity,
                accum_out=_o[:, _j:_j + 1]), drain=False)
        tok = P.opt("scalar", lambda e, _j=nslots - 1, _o=out_ap: e.activation(
            ttp[:, _j, :], ttp[:, _j, :], AF.Identity,
            accum_out=_o[:, _j:_j + 1]))
        hs_tok[0] = tok
        return tok

    def emit_qext(h, utab):
        P.op("vector", lambda e: e.tensor_copy(qext[:, 0:256], qhat[:, :]))
        return P.opt("vector", lambda e, _h=h, _u=utab: e.tensor_copy(
            qext[:, 256:288], _u[:, _h, :]))

    def emit_qhat(s, h):
        bk = pA if h % 2 == 0 else pB
        bk.mm_group([(bk.ap[:, 0:256], cq_l[:, cc, :], wth[:, h * 4 + cc, :])
                     for cc in range(4)])
        return bk.consume("vector", lambda e, p: e.tensor_copy(qhat[:, :], p[:, 0:256]))

    def emit_opath(h, s_, src_f32):
        ctok = P.opt("vector", lambda e: e.tensor_copy(atbf[:, :], src_f32))
        P.waits(("tensor",), [ctok])
        for c2 in range(2):
            pt = pT0 if c2 == 0 else pT1
            pt.tp_into(atbf[:, ts(c2, 128)], ident[:, :])
            pt.consume("vector", lambda e, p, _c=c2: e.tensor_copy(
                qhatT[:, _c, :], p[:, 0:128]))
        P.waits(("tensor",), [(pT0.free, pT0.freed_target), (pT1.free, pT1.freed_target)])
        pO0.mm_group([(None, qhatT[:, c2, :], wuv_sb[:, c2, 64 * h:64 * h + 64])
                      for c2 in range(2)])
        otok = pO0.consume("scalar", lambda e, p: e.copy(osb[:, :], p[:, 0:64]))
        P.waits(("tensor",), [otok])
        pT0.tp_into(osb[:, 0:64], ident[:, :])
        bi = ot_ctr[0] % 2
        ot_ctr[0] += 1
        if ot_tok[bi] is not None:
            P.waits(("vector",), [ot_tok[bi]])
        ev = pT0.consume("vector", lambda e, p, _b=bi: e.tensor_copy(
            otsb[_b][:, :], p[0:64, 0:128]))
        P.waits(("gpsimd",), [ev])
        ot_tok[bi] = st_ot.issue(a["otd"][h // 2, 64 * (h % 2):64 * (h % 2) + 64, ts(s_, 128)],
                                 otsb[bi][:, :])

    prev_e1 = [None]
    prev_e2 = [None]
    prev_atok = [None]
    qi_guard = [None]
    for s in range(1, NSTRIP):
        def strip(s=s):
            nK = s // 4 + 1
            if qi_guard[0] is not None:
                P.waits(("sync",), [qi_guard[0]])
            tok = ld_qi.issue(qi_l[:, :, :], a["qiT_d"][s])
            tokc = ld_qi.issue(cq_l[:, :, :], a["cqd"][s])
            P.waits(("tensor",), [tok, tokc])
            emit_I(s, nK)
            emit_topk(Ibuf[:, 0:nK * 512], 4, 0)
            cvt = P.opt("vector", lambda e: e.tensor_copy(tidxi[:, 0:32], tidx[:, 0:32]))
            if ksel_guard[0] is not None:
                P.waits(("gpsimd",), [ksel_guard[0], ksel_guard2[0]])
            P.waits(("gpsimd",), [cvt])
            emit_gather_rotate(0, 32, 2)
            qtok = emit_qside(s, with_band=True)
            for h in range(8):
                def headp(h=h):
                    htok = emit_qhat(s, h)
                    P.waits(("vector",), [htok])
                    qx = emit_qext(h, usel)
                    ct2 = P.opt("vector", lambda e: e.tensor_copy(atbf[:, :], qhat[:, :]))
                    P.waits(("tensor",), [ct2])
                    for c2 in range(2):
                        pt = pT0 if c2 == 0 else pT1
                        pt.tp_into(atbf[:, ts(c2, 128)], ident[:, :])
                        pt.consume("vector", lambda e, p, _c=c2: e.tensor_copy(
                            qhatT[:, _c, :], p[:, 0:128]))
                    P.waits(("tensor",), [(pT0.free, pT0.freed_target),
                                          (pT1.free, pT1.freed_target)])
                    ub = P.opt("vector", lambda e, _h=h: e.tensor_copy(
                        ttrj[:, 0:32], uband[:, _h, :]))
                    P.waits(("tensor",), [ub])
                    pT0.tp_into(ttrj[:, 0:32], ident[:, :])
                    ubtok = pT0.consume("vector", lambda e, p: e.tensor_copy(
                        ubT[:, :], p[0:32, 0:128]))
                    P.waits(("tensor",), [ubtok])
                    win0 = (s - 1) * 128
                    pC.mm_group([(pC.ap[:, 0:256], qhatT[:, 0, :],
                                  ckvT[:, 0, win0:win0 + 256]),
                                 (pC.ap[:, 0:256], qhatT[:, 1, :],
                                  ckvT[:, 1, win0:win0 + 256]),
                                 (pC.ap[:, 0:256], ubT[:, :], vT[:, win0:win0 + 256])])
                    if prev_e1[0] is not None:
                        P.waits(("vector",), [prev_e1[0], prev_e2[0]])
                    btok = pC.consume("vector", lambda e, p: e.tensor_add(
                        sbband[:, :], p[:, 0:256], bandm[:, :]))
                    sstok = emit_headsel_scores(h, selS[:, :], 32)
                    P.op("vector", lambda e: e.reduce_max(negm[:, :], sbband[:, :], axis=AXX))
                    P.waits(("vector",), [sstok])
                    P.op("vector", lambda e: e.reduce_max(m2t[:, :], selS[:, :], axis=AXX))
                    P.op("vector", lambda e: e.tensor_max(negm[:, :], negm[:, :], m2t[:, :]))
                    mtok = P.opt("vector", lambda e: e.tensor_scalar_mul(
                        negm[:, :], negm[:, :], -1.0))
                    P.waits(("scalar",), [mtok])
                    e1 = P.opt("scalar", lambda e: e.activation(
                        Pb[:, :], sbband[:, :], AF.Exp, bias=negm[:, :],
                        accum_out=l1t[:, :]))
                    e2 = P.opt("scalar", lambda e: e.activation(
                        Ps[:, :], selS[:, :], AF.Exp, bias=negm[:, :],
                        accum_out=l2t[:, :]))
                    prev_e1[0], prev_e2[0] = e1, e2
                    P.waits(("vector",), [e1, e2])
                    P.op("vector", lambda e: e.tensor_add(l1t[:, :], l1t[:, :], l2t[:, :]))
                    P.op("vector", lambda e: e.reciprocal(rlt[:, :], l1t[:, :]))
                    P.op("vector", lambda e: e.tensor_scalar_mul(Pb[:, :], Pb[:, :], rlt[:, :]))
                    pstok = P.opt("vector", lambda e: e.tensor_scalar_mul(
                        Ps[:, :], Ps[:, :], rlt[:, :]))
                    P.waits(("tensor",), [pstok])
                    for c2 in range(2):
                        pt = pT0 if c2 == 0 else pT1
                        pt.tp_into(Pb[:, ts(c2, 128)], ident[:, :])
                        pt.consume("vector", lambda e, p, _c=c2: e.tensor_copy(
                            ptb[_c][:, :], p[:, 0:128]))
                    P.waits(("tensor",), [(pT0.free, pT0.freed_target),
                                          (pT1.free, pT1.freed_target)])
                    pD.mm_group([(pD.ap[:, 0:256], ptb[0][:, :], cknat[:, s - 1, 0:256]),
                                 (pD.ap[:, 0:256], ptb[1][:, :], cknat[:, s, 0:256])])
                    psb = Ps[:, :].unsqueeze(2).to_broadcast([128, 32, 256])
                    P.op("vector", lambda e, _p=psb: e.tensor_tensor(
                        ttq[:, :, :], ksel[:, :, 0:256], _p, ALU.mult))
                    tfl = ttq[:, :, :].rearrange("p j c -> p (j c)")
                    for w in (4096, 2048, 1024, 512):
                        P.op("vector", lambda e, _w=w, _f=tfl: e.tensor_add(
                            _f[:, 0:_w], _f[:, 0:_w], _f[:, _w:2 * _w]))
                    P.op("vector", lambda e, _f=tfl: e.tensor_add(
                        asel[:, :], _f[:, 0:256], _f[:, 256:512]))
                    atok = pD.consume("vector", lambda e, p: e.tensor_add(
                        atot[:, :], p[:, 0:256], asel[:, :]))
                    P.waits(("vector",), [atok])
                    emit_opath(h, s, atot[:, :])
                headp(h)
            ksel_guard[0] = P.opt("vector", lambda e: e.nop(), drain=False)
            ksel_guard2[0] = ksel_guard[0]
            qi_guard[0] = (pB.full, pB.fills)
        strip(s)

    def strip0():
        s96_tok = [None]
        if qi_guard[0] is not None:
            P.waits(("sync",), [qi_guard[0]])
        tok = ld_qi.issue(qi_l[:, :, :], a["qiT_d"][0])
        tokc = ld_qi.issue(cq_l[:, :, :], a["cqd"][0])
        P.waits(("tensor",), [tok, tokc])
        for hh in range(4):
            def ih(hh=hh):
                bk = pC if hh % 2 == 0 else pD
                hb = (hh % 2) * 64
                bk.mm_group([(None, qi_l[hb:hb + 64, hh // 2, :], kiT[hb:hb + 64, 0:512])])
                if ri_tok[0] is not None:
                    P.waits(("scalar",), [ri_tok[0]])
                stok = bk.consume("scalar", lambda e, p: e.activation(
                    relu_t[:, :], p[:, :], AF.Relu))
                P.waits(("vector",), [stok])
                if hh == 0:
                    ri_tok[0] = P.opt("vector", lambda e: e.tensor_scalar_mul(
                        Ibuf[:, 0:512], relu_t[:, :], wrep4[:, 0:1]))
                else:
                    ri_tok[0] = P.opt("vector", lambda e, _h=hh: e.scalar_tensor_tensor(
                        Ibuf[:, 0:512], relu_t[:, :], wrep4[:, _h:_h + 1],
                        Ibuf[:, 0:512], ALU.mult, ALU.add))
            ih(hh)
        P.op("vector", lambda e: e.tensor_add(sent[:, :], Ibuf[:, 0:192],
                                              s0add[:, :]))
        emit_topk(sent[:, :], 12, 0)
        cvt = P.opt("vector", lambda e: e.tensor_copy(tidxi[:, 0:96], tidx[:, 0:96]))
        if ksel_guard[0] is not None:
            P.waits(("gpsimd",), [ksel_guard[0], ksel_guard2[0]])
        P.waits(("gpsimd",), [cvt])
        qtok = emit_qside(0, with_band=False)
        for b in range(3):
            def chunkb(b=b):
                if b > 0 and ksel_guard[0] is not None:
                    P.waits(("gpsimd",), [ksel_guard[0]])
                emit_gather_rotate(32 * b, 32, b)
                for h in range(8):
                    def hsc(h=h, b=b):
                        htok = emit_qhat(0, h)
                        P.waits(("vector",), [htok, qtok])
                        qx = emit_qext(h, usel)
                        s96_tok[0] = emit_headsel_scores(
                            h, s96[:, h, 32 * b:32 * b + 32], 32)
                    hsc(h, b)
                gd = P.opt("vector", lambda e: e.nop(), drain=False)
                ksel_guard[0] = gd
                ksel_guard2[0] = gd
            chunkb(b)
        P.waits(("vector",), [s96_tok[0]])
        p96_tok = [None]
        for h in range(8):
            def hsm(h=h):
                P.op("vector", lambda e, _h=h: e.reduce_max(negm[:, :], s96[:, _h, :],
                                                            axis=AXX))
                mtok = P.opt("vector", lambda e: e.tensor_scalar_mul(
                    negm[:, :], negm[:, :], -1.0))
                P.waits(("scalar",), [mtok])
                e1 = P.opt("scalar", lambda e, _h=h: e.activation(
                    P96[:, _h, :], s96[:, _h, :], AF.Exp, bias=negm[:, :],
                    accum_out=l1t[:, :]))
                P.waits(("vector",), [e1])
                P.op("vector", lambda e: e.reciprocal(rlt[:, :], l1t[:, :]))
                p96_tok[0] = P.opt("vector", lambda e, _h=h: e.tensor_scalar_mul(
                    P96[:, _h, :], P96[:, _h, :], rlt[:, :]))
            hsm(h)
        P.waits(("gpsimd",), [p96_tok[0]])
        first = [True] * 8
        for b in range(3):
            def avb(b=b):
                P.waits(("gpsimd",), [ksel_guard[0], ksel_guard2[0]])
                toks = []
                for j in range(32):
                    toks.append(gsel.issue_gather(
                        ksel[:, j, :], a["ckd"], tidxi[:, 32 * b + j:32 * b + j + 1],
                        None))
                P.waits(("vector", "gpsimd"), toks)
                for h in range(8):
                    def hav(h=h, b=b):
                        pvb = P96[:, h, 32 * b:32 * b + 32].unsqueeze(2).to_broadcast(
                            [128, 32, 256])
                        P.op("vector", lambda e, _p=pvb: e.tensor_tensor(
                            ttq[:, :, :], ksel[:, :, 0:256], _p, ALU.mult))
                        tfl = ttq[:, :, :].rearrange("p j c -> p (j c)")
                        for w in (4096, 2048, 1024, 512):
                            P.op("vector", lambda e, _w=w, _f=tfl: e.tensor_add(
                                _f[:, 0:_w], _f[:, 0:_w], _f[:, _w:2 * _w]))
                        if first[h]:
                            P.op("vector", lambda e, _h=h, _f=tfl: e.tensor_add(
                                a8[:, _h, :], _f[:, 0:256], _f[:, 256:512]))
                            first[h] = False
                        else:
                            P.op("vector", lambda e, _f=tfl: e.tensor_add(
                                asel[:, :], _f[:, 0:256], _f[:, 256:512]))
                            P.op("vector", lambda e, _h=h: e.tensor_add(
                                a8[:, _h, :], a8[:, _h, :], asel[:, :]))
                    hav(h, b)
                g1 = P.opt("vector", lambda e: e.nop(), drain=False)
                ksel_guard[0] = g1
                ksel_guard2[0] = g1
            avb(b)
        P.waits(("vector",), [ksel_guard2[0]])
        for h in range(8):
            emit_opath(h, 0, a8[:, h, :])
    strip0()

    P.waits(("sync",), st_ot.tokens())
    at.close()
    op_sc = ExitStack()
    sbO = lambda n, s, d: op_sc.enter_context(nc.sbuf_tensor(n, s, d))
    woutsb = sbO("woutsb", [128, 4, DM], BF16)
    otl = [sbO(f"otl{i}", [128, 4, 128], BF16) for i in range(2)]
    xs2 = [sbO(f"xs2_{i}", [128, DM], F32) for i in range(2)]
    ys2 = [sbO(f"ys2_{i}", [128, DM], F32) for i in range(2)]
    s_cstg = P.S("cstg", sw=True)
    P.dma("gpsimd", s_cstg, woutsb[:, :, :], a["wout"].rearrange("(c p) h -> p c h", p=128))
    P.waits(("tensor",), [(s_cstg, s_cstg.n)])
    ld_x2 = DStream(P, "ldx2", "sync", 2)
    ld_ot = DStream(P, "ldot", "sync", 2)
    st_y2 = DStream(P, "sty2", "gpsimd", 2)
    y_tok2 = [None, None]
    x_free2 = [None, None]
    ot_free2 = [None, None]
    for s in range(NSTRIP):
        def ostrip(s=s):
            xb, yb, ob_ = xs2[s % 2], ys2[s % 2], otl[s % 2]
            if x_free2[s % 2] is not None:
                P.waits(("sync",), [x_free2[s % 2]])
            tok = ld_x2.issue(xb[:, :], a["x"][ts(s, 128), :])
            if ot_free2[s % 2] is not None:
                P.waits(("sync",), [ot_free2[s % 2]])
            tok2 = ld_ot.issue(ob_[:, :, :],
                               a["otd"][:, :, ts(s, 128)].rearrange("c p t -> p c t"))
            P.waits(("tensor",), [tok2])
            if y_tok2[s % 2] is not None:
                P.waits(("vector",), [y_tok2[s % 2]])
            last_v = None
            for dh in range(2):
                bk = pA if dh == 0 else pB
                bk.mm_group([(None, ob_[:, cc, :], woutsb[:, cc, ts(dh, 512)])
                             for cc in range(4)])
                if dh == 0:
                    P.waits(("vector",), [tok])
                last_v = bk.consume("vector", lambda e, p, _d=dh, _y=yb, _x=xb:
                                    e.scalar_tensor_tensor(
                                        _y[:, ts(_d, 512)], p[:, :], a["scale"],
                                        _x[:, ts(_d, 512)], ALU.mult, ALU.add))
            x_free2[s % 2] = last_v
            ot_free2[s % 2] = (pB.full, pB.fills)
            P.waits(("gpsimd",), [last_v])
            y_tok2[s % 2] = st_y2.issue(a["y"][ts(s, 128), :], yb[:, :])
        ostrip(s)
    P.waits(("gpsimd",), st_y2.tokens())
    op_sc.close()


class NCProxy:
    def __init__(self, nc, tag):
        self._nc, self._tag = nc, tag

    def sbuf_tensor(self, name, shape, dtype, **kw):
        return self._nc.sbuf_tensor(self._tag + name, shape, dtype, **kw)

    def dram_tensor(self, name, shape, dtype, **kw):
        return self._nc.dram_tensor(self._tag + name, shape, dtype, **kw)

    def __getattr__(self, k):
        return getattr(self._nc, k)


def build_fused(C, stages=("f0a", "a0", "f0b", "f1a", "a1", "f1b")):
    nc = bass.Bass("TRN2", target_bir_lowering=False)
    dr = lambda n, s, d, k="ExternalInput": nc.dram_tensor(n, s, d, kind=k).ap()
    ext = dict(
        x=dr("x", [T, DM], F32),
        y=dr("y", [T, DM], F32, "ExternalOutput"),
        onesf=dr("onesf", [1, 128], F32), identbf=dr("identbf", [128, 128], BF16),
        iota8=dr("iota8", [128, 8], F32),
        onesbf=dr("onesbf", [1, 128], BF16), identf=dr("identf", [128, 128], F32),
        sw1T=dr("sw1T", [DM, 512], F32), sw2T=dr("sw2T", [DM, 512], F32),
        sw3T=dr("sw3T", [512, DM], F32),
        sb1=dr("sb1", [1, 512], F32), sb2=dr("sb2", [1, 512], F32), sb3=dr("sb3", [1, DM], F32),
        slng=dr("slng", [1, 512], F32), slnb=dr("slnb", [128, 4], F32),
        sgeg=dr("sgeg", [DM, 16], F32), sgegb=dr("sgegb", [1, 16], F32),
        expb=dr("expb", [1, 8], F32),
        ew1T=dr("ew1T", [8, DM, 512], F32), ew2T=dr("ew2T", [8, DM, 512], F32),
        ew3T=dr("ew3T", [8, 512, DM], F32),
        eb1=dr("eb1", [8, 1, 512], F32), eb2=dr("eb2", [8, 1, 512], F32),
        eb3=dr("eb3", [8, 1, DM], F32),
        selg=dr("selg", [1, 512], F32), selb=dr("selb", [128, 4], F32),
        midx=dr("midx", [128, C // 128], I32),
        mmask=dr("mmask", [128, C // 128], F32),
        wqkvh=dr("wqkvh", [8, DM, 320], F32), wvc=dr("wvc", [8, DM, 64], F32),
        woutT=dr("woutT", [512, DM], F32),
        t1t_mask=dr("t1t_mask", [128, 2, 256], BF16),
        lat_mask=dr("lat_mask", [128, 4, 512], BF16),
        causal_mask=dr("causal_mask", [128, 4, 512], BF16),
        wr=dr("wr", [DM, 8], F32), wg=dr("wg", [8, DM, 2048], F32),
        wu=dr("wu", [8, DM, 2048], F32), wd=dr("wd", [8, 2048, DM], F32),
        trilow=dr("trilow", [128, 128], BF16),
        pidx=dr("pidx", [128, 1], F32),
        wdkvkr=dr("wdkvkr", [DM, 288], F32), wdq=dr("wdq", [DM, 512], F32),
        wqr=dr("wqr", [512, 256], F32),
        widxq=dr("widxq", [DM, 256], F32), widxk=dr("widxk", [DM, 64], F32),
        wuqT=dr("wuqT", [512, 512], F32), wukT=dr("wukT", [512, 256], F32),
        wuv=dr("wuv", [256, 512], F32), wout=dr("wout", [512, DM], F32),
        idxw=dr("idxw", [1, 4], F32), rawdelta=dr("rawdelta", [1, 16], F32),
        cosT=dr("cosT", [128, 32, 16], F32), sinT=dr("sinT", [128, 32, 16], F32),
        cosB=dr("cosB", [128, 32, 16], F32), sinB=dr("sinB", [128, 32, 16], F32),
        cosS=dr("cosS", [128, 32, 16], F32), sinS=dr("sinS", [128, 32, 16], F32),
        cosR=dr("cosR", [128, 3, 32, 16], F32), sinR=dr("sinR", [128, 3, 32, 16], F32),
        bandmask=dr("bandmask", [128, 256], BF16),
        imaskd=dr("imaskd", [128, 4, 512], BF16),
        imaskp=dr("imaskp", [128, 512], BF16),
        s0add=dr("s0add", [128, 192], F32),
    )
    scr = dict(
        f=dict(xnat_d=dr("s_xnat", [T, DM], BF16, "Internal"),
               shout_d=dr("s_shout", [T, DM], BF16, "Internal")),
        g=dict(xnat_d=dr("s_xnat1", [T, DM], BF16, "Internal"),
               tix_d=dr("s_tix", [8 * CE, 2], F32, "Internal"),
               out_d=dr("s_outd", [8 * CE, DM], F32, "Internal"),
               ptd=dr("s_ptd", [T, 4], F32, "Internal")),
        a0=dict(lat_d=dr("s_lat", [len(LAT_SLOTS) * 128, 512], BF16, "Internal"),
                otd=dr("s_otd0", [4, 128, T], BF16, "Internal")),
        a1=dict(qiT_d=dr("s_qiT", [NSTRIP, 128, 2, 128], BF16, "Internal"),
                cqd=dr("s_cqd", [NSTRIP, 128, 4, 128], BF16, "Internal"),
                ckd=dr("s_ckd", [T, 288], BF16, "Internal"),
                otd=dr("s_otd1", [4, 128, T], BF16, "Internal")),
    )
    n = len(stages)
    acts = [ext["x"]] + [dr(f"act{i}", [T, DM], F32, "Internal")
                         for i in range(1, n)] + [ext["y"]]
    octx = ExitStack()
    P = Prog(nc, octx)
    P.barsc = octx.enter_context(nc.sbuf_tensor("barsc", [1, 8], F32))
    cs = P._pool("_binit")
    P.op("vector", lambda e: e.memset(P.barsc[:, :], 0.0), sem=cs, drain=False)
    P.waits(("scalar",), [(cs, cs.n)])
    for i, tag in enumerate(stages):
        P.new_stage()
        P.stage_label = tag
        sctx = ExitStack()
        ncp = NCProxy(nc, tag + "_")
        a = dict(ext)
        a["x"], a["y"] = acts[i], acts[i + 1]
        if tag.startswith("f0"):
            a.update(scr["f"]); a.update(C=C, scale=0.5)
            emit_l0ffn(ncp, sctx, P, a)
        elif tag == "a0":
            a.update(scr["a0"]); a["scale"] = 1.0
            emit_l0attn(ncp, sctx, P, a)
        elif tag.startswith("f1"):
            a.update(scr["g"]); a["scale"] = 0.5
            emit_l1ffn_sparse(ncp, sctx, P, a)
        else:
            a.update(scr["a1"]); a["scale"] = 1.0
            emit_l1attn(ncp, sctx, P, a)
        sctx.close()
        if i < n - 1:
            P.barrier()
    P.build()
    octx.close()
    return nc


def _f32(x):
    return np.ascontiguousarray(x, dtype=np.float32)


def l0ffn_host_inputs(ww):
    sgeg = np.zeros((DM, 16), np.float32)
    sgeg[:, 0] = _f32(ww["L0_ffn_sg_w"])[0]
    sgeg[:, 1:9] = _f32(ww["L0_ffn_eg_w"]).T
    sgegb = np.zeros((1, 16), np.float32)
    sgegb[0, 0] = _f32(ww["L0_ffn_sg_b"])[0]
    sgegb[0, 1:9] = _f32(ww["L0_ffn_eg_b"])
    return dict(
        sw1T=_f32(np.asarray(ww["L0_ffn_sw1"]).T), sw2T=_f32(np.asarray(ww["L0_ffn_sw2"]).T),
        sw3T=_f32(np.asarray(ww["L0_ffn_sw3"]).T),
        sb1=_f32(ww["L0_ffn_sb1"]).reshape(1, 512), sb2=_f32(ww["L0_ffn_sb2"]).reshape(1, 512),
        sb3=_f32(ww["L0_ffn_sb3"]).reshape(1, DM),
        slng=_f32(ww["L0_ffn_s_ln_g"]).reshape(1, 512),
        slnb=_f32(np.asarray(ww["L0_ffn_s_ln_b"]).reshape(4, 128).T),
        sgeg=sgeg, sgegb=sgegb,
        expb=_f32(ww["L0_ffn_exp_bias"]).reshape(1, 8),
        ew1T=_f32(np.transpose(np.asarray(ww["L0_ffn_ew1"]), (0, 2, 1))),
        ew2T=_f32(np.transpose(np.asarray(ww["L0_ffn_ew2"]), (0, 2, 1))),
        ew3T=_f32(np.transpose(np.asarray(ww["L0_ffn_ew3"]), (0, 2, 1))),
        eb1=_f32(ww["L0_ffn_eb1"]).reshape(8, 1, 512),
        eb2=_f32(ww["L0_ffn_eb2"]).reshape(8, 1, 512),
        eb3=_f32(ww["L0_ffn_eb3"]).reshape(8, 1, DM),
        selg=_f32(ww["L0_ffn_sel_ln_g"]).reshape(1, 512),
        selb=_f32(np.asarray(ww["L0_ffn_sel_ln_b"]).reshape(4, 128).T),
        onesf=np.ones((1, 128), np.float32),
        identbf=np.eye(128, dtype=np.float32).astype(BF),
        iota8=np.tile(np.arange(8, dtype=np.float32), (128, 1)),
    )


def l0ffn_moe_tables(index_b, index_t):
    idx_per_core = []
    for c in range(NCORES):
        own = np.where(np.asarray(index_b) == c)[0]
        idx_per_core.append(np.asarray(index_t)[own].astype(np.int64))
    mx = max(max((len(v) for v in idx_per_core)), 1)
    C = int(np.ceil(mx / 128.0)) * 128
    tables, masks = [], []
    for v in idx_per_core:
        used = set(int(i) for i in v)
        safe = next(i for i in range(T) if i not in used)
        t = np.full((C,), safe, np.int64)
        t[: len(v)] = v
        m = np.zeros((C,), np.float32)
        m[: len(v)] = 1.0
        tables.append(np.ascontiguousarray(t.reshape(C // 128, 128).T).astype(np.int32))
        masks.append(np.ascontiguousarray(m.reshape(C // 128, 128).T).astype(np.float32))
    return C, tables, masks


def run_l0ffn(x_full, ww, nc_prog=None):
    C, tables, masks = l0ffn_moe_tables(ww["L0_ffn_index_b"], ww["L0_ffn_index_t"])
    nc = nc_prog if nc_prog is not None else build_l0ffn(C)
    base = l0ffn_host_inputs(ww)
    in_maps = []
    for c in range(NCORES):
        m = dict(base)
        m["x"] = _f32(x_full[c])
        m["midx"] = tables[c]
        m["mmask"] = masks[c]
        in_maps.append(m)
    res = run_bass_kernel_spmd(nc, in_maps, core_ids=list(range(NCORES)))
    return np.stack([r["y"] for r in res.results], axis=0)


def l0attn_host_inputs(ww):
    W_qkv = _f32(ww["L0_attn_W_qkv"])
    W_out = _f32(ww["L0_attn_W_out"])
    wqkvh = np.zeros((8, DM, 320), np.float32)
    wvc = np.zeros((8, DM, 64), np.float32)
    for h in range(8):
        for j in range(320):
            s_, d_ = j // 64, j % 64
            wqkvh[h, :, j] = W_qkv[s_ * 512 + h * 64 + d_, :]
        for d_ in range(64):
            wvc[h, :, d_] = W_qkv[5 * 512 + h * 64 + d_, :]
    iota_i = np.arange(128)
    diag_keep = (iota_i[None, :] >= iota_i[:, None]).astype(np.float32)
    t1m = np.zeros((128, 2, 256), np.float32)
    t1m[:, 0, 0:128] = diag_keep
    t1m[:, 0, 128:256] = 1.0
    t1m[:, 1, 128:256] = diag_keep
    latm = np.zeros((128, 4, 512), np.float32)
    kk = np.arange(512)
    for m in range(4):
        latm[:, m, :] = ((128 * m + iota_i)[:, None] > kk[None, :]).astype(np.float32)
    caum = np.zeros((128, 4, 512), np.float32)
    for m in range(4):
        caum[:, m, :] = np.where(kk[None, :] <= (128 * m + iota_i)[:, None], 0.0, -1e38)
    return dict(
        wqkvh=wqkvh, wvc=wvc, woutT=np.ascontiguousarray(W_out.T),
        identbf=np.eye(128, dtype=np.float32).astype(BF),
        t1t_mask=t1m.astype(BF),
        lat_mask=latm.astype(BF),
        causal_mask=caum.astype(BF),
    )


def l1ffn_host_inputs(ww):
    return dict(
        wr=_f32(ww["L1_ffn_router_kernel_DE"]),
        wg=_f32(ww["L1_ffn_kernel_gating_EDF"]),
        wu=_f32(ww["L1_ffn_kernel_up_EDF"]),
        wd=_f32(ww["L1_ffn_kernel_down_EFD"]),
        onesf=np.ones((1, 128), np.float32),
        identbf=np.eye(128, dtype=np.float32).astype(BF),
        onesbf=np.ones((1, 128), np.float32).astype(BF),
        identf=np.eye(128, dtype=np.float32),
        iota8=np.tile(np.arange(8, dtype=np.float32), (128, 1)),
        trilow=np.triu(np.ones((128, 128), np.float32), 1).astype(BF),
        pidx=np.arange(128, dtype=np.float32).reshape(128, 1),
    )


def l1attn_host_inputs(ww):
    NH, DH, DR, ROPE = 8, 64, 32, 10000.0
    SCL = (DH + DR) ** -0.5
    theta = 1.0 / ROPE ** (2.0 * np.arange(16) / 32.0)
    tpos = np.arange(T)

    def tab(ang, scl=1.0):
        return np.ascontiguousarray(
            (ang * scl).reshape(32, 128, 16).transpose(1, 0, 2)).astype(np.float32)

    angT = tpos[:, None] * theta[None, :]
    angB = (2 * tpos - 63)[:, None] * theta[None, :]
    cosT, sinT = tab(np.cos(angT), SCL), tab(np.sin(angT), SCL)
    cosB, sinB = tab(np.cos(angB), SCL), tab(np.sin(angB), SCL)
    cosS, sinS = tab(np.cos(angT)), tab(np.sin(angT))
    ranks = np.arange(96)
    angR = ranks[:, None] * theta[None, :]
    cosR = np.tile(np.cos(angR).reshape(1, 3, 32, 16), (128, 1, 1, 1)).astype(np.float32)
    sinR = np.tile(np.sin(angR).reshape(1, 3, 32, 16), (128, 1, 1, 1)).astype(np.float32)
    p = np.arange(128)
    wv = np.arange(256)
    bandmask = np.where((wv[None, :] >= p[:, None] + 65) & (wv[None, :] <= p[:, None] + 128),
                        0.0, -1e38).astype(np.float32)
    kf = np.arange(512)
    imaskd = np.zeros((128, 4, 512), np.float32)
    for m in range(4):
        thr = np.maximum(0, 128 * m + p - 63)
        imaskd[:, m, :] = np.where(kf[None, :] >= thr[:, None], -1e38, 0.0)
    imaskp = np.where((p[:, None] < 63) & (kf[None, :] >= 449 + p[:, None]), -1e38, 0.0
                      ).astype(np.float32)
    sv = np.arange(192)
    s0loc = (((sv[None, :] >= np.maximum(0, p - 63)[:, None]) & (sv[None, :] <= p[:, None]))
             ).astype(np.float32)
    s0cau = (sv[None, :] > p[:, None]).astype(np.float32)
    s0add = (1e38 * s0loc - 1e38 * s0cau).astype(np.float32)
    return dict(
        wdkvkr=np.ascontiguousarray(np.concatenate(
            [_f32(ww["L1_attn_W_dkv"]), _f32(ww["L1_attn_W_kr"])], 1)),
        wdq=_f32(ww["L1_attn_W_dq"]), wqr=_f32(ww["L1_attn_W_qr"]),
        widxq=_f32(ww["L1_attn_idx_wq"]), widxk=_f32(ww["L1_attn_idx_wk"]),
        wuqT=np.ascontiguousarray(_f32(ww["L1_attn_W_uq"]).T),
        wukT=np.ascontiguousarray(_f32(ww["L1_attn_W_uk"]).T),
        wuv=_f32(ww["L1_attn_W_uv"]), wout=_f32(ww["L1_attn_W_out"]),
        idxw=_f32(ww["L1_attn_idx_w"]).reshape(1, 4),
        rawdelta=_f32(ww["L1_attn_raw_delta"]).reshape(1, 16),
        cosT=cosT, sinT=sinT, cosB=cosB, sinB=sinB, cosS=cosS, sinS=sinS,
        cosR=cosR, sinR=sinR,
        identbf=np.eye(128, dtype=np.float32).astype(BF),
        onesf=np.ones((1, 128), np.float32),
        bandmask=bandmask.astype(BF),
        imaskd=imaskd.astype(BF),
        imaskp=imaskp.astype(BF),
        s0add=s0add,
    )


def _launch(nc, base, x_full, extra=None):
    in_maps = []
    for c in range(NCORES):
        m = dict(base)
        m["x"] = np.ascontiguousarray(x_full[c], dtype=np.float32)
        if extra is not None:
            m.update(extra[c])
        in_maps.append(m)
    res = run_bass_kernel_spmd(nc, in_maps, core_ids=list(range(NCORES)))
    return np.stack([r["y"] for r in res.results], axis=0)


def kernel(**inputs):
    x = np.asarray(inputs["L0_attn_x"], dtype=np.float32)
    C, tables, masks = l0ffn_moe_tables(np.asarray(inputs["L0_ffn_index_b"]),
                                        np.asarray(inputs["L0_ffn_index_t"]))
    base = {}
    base.update(l0ffn_host_inputs(inputs))
    base.update(l0attn_host_inputs(inputs))
    base.update(l1ffn_host_inputs(inputs))
    base.update(l1attn_host_inputs(inputs))
    nc = build_fused(C)
    in_maps = []
    for c in range(NCORES):
        m = dict(base)
        m["x"] = np.ascontiguousarray(x[c], dtype=np.float32)
        m["midx"] = tables[c]
        m["mmask"] = masks[c]
        in_maps.append(m)
    res = run_bass_kernel_spmd(nc, in_maps, core_ids=list(range(NCORES)))
    y = np.stack([r["y"] for r in res.results], axis=0)
    return np.ascontiguousarray(y, dtype=np.float32)


# version 07
# speedup vs baseline: 1.0563x; 1.0563x over previous
import numpy as np
import ml_dtypes
from contextlib import ExitStack

import concourse.bass as bass
import concourse.mybir as mybir
from concourse.bass import IndirectOffsetOnAxis
from concourse.bass_utils import run_bass_kernel_spmd

F32 = mybir.dt.float32
BF16 = mybir.dt.bfloat16
I32 = mybir.dt.int32
U32 = mybir.dt.uint32
AF = mybir.ActivationFunctionType
ALU = mybir.AluOpType
AXX = mybir.AxisListType.X
BF = ml_dtypes.bfloat16

NCORES = 8
T = 4096
DM = 1024
NSTRIP = T // 128
EPS = 1e-5
NEGBIG = -3.402823466e38
OOB = 1 << 30

ENGINES = ("sync", "tensor", "vector", "scalar", "gpsimd")
ts = bass.ts


class Sem:
    __slots__ = ("h", "n", "name", "sw")
    def __init__(self, h, name, sw=False):
        self.h, self.n, self.name, self.sw = h, 0, name, sw
    def inc(self, k=1):
        self.n += k
        return self.n


class Prog:
    def __init__(self, nc, ctx):
        self.nc, self.ctx = nc, ctx
        self.ops = {e: [] for e in ENGINES}
        self.labels = {e: [] for e in ENGINES}
        self.stage_label = None
        self._sems = {}
        self._alias = None
        self.barsc = None

    def _pool(self, name, sw=False):
        if name not in self._sems:
            h = self.ctx.enter_context(self.nc.semaphore(name))
            self._sems[name] = Sem(h, name, sw)
        assert self._sems[name].sw == sw, name
        return self._sems[name]

    def S(self, name, sw=False):
        if self._alias is None:
            return self._pool(name, sw)
        if name not in self._alias:
            k = sum(1 for s in self._alias.values() if s.sw == sw)
            self._alias[name] = self._pool(f"gs{k}" if sw else f"g{k}", sw)
        s = self._alias[name]
        assert s.sw == sw, f"sem {name} class mismatch"
        return s

    def new_stage(self):
        self._alias = {}

    def barrier(self):
        bs = self._pool("_bar")
        def ft(e):
            return e.nop()
        def fv(e):
            e.drain()
            return e.nop()
        def fs(e):
            e.drain()
            return e.activation(self.barsc[:, :], self.barsc[:, :], AF.Relu)
        def fg(e):
            e.drain()
            return e.engine_nop()
        self.op("tensor", ft, sem=bs, drain=False)
        self.op("vector", fv, sem=bs, drain=False)
        self.op("scalar", fs, sem=bs, drain=False)
        self.op("gpsimd", fg, sem=bs, drain=False)
        tgt = bs.n
        self.waits(ENGINES, [(bs, tgt)])

    def add(self, eng, fn):
        self.ops[eng].append(fn)
        self.labels[eng].append(self.stage_label)

    def wait(self, eng, sem, val):
        if val > 0:
            self.add(eng, lambda e, _s=sem.h, _v=val: e.wait_ge(_s, _v))

    def waits(self, engs, pairs):
        for eng in engs:
            for sem, val in pairs:
                self.wait(eng, sem, val)

    def dma(self, eng, sem, out, in_, **kw):
        if self._alias is not None:
            assert sem.sw == (eng == "gpsimd"), f"sem {sem.name} dma class"
        v = sem.inc(16)
        def f(e, _o=out, _i=in_, _s=sem.h, _kw=kw):
            e.dma_start(out=_o, in_=_i, **_kw).then_inc(_s, 16)
        self.add(eng, f)
        return v

    def gather(self, sem, out, in_dram, idx_ap, bound):
        if self._alias is not None:
            assert sem.sw, f"sem {sem.name} gather class"
        v = sem.inc(16)
        def f(e, _o=out, _i=in_dram, _x=idx_ap, _b=bound, _s=sem.h):
            if _b is None:
                e.indirect_dma_start(_o, None, _i, IndirectOffsetOnAxis(_x, axis=0)
                                     ).then_inc(_s, 16)
            else:
                e.indirect_dma_start(_o, None, _i, IndirectOffsetOnAxis(_x, axis=0),
                                     bounds_check=_b, oob_is_err=False).then_inc(_s, 16)
        self.add("gpsimd", f)
        return v

    def scatter(self, sem, out_dram, idx_ap, in_, bound, op=ALU.bypass):
        if self._alias is not None:
            assert sem.sw, f"sem {sem.name} scatter class"
        v = sem.inc(16)
        def f(e, _o=out_dram, _i=in_, _x=idx_ap, _b=bound, _s=sem.h, _op=op):
            if _b is None:
                e.indirect_dma_start(_o, IndirectOffsetOnAxis(_x, axis=0), _i, None,
                                     compute_op=_op).then_inc(_s, 16)
            else:
                e.indirect_dma_start(_o, IndirectOffsetOnAxis(_x, axis=0), _i, None,
                                     bounds_check=_b, oob_is_err=False,
                                     compute_op=_op).then_inc(_s, 16)
        self.add("gpsimd", f)
        return v

    def mm(self, out, lhsT, rhs, start, stop, sem=None):
        v = sem.inc(1) if sem is not None else None
        def f(e, _o=out, _l=lhsT, _r=rhs, _st=start, _sp=stop, _s=sem):
            i = e.matmul(_o, _l, _r, start=_st, stop=_sp)
            if _s is not None:
                i.then_inc(_s.h, 1)
        self.add("tensor", f)
        return v

    def tp(self, out, in_, ident, sem=None):
        v = sem.inc(1) if sem is not None else None
        def f(e, _o=out, _i=in_, _id=ident, _s=sem):
            i = e.transpose(_o, _i, _id)
            if _s is not None:
                i.then_inc(_s.h, 1)
        self.add("tensor", f)
        return v

    def tnop(self, sem):
        v = sem.inc(1)
        self.add("tensor", lambda e, _s=sem.h: e.nop().then_inc(_s, 1))
        return v

    def opt(self, eng, fn, drain=True):
        s = self.S("_tk_" + eng)
        v = self.op(eng, fn, sem=s, drain=drain)
        return (s, v)

    def mmt(self, out, lhsT, rhs, start, stop):
        s = self.S("_tk_mm")
        v = self.mm(out, lhsT, rhs, start, stop, sem=s)
        return (s, v)

    def op(self, eng, fn, sem=None, drain=True):
        v = sem.inc(1) if sem is not None else None
        def f(e, _fn=fn, _s=sem, _d=drain):
            i = _fn(e)
            if _s is not None:
                i.then_inc(_s.h, 1)
            if _d:
                e.drain()
        self.add(eng, f)
        return v

    def _replay(self, eng, e):
        fns, labs = self.ops[eng], self.labels[eng]
        i, n = 0, len(fns)
        while i < n:
            lab = labs[i]
            j = i
            while j < n and labs[j] == lab:
                j += 1
            if lab is None:
                for k in range(i, j):
                    fns[k](e)
            else:
                with self.nc.named_scope(lab):
                    for k in range(i, j):
                        fns[k](e)
            i = j

    def build(self):
        with self.nc.Block() as block:
            @block.sync
            def _(e):
                self._replay("sync", e)
            @block.tensor
            def _(e):
                self._replay("tensor", e)
            @block.vector
            def _(e):
                self._replay("vector", e)
            @block.scalar
            def _(e):
                self._replay("scalar", e)
            @block.gpsimd
            def _(e):
                self._replay("gpsimd", e)


class PBank:
    def __init__(self, P, ctx, nc, name, shape, dt=F32):
        self.P = P
        self.ap = ctx.enter_context(nc.psum_tensor(shape, dt))
        self.full = P.S(name + "_fu")
        self.free = P.S(name + "_fr")
        self.fills = self.full.n
        self._free_base = self.free.n
        self.freed_target = self.free.n

    def mm_group(self, parts, extra_sem=None):
        P = self.P
        if self.freed_target > self._free_base:
            P.wait("tensor", self.free, self.freed_target)
        n = len(parts)
        for i, (osl, lhsT, rhs) in enumerate(parts):
            out = self.ap[:, :] if osl is None else osl
            P.mm(out, lhsT, rhs, i == 0, i == n - 1,
                 sem=self.full if i == n - 1 else None)
        if extra_sem is not None:
            self.P.tnop(extra_sem)
        self.fills += 1

    def tp_into(self, in_, ident):
        P = self.P
        if self.freed_target > self._free_base:
            P.wait("tensor", self.free, self.freed_target)
        P.tp(self.ap[0:in_.shape[-1], 0:in_.shape[0]], in_, ident, sem=self.full)
        self.fills += 1

    def acc_start(self):
        if self.freed_target > self._free_base:
            self.P.wait("tensor", self.free, self.freed_target)

    def acc_mm(self, out_slice, lhsT, rhs, first, last):
        self.P.mm(self.ap[:, :] if out_slice is None else out_slice, lhsT, rhs,
                  first, last, sem=self.full if last else None)
        if last:
            self.fills += 1

    def consume(self, eng, fn):
        P = self.P
        P.wait(eng, self.full, self.fills)
        self.freed_target += 1
        P.op(eng, lambda e, _f=fn, _a=self.ap: _f(e, _a), sem=self.free)
        return self.free, self.freed_target




class DStream:
    def __init__(self, P, name, eng, depth=2):
        self.P, self.eng, self.depth = P, eng, depth
        self.sems = [P.S(f"{name}_{i}", sw=(eng == "gpsimd")) for i in range(depth)]
        self.last = [None] * depth
        self.k = 0

    def _next(self):
        i = self.k % self.depth
        s = self.sems[i]
        if self.last[i] is not None:
            self.P.waits((self.eng,), [self.last[i]])
        self.k += 1
        return s, i

    def issue(self, out, in_, **kw):
        s, i = self._next()
        v = self.P.dma(self.eng, s, out, in_, **kw)
        self.last[i] = (s, v)
        return (s, v)

    def issue_gather(self, out, in_dram, idx_ap, bound):
        s, i = self._next()
        v = self.P.gather(s, out, in_dram, idx_ap, bound)
        self.last[i] = (s, v)
        return (s, v)

    def issue_scatter(self, out_dram, idx_ap, in_, bound, op=ALU.bypass):
        s, i = self._next()
        v = self.P.scatter(s, out_dram, idx_ap, in_, bound, op=op)
        self.last[i] = (s, v)
        return (s, v)

    def tokens(self):
        return [(s, s.n) for s in self.sems]


class LNKit:
    def __init__(self, P, sb, tag=""):
        self.P = P
        self.tag = tag
        self.sum = sb(f"ln_sum{tag}", [128, 1], F32)
        self.ssq = sb(f"ln_ssq{tag}", [128, 1], F32)
        self.mu = sb(f"ln_mu{tag}", [128, 1], F32)
        self.mu2 = sb(f"ln_mu2{tag}", [128, 1], F32)
        self.var = sb(f"ln_var{tag}", [128, 1], F32)
        self.rstd = sb(f"ln_rstd{tag}", [128, 1], F32)
        self.junk = sb(f"ln_junk{tag}", [128, DM], BF16)
        self.s_a = P.S(f"ln_a{tag}")
        self.s_v = P.S(f"ln_v{tag}")
        self.s_fr = P.S(f"ln_fr{tag}")
        self.s_done = P.S(f"ln_dn{tag}")
        self.ncall = 0
        self._frb = self.s_fr.n

    def emit(self, xs, out_bf, n_free, g_rep=None, in_tok=None):
        P = self.P
        if in_tok is not None:
            P.waits(("scalar", "vector"), [in_tok])
        if self.ncall > 0:
            P.wait("scalar", self.s_fr, self._frb + self.ncall)
        self.ncall += 1
        P.op("scalar", lambda e, _x=xs, _n=n_free: e.activation(
            self.junk[:, 0:_n], _x, AF.Square, accum_out=self.ssq[:, :]), sem=self.s_a)
        P.op("vector", lambda e, _x=xs: e.reduce_sum(self.sum[:, :], _x, axis=AXX))
        P.wait("vector", self.s_a, self.s_a.n)
        inv = 1.0 / n_free
        P.op("vector", lambda e, _i=inv: e.tensor_scalar_mul(self.mu[:, :], self.sum[:, :], _i))
        P.op("vector", lambda e, _i=inv: e.tensor_scalar_mul(self.var[:, :], self.ssq[:, :], _i))
        P.op("vector", lambda e: e.scalar_tensor_tensor(self.mu2[:, :], self.mu[:, :], -1.0,
                                                        self.mu[:, :], ALU.mult, ALU.mult))
        P.op("vector", lambda e: e.scalar_tensor_tensor(self.var[:, :], self.var[:, :], float(EPS),
                                                        self.mu2[:, :], ALU.add, ALU.add),
             sem=self.s_v)
        P.wait("scalar", self.s_v, self.s_v.n)
        P.op("scalar", lambda e: e.activation(self.rstd[:, :], self.var[:, :], AF.Sqrt), sem=self.s_a)
        P.wait("vector", self.s_a, self.s_a.n)
        P.op("vector", lambda e: e.reciprocal(self.rstd[:, :], self.rstd[:, :]))
        if g_rep is None:
            P.op("vector", lambda e, _x=xs, _o=out_bf: e.tensor_scalar(
                _o, _x, self.mu[:, :], self.rstd[:, :], ALU.subtract, ALU.mult),
                sem=self.s_done)
        else:
            P.op("vector", lambda e, _x=xs, _n=n_free: e.tensor_scalar(
                self.junk[:, 0:_n], _x, self.mu[:, :], self.rstd[:, :],
                ALU.subtract, ALU.mult))
            P.op("vector", lambda e, _o=out_bf, _g=g_rep, _n=n_free: e.tensor_mul(
                _o, self.junk[:, 0:_n], _g), sem=self.s_done)
        P.op("vector", lambda e: e.nop(), sem=self.s_fr, drain=False)
        self.sq_tok = (self.s_a, self.s_a.n - 1)
        return self.s_done, self.s_done.n


def build_l0ffn(C, stage="full"):
    nc = bass.Bass("TRN2", target_bir_lowering=False)
    dr = lambda n, s, d, k="ExternalInput": nc.dram_tensor(n, s, d, kind=k).ap()
    a = dict(
        x=dr("x", [T, DM], F32),
        y=dr("y", [T, DM], F32, "ExternalOutput"),
        sw1T=dr("sw1T", [DM, 512], F32), sw2T=dr("sw2T", [DM, 512], F32),
        sw3T=dr("sw3T", [512, DM], F32),
        sb1=dr("sb1", [1, 512], F32), sb2=dr("sb2", [1, 512], F32), sb3=dr("sb3", [1, DM], F32),
        slng=dr("slng", [1, 512], F32), slnb=dr("slnb", [128, 4], F32),
        sgeg=dr("sgeg", [DM, 16], F32), sgegb=dr("sgegb", [1, 16], F32),
        expb=dr("expb", [1, 8], F32),
        ew1T=dr("ew1T", [8, DM, 512], F32), ew2T=dr("ew2T", [8, DM, 512], F32),
        ew3T=dr("ew3T", [8, 512, DM], F32),
        eb1=dr("eb1", [8, 1, 512], F32), eb2=dr("eb2", [8, 1, 512], F32),
        eb3=dr("eb3", [8, 1, DM], F32),
        selg=dr("selg", [1, 512], F32), selb=dr("selb", [128, 4], F32),
        midx=dr("midx", [128, C // 128], I32),
        mmask=dr("mmask", [128, C // 128], F32),
        onesf=dr("onesf", [1, 128], F32), identbf=dr("identbf", [128, 128], BF16),
        iota8=dr("iota8", [128, 8], F32),
        xnat_d=dr("xnat_d", [T, DM], BF16, "Internal"),
        shout_d=dr("shout_d", [T, DM], BF16, "Internal"),
        C=C, scale=0.5,
    )
    ctx = ExitStack()
    P = Prog(nc, ctx)
    emit_l0ffn(nc, ctx, P, a, stage=stage)
    P.build()
    ctx.close()
    return nc


def emit_l0ffn(nc, ctx, P, a, stage="full"):
    a = dict(a); a["stage"] = stage
    C = a["C"]
    ngrp = C // 128
    sb = lambda n, s, d: ctx.enter_context(nc.sbuf_tensor(n, s, d))

    onesf = sb("onesf_sb", [1, 128], F32)
    ident = sb("ident_sb", [128, 128], BF16)
    iota8 = sb("iota8_sb", [128, 8], F32)
    s_cst = P.S("cst")
    P.dma("sync", s_cst, onesf[:, :], a["onesf"])
    P.dma("sync", s_cst, ident[:, :], a["identbf"])
    P.dma("sync", s_cst, iota8[:, :], a["iota8"])
    v_cst = s_cst.n

    ab = ExitStack()
    sbA = lambda n, s, d: ab.enter_context(nc.sbuf_tensor(n, s, d))

    xlnT = sbA("xlnT", [128, 8, T], BF16)
    w1 = sbA("w1", [128, 8, 512], BF16)
    w2 = sbA("w2", [128, 8, 512], BF16)
    w3 = sbA("w3", [128, 4, DM], BF16)
    b1 = sbA("b1", [1, 512], F32)
    b2 = sbA("b2", [1, 512], F32)
    b3 = sbA("b3", [1, DM], F32)
    slnbbf = sbA("slnbbf", [128, 4], BF16)
    grow = sbA("grow", [1, 512], F32)
    G = sbA("G", [128, 512], F32)
    beff = sbA("beff", [1, DM], F32)
    xs = [sbA(f"xs{i}", [128, DM], F32) for i in range(2)]
    xln = sbA("xln", [128, DM], BF16)
    s1 = sbA("s1", [128, 512], F32)
    prod = sbA("prod", [128, 512], F32)
    shbf = sbA("shbf", [128, 512], BF16)
    shT = sbA("shT", [128, 4, 128], BF16)
    sobf = [sbA(f"sobf{i}", [128, DM], BF16) for i in range(2)]
    ysb = [sbA(f"ysb{i}", [128, DM], F32) for i in range(2)]

    pb1 = PBank(P, ab, nc, "pb1", [128, 512])
    pb2 = PBank(P, ab, nc, "pb2", [128, 512])
    pb3 = PBank(P, ab, nc, "pb3", [128, 512])
    pb4 = PBank(P, ab, nc, "pb4", [128, 512])
    ptA = PBank(P, ab, nc, "ptA", [128, 128], BF16)
    ptB = PBank(P, ab, nc, "ptB", [128, 128], BF16)

    lnk = LNKit(P, sbA)

    s_ldwg = P.S("ldwg", sw=True)
    s_ldws = P.S("ldws")
    P.dma("gpsimd", s_ldwg, w1[:, :, :], a["sw1T"].rearrange("(c p) h -> p c h", p=128))
    P.dma("gpsimd", s_ldwg, w2[:, :, :], a["sw2T"].rearrange("(c p) h -> p c h", p=128))
    P.dma("gpsimd", s_ldwg, w3[:, :, :], a["sw3T"].rearrange("(c p) h -> p c h", p=128))
    P.dma("gpsimd", s_ldwg, slnbbf[:, :], a["slnb"])
    P.dma("sync", s_ldws, b1[:, :], a["sb1"])
    P.dma("sync", s_ldws, b2[:, :], a["sb2"])
    P.dma("sync", s_ldws, b3[:, :], a["sb3"])
    P.dma("sync", s_ldws, grow[:, :], a["slng"])
    v_ldwg, v_ldws = s_ldwg.n, s_ldws.n

    P.waits(("tensor",), [(s_cst, v_cst), (s_ldwg, v_ldwg), (s_ldws, v_ldws)])
    pb1.mm_group([(None, onesf[:, :], grow[:, :])])
    pb1.consume("vector", lambda e, p: e.tensor_copy(G[:, :], p[:, :]))
    P.wait("vector", s_ldws, v_ldws)
    for dt_ in range(2):
        bk = pb3 if dt_ == 0 else pb4
        bk.mm_group([(bk.ap[0:1, :], slnbbf[:, cc:cc+1], w3[:, cc, ts(dt_, 512)]) for cc in range(4)])
        bk.consume("vector", lambda e, p, _d=dt_: e.tensor_add(
            beff[:, ts(_d, 512)], p[0:1, :], b3[:, ts(_d, 512)]))
    sem_, val_ = pb4.free, pb4.freed_target
    P.waits(("tensor", "scalar"), [(sem_, val_)])

    ld_x = DStream(P, "ldx", "sync", 2)
    st_y = DStream(P, "sty", "gpsimd", 2)
    st_sh = DStream(P, "stsh", "gpsimd", 2)
    st_xn = DStream(P, "stxn", "gpsimd", 2)
    s_xfree = P.S("xfree")
    sxf_base = s_xfree.n

    tp_end = [0, 0]
    prev_mul = [None]
    prev_sq = [None]
    xn_tok = None
    y_tok = [None, None]
    sh_tok = [None, None]

    for s in range(NSTRIP):
        def strip(s=s):
            nonlocal xn_tok
            xb, yb, sob = xs[s % 2], ysb[s % 2], sobf[s % 2]
            if s >= 2:
                P.wait("sync", s_xfree, sxf_base + s - 1)
            tok = ld_x.issue(xb[:, :], a["x"][ts(s, 128), :])
            P.waits(("vector", "scalar"), [tok])
            if s >= 1:
                P.wait("vector", ptA.full, tp_end[0])
                P.wait("vector", ptB.full, tp_end[1])
                P.waits(("vector",), [xn_tok])
            sd, vd = lnk.emit(xb[:, :], xln[:, :], DM)
            P.wait("gpsimd", sd, vd)
            xn_tok = st_xn.issue(a["xnat_d"][ts(s, 128), :], xln[:, :])
            P.wait("tensor", sd, vd)
            for c in range(8):
                pt = ptA if c % 2 == 0 else ptB
                pt.tp_into(xln[:, ts(c, 128)], ident[:, :])
                pt.consume("vector", lambda e, p, _c=c: e.tensor_copy(
                    xlnT[:, _c, ts(s, 128)], p[:, 0:128]))
            P.wait("tensor", ptA.free, ptA.freed_target)
            P.wait("tensor", ptB.free, ptB.freed_target)
            if stage == "ln":
                if y_tok[s % 2] is not None:
                    P.waits(("gpsimd",), [y_tok[s % 2]])
                y_tok[s % 2] = st_y.issue(a["y"][ts(s, 128), :], xb[:, :])
                P.op("vector", lambda e: e.nop(), sem=s_xfree, drain=False)
                tp_end[0], tp_end[1] = ptA.fills, ptB.fills
                return
            pb1.mm_group([(None, xlnT[:, cc, ts(s, 128)], w1[:, cc, :]) for cc in range(8)]
                         + [(None, onesf[:, :], b1[:, :])])
            pb2.mm_group([(None, xlnT[:, cc, ts(s, 128)], w2[:, cc, :]) for cc in range(8)]
                         + [(None, onesf[:, :], b2[:, :])])
            if prev_mul[0] is not None:
                P.waits(("scalar",), [prev_mul[0]])
            fsem, fval = pb1.consume("scalar", lambda e, p: e.activation(s1[:, :], p[:, :], AF.Silu))
            P.wait("vector", fsem, fval)
            if prev_sq[0] is not None:
                P.waits(("vector",), [prev_sq[0]])
            ptok = pb2.consume("vector", lambda e, p: e.tensor_mul(prod[:, :], s1[:, :], p[:, :]))
            prev_mul[0] = ptok
            if stage == "x12":
                if y_tok[s % 2] is not None:
                    P.waits(("gpsimd",), [y_tok[s % 2]])
                P.waits(("gpsimd",), [ptok])
                y_tok[s % 2] = st_y.issue(a["y"][ts(s, 128), :], xb[:, :])
                P.op("vector", lambda e: e.nop(), sem=s_xfree, drain=False)
                tp_end[0], tp_end[1] = ptA.fills, ptB.fills
                return
            sd2, vd2 = lnk.emit(prod[:, :], shbf[:, :], 512, g_rep=G[:, :], in_tok=ptok)
            prev_sq[0] = lnk.sq_tok
            P.wait("tensor", sd2, vd2)
            for c in range(4):
                pt = ptA if c % 2 == 0 else ptB
                pt.tp_into(shbf[:, ts(c, 128)], ident[:, :])
                pt.consume("vector", lambda e, p, _c=c: e.tensor_copy(
                    shT[:, _c, :], p[:, 0:128]))
            P.wait("tensor", ptA.free, ptA.freed_target)
            P.wait("tensor", ptB.free, ptB.freed_target)
            if stage == "shln":
                if y_tok[s % 2] is not None:
                    P.waits(("gpsimd",), [y_tok[s % 2]])
                y_tok[s % 2] = st_y.issue(a["y"][ts(s, 128), :], xb[:, :])
                P.op("vector", lambda e: e.nop(), sem=s_xfree, drain=False)
                tp_end[0], tp_end[1] = ptA.fills, ptB.fills
                return
            if y_tok[s % 2] is not None:
                P.waits(("vector",), [y_tok[s % 2]])
                if sh_tok[s % 2] is not None:
                    P.waits(("scalar",), [sh_tok[s % 2]])
            for dt_ in range(2):
                bk = pb3 if dt_ == 0 else pb4
                bk.mm_group([(None, shT[:, cc, :], w3[:, cc, ts(dt_, 512)]) for cc in range(4)]
                            + [(None, onesf[:, :], beff[:, ts(dt_, 512)])])
                ytok_c = bk.consume("vector", lambda e, p, _d=dt_, _y=yb, _x=xb: e.scalar_tensor_tensor(
                    _y[:, ts(_d, 512)], p[:, :], a["scale"], _x[:, ts(_d, 512)],
                    ALU.mult, ALU.add))
                if stage not in ("ab_nosh", "ab_nocopy"):
                    P.waits(("scalar",), [ytok_c])
                    bk.consume("scalar", lambda e, p, _d=dt_, _o=sob: e.copy(_o[:, ts(_d, 512)], p[:, :]))
            P.op("vector", lambda e: e.nop(), sem=s_xfree, drain=False)
            P.wait("gpsimd", pb3.free, pb3.freed_target)
            P.wait("gpsimd", pb4.free, pb4.freed_target)
            y_tok[s % 2] = st_y.issue(a["y"][ts(s, 128), :], yb[:, :])
            if stage != "ab_nosh":
                sh_tok[s % 2] = st_sh.issue(a["shout_d"][ts(s, 128), :], sob[:, :])
            tp_end[0], tp_end[1] = ptA.fills, ptB.fills
        strip(s)

    ab_tokens = st_y.tokens() + st_sh.tokens() + st_xn.tokens()
    ab.close()
    if stage in ("ab", "ln", "x12", "shln", "ab_nosh", "ab_nocopy", "ab_shsync"):
        P.waits(("gpsimd",), ab_tokens)
        return

    mo = ExitStack()
    sbC = lambda n, s, d: mo.enter_context(nc.sbuf_tensor(n, s, d))

    P.waits(ENGINES, ab_tokens)

    idx_sb = sbC("idx_sb", [128, ngrp], I32)
    mmask = sbC("mmask", [128, ngrp], F32)
    xmoe = sbC("xmoe", [128, DM], BF16)
    xmoeT = sbC("xmoeT", [128, 8, C], BF16)
    wsg = sbC("wsg", [128, 8, 16], BF16)
    bsg = sbC("bsg", [1, 16], F32)
    expb_sb = sbC("expb_sb", [1, 8], F32)
    expb_r = sbC("expb_r", [128, 8], F32)
    scs = sbC("scs", [128, ngrp, 16], F32)
    ss = sbC("ss", [128, ngrp], F32)
    es = sbC("es", [128, ngrp, 8], F32)
    top8 = sbC("top8", [128, 8], F32)
    tidx = sbC("tidx", [128, 8], U32)
    t1f = sbC("t1f", [128, 1], F32)
    oh = sbC("oh", [128, ngrp, 8], F32)
    tsq = sbC("tsq", [128, 8], F32)
    tscore = sbC("tscore", [128, 1], F32)
    shw = sbC("shw", [128, ngrp], F32)
    expw = sbC("expw", [128, ngrp], F32)
    dtmp = sbC("dtmp", [128, 1], F32)
    ew1 = [sbC(f"ew1_{i}", [128, 8, 512], BF16) for i in range(2)]
    ew2 = [sbC(f"ew2_{i}", [128, 8, 512], BF16) for i in range(2)]
    ew3 = [sbC(f"ew3_{i}", [128, 4, DM], BF16) for i in range(2)]
    be1 = [sbC(f"be1_{i}", [1, 512], F32) for i in range(2)]
    be2 = [sbC(f"be2_{i}", [1, 512], F32) for i in range(2)]
    be3 = [sbC(f"be3_{i}", [1, DM], F32) for i in range(2)]
    be3eff = [sbC(f"be3eff_{i}", [1, DM], F32) for i in range(2)]
    selbbf = sbC("selbbf", [128, 4], BF16)
    selgrow = sbC("selgrow", [1, 512], F32)
    selG = sbC("selG", [128, 512], F32)
    sl1 = sbC("sl1", [128, 512], F32)
    prodC = sbC("prodC", [128, 512], F32)
    zbf = sbC("zbf", [128, 512], BF16)
    zT = sbC("zT", [128, 4, 128], BF16)
    acc = [sbC(f"acc{i}", [128, ngrp, DM], F32) for i in range(2)]
    shm = sbC("shm", [128, DM], BF16)
    corr = sbC("corr", [128, DM], F32)
    w01 = sbC("w01", [128, 1], F32)
    w02 = sbC("w02", [128, 1], F32)

    qb1 = PBank(P, mo, nc, "qb1", [128, 512])
    qb2 = PBank(P, mo, nc, "qb2", [128, 512])
    qb3 = PBank(P, mo, nc, "qb3", [128, 512])
    qb4 = PBank(P, mo, nc, "qb4", [128, 512])
    qtA = PBank(P, mo, nc, "qtA", [128, 128], BF16)
    qtB = PBank(P, mo, nc, "qtB", [128, 128], BF16)

    lnkC = LNKit(P, sbC, tag="C")

    s_ld = P.S("mld")
    s_ldg = P.S("mldg", sw=True)
    P.dma("sync", s_ld, idx_sb[:, :], a["midx"])
    P.dma("sync", s_ld, mmask[:, :], a["mmask"])
    P.dma("gpsimd", s_ldg, wsg[:, :, :], a["sgeg"].rearrange("(c p) h -> p c h", p=128))
    P.dma("gpsimd", s_ldg, selbbf[:, :], a["selb"])
    P.dma("sync", s_ld, bsg[:, :], a["sgegb"])
    P.dma("sync", s_ld, selgrow[:, :], a["selg"])
    P.dma("sync", s_ld, expb_sb[:, :], a["expb"])
    v_ld0, v_ldg0 = s_ld.n, s_ldg.n
    P.waits(("tensor", "gpsimd"), [(s_ld, v_ld0), (s_ldg, v_ldg0)])
    qb1.mm_group([(qb1.ap[:, 0:8], onesf[:, :], expb_sb[:, :])])
    qb1.consume("vector", lambda e, p: e.tensor_copy(expb_r[:, :], p[:, 0:8]))
    qb2.mm_group([(None, onesf[:, :], selgrow[:, :])])
    qb2.consume("vector", lambda e, p: e.tensor_copy(selG[:, :], p[:, :]))

    s_mz = P.S("mz")
    P.op("vector", lambda e: e.memset(xmoe[:, :], 0.0))
    P.op("vector", lambda e: e.memset(shm[:, :], 0.0), sem=s_mz)
    P.wait("gpsimd", s_mz, s_mz.n)
    gth = DStream(P, "gth", "gpsimd", 2)
    gat_tp_end = [qtA.fills, qtB.fills]
    for g in range(ngrp):
        def grp(g=g):
            if g >= 1:
                P.wait("gpsimd", qtA.full, gat_tp_end[0])
                P.wait("gpsimd", qtB.full, gat_tp_end[1])
            tok = gth.issue_gather(xmoe[:, :], a["xnat_d"], idx_sb[:, g:g+1], None)
            P.waits(("tensor",), [tok])
            for c in range(8):
                pt = qtA if c % 2 == 0 else qtB
                pt.tp_into(xmoe[:, ts(c, 128)], ident[:, :])
                pt.consume("vector", lambda e, p, _c=c, _g=g: e.tensor_copy(
                    xmoeT[:, _c, ts(_g, 128)], p[:, 0:128]))
            gat_tp_end[0], gat_tp_end[1] = qtA.fills, qtB.fills
            P.wait("tensor", qtA.free, qtA.freed_target)
            P.wait("tensor", qtB.free, qtB.freed_target)
            qb3.mm_group([(qb3.ap[:, 0:16], xmoeT[:, cc, ts(g, 128)], wsg[:, cc, :])
                          for cc in range(8)]
                         + [(qb3.ap[:, 0:16], onesf[:, :], bsg[:, :])])
            qb3.consume("vector", lambda e, p, _g=g: e.tensor_copy(scs[:, _g, :], p[:, 0:16]))
        grp(g)

    s_sc = P.S("msc")
    s_gv = P.S("mgv")
    P.wait("scalar", qb3.free, qb3.freed_target)
    for g in range(ngrp):
        def gate(g=g):
            P.op("scalar", lambda e, _g=g: e.activation(ss[:, _g:_g+1], scs[:, _g, 0:1], AF.Sigmoid))
            P.op("scalar", lambda e, _g=g: e.activation(es[:, _g, :], scs[:, _g, 1:9], AF.Sigmoid),
                 sem=s_sc)
            P.wait("vector", s_sc, s_sc.n)
            P.op("vector", lambda e, _g=g: e.tensor_add(tsq[:, :], es[:, _g, :], expb_r[:, :]))
            P.op("vector", lambda e: e.max(out=top8[:, :], in_=tsq[:, :]))
            P.op("vector", lambda e: e.max_index(tidx[:, :], top8[:, :], tsq[:, :]))
            P.op("vector", lambda e: e.tensor_copy(t1f[:, :], tidx[:, 0:1]))
            P.op("vector", lambda e, _g=g: e.tensor_tensor(
                oh[:, _g, :], iota8[:, :], t1f[:, :].to_broadcast([128, 8]), ALU.is_equal))
            P.op("vector", lambda e, _g=g: e.tensor_mul(tsq[:, :], oh[:, _g, :], es[:, _g, :]))
            P.op("vector", lambda e: e.reduce_sum(tscore[:, :], tsq[:, :], axis=AXX))
            P.op("vector", lambda e, _g=g: e.tensor_sub(dtmp[:, :], ss[:, _g:_g+1], tscore[:, :]),
                 sem=s_gv)
            P.wait("scalar", s_gv, s_gv.n)
            P.op("scalar", lambda e, _g=g: e.activation(shw[:, _g:_g+1], dtmp[:, :], AF.Sigmoid),
                 sem=s_sc)
            P.wait("vector", s_sc, s_sc.n)
            P.op("vector", lambda e, _g=g: e.tensor_scalar(
                expw[:, _g:_g+1], shw[:, _g:_g+1], -1.0, 1.0, ALU.mult, ALU.add))
        gate(g)
    s_gates = P.S("gdone")
    P.op("vector", lambda e: e.nop(), sem=s_gates, drain=False)
    v_gates = s_gates.n
    P.wait("scalar", s_gates, v_gates)
    if stage == "abg":
        P.waits(("gpsimd",), [(s_gates, v_gates)])
        mo.close()
        return

    s_wl = [P.S("wl0", sw=True), P.S("wl1", sw=True)]
    s_wls = [P.S("wls0"), P.S("wls1")]
    wu_tok = [None, None]

    def load_expert(e_, b):
        if wu_tok[b] is not None:
            P.waits(("gpsimd", "sync"), [wu_tok[b]])
        P.dma("gpsimd", s_wl[b], ew1[b][:, :, :], a["ew1T"][e_].rearrange("(c p) h -> p c h", p=128))
        P.dma("gpsimd", s_wl[b], ew2[b][:, :, :], a["ew2T"][e_].rearrange("(c p) h -> p c h", p=128))
        P.dma("gpsimd", s_wl[b], ew3[b][:, :, :], a["ew3T"][e_].rearrange("(c p) h -> p c h", p=128))
        P.dma("sync", s_wls[b], be1[b][:, :], a["eb1"][e_])
        P.dma("sync", s_wls[b], be2[b][:, :], a["eb2"][e_])
        P.dma("sync", s_wls[b], be3[b][:, :], a["eb3"][e_])
        return (s_wl[b].n, s_wls[b].n)

    wv = [load_expert(0, 0), load_expert(1, 1)]
    prev_mulC = [None]
    prev_sqC = [None]
    for e_ in range(8):
        def expert(e_=e_):
            b = e_ % 2
            P.waits(("tensor", "vector"), [(s_wl[b], wv[b][0]), (s_wls[b], wv[b][1])])
            for dt_ in range(2):
                bk = qb3 if dt_ == 0 else qb4
                bk.mm_group([(bk.ap[0:1, :], selbbf[:, cc:cc+1], ew3[b][:, cc, ts(dt_, 512)])
                             for cc in range(4)])
                bk.consume("vector", lambda e, p, _d=dt_, _b=b: e.tensor_add(
                    be3eff[_b][:, ts(_d, 512)], p[0:1, :], be3[_b][:, ts(_d, 512)]))
            sem_b3, val_b3 = qb4.free, qb4.freed_target
            P.wait("tensor", sem_b3, val_b3)
            for g in range(ngrp):
                def egrp(g=g):
                    qb1.mm_group([(None, xmoeT[:, cc, ts(g, 128)], ew1[b][:, cc, :])
                                  for cc in range(8)] + [(None, onesf[:, :], be1[b][:, :])])
                    qb2.mm_group([(None, xmoeT[:, cc, ts(g, 128)], ew2[b][:, cc, :])
                                  for cc in range(8)] + [(None, onesf[:, :], be2[b][:, :])])
                    if prev_mulC[0] is not None:
                        P.waits(("scalar",), [prev_mulC[0]])
                    fsem, fval = qb1.consume("scalar", lambda e, p: e.activation(
                        sl1[:, :], p[:, :], AF.Silu))
                    P.wait("vector", fsem, fval)
                    if prev_sqC[0] is not None:
                        P.waits(("vector",), [prev_sqC[0]])
                    ptok = qb2.consume("vector", lambda e, p: e.tensor_mul(prodC[:, :], sl1[:, :], p[:, :]))
                    prev_mulC[0] = ptok
                    sd, vd = lnkC.emit(prodC[:, :], zbf[:, :], 512, g_rep=selG[:, :], in_tok=ptok)
                    prev_sqC[0] = lnkC.sq_tok
                    P.wait("tensor", sd, vd)
                    for c in range(4):
                        pt = qtA if c % 2 == 0 else qtB
                        pt.tp_into(zbf[:, ts(c, 128)], ident[:, :])
                        pt.consume("vector", lambda e, p, _c=c: e.tensor_copy(
                            zT[:, _c, :], p[:, 0:128]))
                    P.wait("tensor", qtA.free, qtA.freed_target)
                    P.wait("tensor", qtB.free, qtB.freed_target)
                    for dt_ in range(2):
                        bk = qb3 if dt_ == 0 else qb4
                        bk.mm_group([(None, zT[:, cc, :], ew3[b][:, cc, ts(dt_, 512)])
                                     for cc in range(4)]
                                    + [(None, onesf[:, :], be3eff[b][:, ts(dt_, 512)])])
                        if e_ == 0:
                            bk.consume("vector", lambda e, p, _d=dt_, _g=g: e.tensor_scalar_mul(
                                acc[0][:, _g, ts(_d, 512)], p[:, :], oh[:, _g, 0:1]))
                        else:
                            src_, dst_ = acc[(e_ + 1) % 2], acc[e_ % 2]
                            bk.consume("vector", lambda e, p, _d=dt_, _g=g, _s=src_, _t=dst_:
                                       e.scalar_tensor_tensor(
                                           _t[:, _g, ts(_d, 512)], p[:, :], oh[:, _g, e_:e_+1],
                                           _s[:, _g, ts(_d, 512)], ALU.mult, ALU.add))
                egrp(g)
            wu_tok[b] = (qb4.free, qb4.freed_target)
            if e_ + 2 < 8:
                wv[b] = load_expert(e_ + 2, b)
        expert(e_)

    if stage == "abe":
        sem_e, val_e = qb4.free, qb4.freed_target
        P.waits(("gpsimd",), [(sem_e, val_e)])
        mo.close()
        return
    accF = acc[7 % 2]
    s_corr = P.S("corr")
    corr_base = s_corr.n
    sct = DStream(P, "sct", "gpsimd", 2)
    gth2 = DStream(P, "gth2", "gpsimd", 2)
    shm_tok = None
    sc_tok = None
    for g in range(ngrp):
        def fix(g=g):
            nonlocal shm_tok, sc_tok
            if g >= 1:
                P.wait("gpsimd", s_corr, corr_base + 3 * (g - 1) + 3)
            if sc_tok is not None:
                P.waits(("vector",), [sc_tok])
            shm_tok = gth2.issue_gather(shm[:, :], a["shout_d"], idx_sb[:, g:g+1], None)
            P.waits(("vector",), [shm_tok])
            P.op("vector", lambda e, _g=g: e.tensor_scalar(
                w01[:, :], shw[:, _g:_g+1], -1.0, 0.5, ALU.add, ALU.mult))
            P.op("vector", lambda e, _g=g: e.tensor_scalar_mul(w02[:, :], expw[:, _g:_g+1], 0.5))
            P.op("vector", lambda e: e.tensor_scalar_mul(corr[:, :], shm[:, :], w01[:, :]),
                 sem=s_corr)
            P.op("vector", lambda e, _g=g: e.scalar_tensor_tensor(
                corr[:, :], accF[:, _g, :], w02[:, :], corr[:, :], ALU.mult, ALU.add),
                 sem=s_corr)
            P.op("vector", lambda e, _g=g: e.tensor_scalar_mul(
                corr[:, :], corr[:, :], mmask[:, _g:_g + 1]), sem=s_corr)
            P.wait("gpsimd", s_corr, s_corr.n)
            sc_tok = sct.issue_scatter(a["y"], idx_sb[:, g:g+1], corr[:, :], None, op=ALU.add)
        fix(g)
    P.waits(("gpsimd",), sct.tokens())
    mo.close()


def emit_ln_xlnT(nc, ctx, P, sb, a, xlnT, ident, lnk, tag, xnat_d=None, banks=None):
    xs = [sb(f"lnx{tag}{i}", [128, DM], F32) for i in range(2)]
    xln = sb(f"lnxo{tag}", [128, DM], BF16)
    if banks is None:
        ptA = PBank(P, ctx, nc, f"lptA{tag}", [128, 128], BF16)
        ptB = PBank(P, ctx, nc, f"lptB{tag}", [128, 128], BF16)
    else:
        ptA, ptB = banks
    ld_x = DStream(P, f"lldx{tag}", "sync", 2)
    st_xn = DStream(P, f"lstxn{tag}", "gpsimd", 2)
    s_xfree = P.S(f"lxf{tag}")
    sxf_base = s_xfree.n
    tp_end = [0, 0]
    xn_tok = None
    for s in range(NSTRIP):
        def strip(s=s):
            nonlocal xn_tok
            xb = xs[s % 2]
            if s >= 2:
                P.wait("sync", s_xfree, sxf_base + s - 1)
            tok = ld_x.issue(xb[:, :], a["x"][ts(s, 128), :])
            P.waits(("vector", "scalar"), [tok])
            if s >= 1:
                P.wait("vector", ptA.full, tp_end[0])
                P.wait("vector", ptB.full, tp_end[1])
                if xn_tok is not None:
                    P.waits(("vector",), [xn_tok])
            sd, vd = lnk.emit(xb[:, :], xln[:, :], DM)
            if xnat_d is not None:
                P.wait("gpsimd", sd, vd)
                xn_tok = st_xn.issue(xnat_d[ts(s, 128), :], xln[:, :])
            P.wait("tensor", sd, vd)
            for c in range(8):
                pt = ptA if c % 2 == 0 else ptB
                pt.tp_into(xln[:, ts(c, 128)], ident[:, :])
                pt.consume("vector", lambda e, p, _c=c: e.tensor_copy(
                    xlnT[:, _c, ts(s, 128)], p[:, 0:128]))
            P.op("vector", lambda e: e.nop(), sem=s_xfree, drain=False)
            tp_end[0], tp_end[1] = ptA.fills, ptB.fills
        strip(s)
    return ([(ptA.free, ptA.freed_target), (ptB.free, ptB.freed_target)]
            + st_xn.tokens())


def build_l1ffn():
    nc = bass.Bass("TRN2", target_bir_lowering=False)
    dr = lambda n, s, d, k="ExternalInput": nc.dram_tensor(n, s, d, kind=k).ap()
    a = dict(
        x=dr("x", [T, DM], F32),
        y=dr("y", [T, DM], F32, "ExternalOutput"),
        wr=dr("wr", [DM, 8], F32),
        wg=dr("wg", [8, DM, 2048], F32),
        wu=dr("wu", [8, DM, 2048], F32),
        wd=dr("wd", [8, 2048, DM], F32),
        onesf=dr("onesf", [1, 128], F32),
        onesbf=dr("onesbf", [1, 128], BF16),
        identbf=dr("identbf", [128, 128], BF16),
        identf=dr("identf", [128, 128], F32),
        iota8=dr("iota8", [128, 8], F32),
        accd=dr("accd", [T, DM], F32, "Internal"),
        fwd=dr("fwd", [8, T], BF16, "Internal"),
        scale=0.5,
    )
    ctx = ExitStack()
    P = Prog(nc, ctx)
    emit_l1ffn(nc, ctx, P, a)
    P.build()
    ctx.close()
    return nc


def emit_l1ffn(nc, ctx, P, a):
    sb = lambda n, s, d: ctx.enter_context(nc.sbuf_tensor(n, s, d))
    onesf = sb("onesf_sb", [1, 128], F32)
    onesbf = sb("onesbf_sb", [1, 128], BF16)
    ident = sb("ident_sb", [128, 128], BF16)
    identf = sb("identf_sb", [128, 128], F32)
    iota8 = sb("iota8_sb", [128, 8], F32)
    s_cst = P.S("cst")
    P.dma("sync", s_cst, onesf[:, :], a["onesf"])
    P.dma("sync", s_cst, onesbf[:, :], a["onesbf"])
    P.dma("sync", s_cst, ident[:, :], a["identbf"])
    P.dma("sync", s_cst, identf[:, :], a["identf"])
    P.dma("sync", s_cst, iota8[:, :], a["iota8"])
    P.waits(("tensor",), [(s_cst, s_cst.n)])

    xlnT = sb("xlnT", [128, 8, T], BF16)
    fwsl = [sb(f"fwsl{i}", [1, 512], BF16) for i in range(2)]
    lnscope = ExitStack()
    sbL = lambda n, s, d: lnscope.enter_context(nc.sbuf_tensor(n, s, d))
    lnk = LNKit(P, sbL)
    ln_toks = emit_ln_xlnT(nc, lnscope, P, sbL, a, xlnT, ident, lnk, tag="F")

    wrsb = sbL("wrsb", [128, 8, 8], BF16)
    probs = sbL("probs", [128, 8], F32)
    nmax = sbL("nmax", [128, 1], F32)
    sacc = sbL("sacc", [128, 1], F32)
    top8 = sbL("top8", [128, 8], F32)
    tidx = sbL("tidx", [128, 8], U32)
    fw8 = sbL("fw8", [8, T], BF16)
    i1f = sbL("i1f", [128, 1], F32)
    i2f = sbL("i2f", [128, 1], F32)
    nw1 = sbL("nw1", [128, 1], F32)
    oh1 = sbL("oh1", [128, 8], F32)
    fw = sbL("fw", [128, 8], F32)
    s_ldr = P.S("ldr", sw=True)
    P.dma("gpsimd", s_ldr, wrsb[:, :, :], a["wr"].rearrange("(c p) h -> p c h", p=128))
    P.waits(("tensor",), [(s_ldr, s_ldr.n)])
    P.waits(("tensor",), ln_toks[0:2])
    rb = PBank(P, lnscope, nc, "rb", [128, 512])
    rt = PBank(P, lnscope, nc, "rt", [128, 128], F32)
    s_rv = P.S("rv")
    s_ra = P.S("ra")
    for s in range(NSTRIP):
        def rstrip(s=s):
            rb.mm_group([(rb.ap[:, 0:8], xlnT[:, cc, ts(s, 128)], wrsb[:, cc, :])
                         for cc in range(8)])
            tok1 = rb.consume("vector", lambda e, p: e.reduce_max(nmax[:, :], p[:, 0:8], axis=AXX))
            P.op("vector", lambda e: e.tensor_scalar_mul(nmax[:, :], nmax[:, :], -1.0), sem=s_rv)
            P.waits(("scalar",), [tok1, (s_rv, s_rv.n)])
            etok = rb.consume("scalar", lambda e, p: e.activation(
                probs[:, :], p[0:128, 0:8], AF.Exp, bias=nmax[:, :], accum_out=sacc[:, :]))
            P.waits(("vector",), [etok])
            P.op("vector", lambda e: e.reciprocal(sacc[:, :], sacc[:, :]))
            P.op("vector", lambda e: e.tensor_scalar_mul(probs[:, :], probs[:, :], sacc[:, :]))
            P.op("vector", lambda e: e.max(out=top8[:, :], in_=probs[:, :]))
            P.op("vector", lambda e: e.max_index(tidx[:, :], top8[:, :], probs[:, :]))
            P.op("vector", lambda e: e.tensor_copy(i1f[:, :], tidx[:, 0:1]))
            P.op("vector", lambda e: e.tensor_copy(i2f[:, :], tidx[:, 1:2]))
            P.op("vector", lambda e: e.tensor_sub(nmax[:, :], top8[:, 0:1], top8[:, 1:2]), sem=s_rv)
            P.wait("scalar", s_rv, s_rv.n)
            P.op("scalar", lambda e: e.activation(nw1[:, :], nmax[:, :], AF.Sigmoid), sem=s_ra)
            P.wait("vector", s_ra, s_ra.n)
            P.op("vector", lambda e: e.tensor_tensor(
                oh1[:, :], iota8[:, :], i1f[:, :].to_broadcast([128, 8]), ALU.is_equal))
            P.op("vector", lambda e: e.tensor_scalar_mul(fw[:, :], oh1[:, :], nw1[:, :]))
            P.op("vector", lambda e: e.tensor_tensor(
                oh1[:, :], iota8[:, :], i2f[:, :].to_broadcast([128, 8]), ALU.is_equal))
            P.op("vector", lambda e: e.tensor_scalar(nw1[:, :], nw1[:, :], -1.0, 1.0,
                                                     ALU.mult, ALU.add))
            P.op("vector", lambda e: e.scalar_tensor_tensor(
                fw[:, :], oh1[:, :], nw1[:, :], fw[:, :], ALU.mult, ALU.add), sem=s_rv)
            P.wait("tensor", s_rv, s_rv.n)
            rt.tp_into(fw[:, 0:8], identf[:, :])
            rt.consume("vector", lambda e, p, _s=s: e.tensor_copy(
                fw8[:, ts(_s, 128)], p[0:8, 0:128]))
        rstrip(s)
    s_rdone = P.S("rdone")
    P.op("vector", lambda e: e.nop(), sem=s_rdone, drain=False)
    P.waits(("tensor", "scalar", "gpsimd", "sync"), [(s_rdone, s_rdone.n)])
    s_fwb = P.S("fwb", sw=True)
    P.dma("gpsimd", s_fwb, a["fwd"], fw8[:, :])
    P.waits(("gpsimd", "sync"), [(s_fwb, s_fwb.n)])

    lnscope.close()
    wg_sb = sb("wg_sb", [128, 8, 2048], BF16)
    wu_sb = sb("wu_sb", [128, 8, 2048], BF16)
    wd_sb = sb("wd_sb", [128, 16, DM], BF16)
    wrep = sb("wrep", [128, 512], F32)
    sg = sb("sg", [128, 512], BF16)
    usc = sb("usc", [128, 512], BF16)
    fuseT = sb("fuseT", [128, 16, 512], BF16)
    accs = [sb(f"accs{i}", [128, DM], F32) for i in range(2)]
    xts = [sb(f"xts{i}", [128, DM], F32) for i in range(2)]

    pg = PBank(P, ctx, nc, "pg", [128, 512])
    pu = PBank(P, ctx, nc, "pu", [128, 512])
    pg2 = PBank(P, ctx, nc, "pg2", [128, 512])
    pu2 = PBank(P, ctx, nc, "pu2", [128, 512])
    pd1 = PBank(P, ctx, nc, "pd1", [128, 512])
    pd2 = PBank(P, ctx, nc, "pd2", [128, 512])
    prp = PBank(P, ctx, nc, "prp", [128, 512])

    s_wle = P.S("wle", sw=True)
    ld_acc = DStream(P, "ldacc", "sync", 2)
    st_acc = DStream(P, "stacc", "gpsimd", 2)
    ld_xf = DStream(P, "ldxf", "sync", 2)
    ld_fw = DStream(P, "ldfw", "sync", 2)
    NT = 8
    acc_st_tok = {}
    buf_st_tok = [None, None]
    xbuf_free = [None, None]
    prev_fuse = [None]
    wfree_tok = [None]
    fsl_free = [None, None]
    s_av = P.S("av")
    subctr = [0]
    for e_ in range(8):
        def expert(e_=e_):
            if wfree_tok[0] is not None:
                P.waits(("gpsimd",), [wfree_tok[0]])
            P.dma("gpsimd", s_wle, wg_sb[:, :, :], a["wg"][e_].rearrange("(c p) f -> p c f", p=128))
            P.dma("gpsimd", s_wle, wu_sb[:, :, :], a["wu"][e_].rearrange("(c p) f -> p c f", p=128))
            P.dma("gpsimd", s_wle, wd_sb[:, :, :], a["wd"][e_].rearrange("(c p) f -> p c f", p=128))
            vw = s_wle.n
            P.waits(("tensor",), [(s_wle, vw)])
            for tt in range(NT):
                def ttile(tt=tt):
                    fsl = fwsl[tt % 2]
                    if fsl_free[tt % 2] is not None:
                        P.waits(("sync",), [fsl_free[tt % 2]])
                    ftok = ld_fw.issue(fsl[:, :], a["fwd"][e_:e_+1, ts(tt, 512)])
                    P.waits(("tensor",), [ftok])
                    prp.mm_group([(None, onesbf[:, :], fsl[:, :])])
                    fsl_free[tt % 2] = (prp.full, prp.fills)
                    prp.consume("vector", lambda e, p: e.tensor_copy(wrep[:, :], p[:, :]))
                    P.wait("tensor", prp.free, prp.freed_target)
                    for ft in range(16):
                        def ftile(ft=ft):
                            bg = pg if ft % 2 == 0 else pg2
                            bu = pu if ft % 2 == 0 else pu2
                            bg.mm_group([(None, wg_sb[:, cc, ts(ft, 128)], xlnT[:, cc, ts(tt, 512)])
                                         for cc in range(8)])
                            bu.mm_group([(None, wu_sb[:, cc, ts(ft, 128)], xlnT[:, cc, ts(tt, 512)])
                                         for cc in range(8)])
                            if prev_fuse[0] is not None:
                                P.waits(("scalar",), [prev_fuse[0]])
                            bg.consume("scalar", lambda e, p: e.activation(sg[:, :], p[:, :], AF.Silu))
                            bu.consume("vector", lambda e, p: e.tensor_mul(usc[:, :], p[:, :], wrep[:, :]))
                            P.waits(("vector",), [(bg.free, bg.freed_target)])
                            s_fu = P.S("fu")
                            prev_fuse[0] = (s_fu, P.op("vector", lambda e, _f=ft: e.tensor_tensor(
                                fuseT[:, _f, :], sg[:, :], usc[:, :], ALU.mult), sem=s_fu))
                        ftile(ft)
                    P.wait("tensor", prev_fuse[0][0], prev_fuse[0][1])
                    for sub in range(4):
                        def dsub(sub=sub):
                            row0 = tt * 512 + sub * 128
                            bi = subctr[0] % 2
                            subctr[0] += 1
                            ab_ = accs[bi]
                            atok = None
                            xtok = None
                            if e_ > 0:
                                if acc_st_tok.get((tt, sub)) is not None:
                                    P.waits(("sync",), [acc_st_tok[(tt, sub)]])
                                if buf_st_tok[bi] is not None:
                                    P.waits(("sync",), [buf_st_tok[bi]])
                                atok = ld_acc.issue(ab_[:, :], a["accd"][row0:row0+128, :])
                            if e_ == 7:
                                if xbuf_free[bi] is not None:
                                    P.waits(("sync",), [xbuf_free[bi]])
                                xtok = ld_xf.issue(xts[bi][:, :], a["x"][row0:row0+128, :])
                            if atok is not None:
                                P.waits(("vector",), [atok])
                            if e_ == 0 and buf_st_tok[bi] is not None:
                                P.waits(("vector",), [buf_st_tok[bi]])
                            if xtok is not None:
                                P.waits(("vector",), [xtok])
                            for dh in range(2):
                                bk = pd1 if dh == 0 else pd2
                                bk.mm_group([(None, fuseT[:, ft2, ts(sub, 128)],
                                              wd_sb[:, ft2, ts(dh, 512)]) for ft2 in range(16)])
                                if e_ == 0:
                                    bk.consume("vector", lambda e, p, _d=dh, _a=ab_:
                                               e.tensor_copy(_a[:, ts(_d, 512)], p[:, :]))
                                elif e_ < 7:
                                    bk.consume("vector", lambda e, p, _d=dh, _a=ab_:
                                               e.tensor_add(_a[:, ts(_d, 512)], p[:, :],
                                                            _a[:, ts(_d, 512)]))
                                else:
                                    bk.consume("vector", lambda e, p, _d=dh, _a=ab_:
                                               e.tensor_add(_a[:, ts(_d, 512)], p[:, :],
                                                            _a[:, ts(_d, 512)]))
                                    P.op("vector", lambda e, _d=dh, _a=ab_, _x=xts[bi]:
                                         e.scalar_tensor_tensor(
                                             _a[:, ts(_d, 512)], _a[:, ts(_d, 512)], a["scale"],
                                             _x[:, ts(_d, 512)], ALU.mult, ALU.add), sem=s_av)
                            P.wait("gpsimd", pd1.free, pd1.freed_target)
                            P.wait("gpsimd", pd2.free, pd2.freed_target)
                            if e_ == 7:
                                P.wait("gpsimd", s_av, s_av.n)
                                xbuf_free[bi] = (s_av, s_av.n)
                                tok = st_acc.issue(a["y"][row0:row0+128, :], ab_[:, :])
                            else:
                                tok = st_acc.issue(a["accd"][row0:row0+128, :], ab_[:, :])
                            acc_st_tok[(tt, sub)] = tok
                            buf_st_tok[bi] = tok
                        dsub(sub)
                ttile(tt)
            wfree_tok[0] = (pd2.free, pd2.freed_target)
        expert(e_)
    P.waits(("gpsimd",), st_acc.tokens())


CE = 1536


def build_l1ffn_sparse():
    nc = bass.Bass("TRN2", target_bir_lowering=False)
    dr = lambda n, s, d, k="ExternalInput": nc.dram_tensor(n, s, d, kind=k).ap()
    a = dict(
        x=dr("x", [T, DM], F32),
        y=dr("y", [T, DM], F32, "ExternalOutput"),
        wr=dr("wr", [DM, 8], F32),
        wg=dr("wg", [8, DM, 2048], F32),
        wu=dr("wu", [8, DM, 2048], F32),
        wd=dr("wd", [8, 2048, DM], F32),
        onesf=dr("onesf", [1, 128], F32),
        onesbf=dr("onesbf", [1, 128], BF16),
        identbf=dr("identbf", [128, 128], BF16),
        identf=dr("identf", [128, 128], F32),
        iota8=dr("iota8", [128, 8], F32),
        trilow=dr("trilow", [128, 128], BF16),
        pidx=dr("pidx", [128, 1], F32),
        xnat_d=dr("xnat_d", [T, DM], BF16, "Internal"),
        tix_d=dr("tix_d", [8 * CE, 2], F32, "Internal"),
        out_d=dr("out_d", [8 * CE, DM], F32, "Internal"),
        ptd=dr("ptd", [T, 4], F32, "Internal"),
        scale=0.5,
    )
    ctx = ExitStack()
    P = Prog(nc, ctx)
    emit_l1ffn_sparse(nc, ctx, P, a)
    P.build()
    ctx.close()
    return nc


def emit_l1ffn_sparse(nc, ctx, P, a):
    sb = lambda n, s, d: ctx.enter_context(nc.sbuf_tensor(n, s, d))
    onesbf = sb("onesbf_sb", [1, 128], BF16)
    ident = sb("ident_sb", [128, 128], BF16)
    identf = sb("identf_sb", [128, 128], F32)
    iota8 = sb("iota8_sb", [128, 8], F32)
    trilow = sb("trilow_sb", [128, 128], BF16)
    pidx = sb("pidx_sb", [128, 1], F32)
    fullon = sb("fullon_sb", [128, 128], BF16)
    zini = sb("zini_sb", [128, 192], F32)
    s_cst = P.S("cst")
    P.dma("sync", s_cst, onesbf[:, :], a["onesbf"])
    P.dma("sync", s_cst, ident[:, :], a["identbf"])
    P.dma("sync", s_cst, identf[:, :], a["identf"])
    P.dma("sync", s_cst, iota8[:, :], a["iota8"])
    P.dma("sync", s_cst, trilow[:, :], a["trilow"])
    P.dma("sync", s_cst, pidx[:, :], a["pidx"])
    v_cst = s_cst.n
    P.waits(("tensor", "vector"), [(s_cst, v_cst)])
    s_mz = P.S("mz2")
    P.op("vector", lambda e: e.memset(fullon[:, :], 1.0))
    P.op("vector", lambda e: e.memset(zini[:, :], 0.0), sem=s_mz)
    P.waits(("gpsimd",), [(s_mz, s_mz.n)])
    s_tixi = P.S("tixi", sw=True)
    P.dma("gpsimd", s_tixi, a["tix_d"].rearrange("(g p) c -> p g c", p=128),
          zini[:, :].rearrange("p (g c) -> p g c", c=2))
    v_tixi = s_tixi.n

    xlnT = sb("xlnT", [128, 8, T], BF16)
    lnscope = ExitStack()
    sbL = lambda n, s, d: lnscope.enter_context(nc.sbuf_tensor(n, s, d))
    lnk = LNKit(P, sbL)
    ln_toks = emit_ln_xlnT(nc, lnscope, P, sbL, a, xlnT, ident, lnk, tag="F",
                           xnat_d=a["xnat_d"])

    wrsb = sbL("wrsb", [128, 8, 8], BF16)
    probs = sbL("probs", [128, 8], F32)
    nmax = sbL("nmax", [128, 1], F32)
    sacc = sbL("sacc", [128, 1], F32)
    top8 = sbL("top8", [128, 8], F32)
    tidx = sbL("tidx", [128, 8], U32)
    i1f = sbL("i1f", [128, 1], F32)
    i2f = sbL("i2f", [128, 1], F32)
    nw1 = sbL("nw1", [128, 1], F32)
    nw2 = sbL("nw2", [128, 1], F32)
    oh1f = sbL("oh1f", [128, 8], F32)
    oh2f = sbL("oh2f", [128, 8], F32)
    ohcb = sbL("ohcb", [128, 8], BF16)
    offs = sbL("offs", [128, 8], F32)
    rk = sbL("rk", [128, 8], F32)
    base8 = sbL("base8", [128, 8], F32)
    t8 = sbL("t8", [128, 8], F32)
    pos1 = sbL("pos1", [128, 1], F32)
    pos2 = sbL("pos2", [128, 1], F32)
    lim1 = sbL("lim1", [128, 1], F32)
    cemo = sbL("cemo", [128, 1], F32)
    tval = sbL("tval", [128, 1], F32)
    val1 = sbL("val1", [128, 2], F32)
    val2 = sbL("val2", [128, 2], F32)
    pri1 = sbL("pri1", [128, 1], I32)
    pri2 = sbL("pri2", [128, 1], I32)
    ptrow = sbL("ptrow", [128, 4], F32)
    s_ldr = P.S("ldr", sw=True)
    P.dma("gpsimd", s_ldr, wrsb[:, :, :], a["wr"].rearrange("(c p) h -> p c h", p=128))
    P.waits(("tensor",), [(s_ldr, s_ldr.n)])
    P.waits(("tensor",), ln_toks[0:2])
    P.op("vector", lambda e: e.memset(offs[:, :], 0.0))
    P.op("vector", lambda e: e.memset(cemo[:, :], float(CE - 1)))
    rb = PBank(P, lnscope, nc, "rb", [128, 512])
    pRank = PBank(P, lnscope, nc, "pRank", [128, 8], F32)
    pCnt = PBank(P, lnscope, nc, "pCnt", [128, 8], F32)
    cnt8 = sbL("cnt8", [128, 8], F32)
    s_rv = P.S("rv")
    s_ra = P.S("ra")
    sct1 = DStream(P, "sct1", "gpsimd", 2)
    st_pt = DStream(P, "stpt", "gpsimd", 2)
    ev_tok = [None]
    sc_guard = [None, None]
    pt_guard = [None]
    for s in range(NSTRIP):
        def rstrip(s=s):
            rb.mm_group([(rb.ap[:, 0:8], xlnT[:, cc, ts(s, 128)], wrsb[:, cc, :])
                         for cc in range(8)])
            tok1 = rb.consume("vector", lambda e, p: e.reduce_max(nmax[:, :], p[:, 0:8], axis=AXX))
            P.op("vector", lambda e: e.tensor_scalar_mul(nmax[:, :], nmax[:, :], -1.0), sem=s_rv)
            P.waits(("scalar",), [tok1, (s_rv, s_rv.n)])
            etok = rb.consume("scalar", lambda e, p: e.activation(
                probs[:, :], p[0:128, 0:8], AF.Exp, bias=nmax[:, :], accum_out=sacc[:, :]))
            P.waits(("vector",), [etok])
            P.op("vector", lambda e: e.reciprocal(sacc[:, :], sacc[:, :]))
            P.op("vector", lambda e: e.tensor_scalar_mul(probs[:, :], probs[:, :], sacc[:, :]))
            P.op("vector", lambda e: e.max(out=top8[:, :], in_=probs[:, :]))
            P.op("vector", lambda e: e.max_index(tidx[:, :], top8[:, :], probs[:, :]))
            P.op("vector", lambda e: e.tensor_copy(i1f[:, :], tidx[:, 0:1]))
            P.op("vector", lambda e: e.tensor_copy(i2f[:, :], tidx[:, 1:2]))
            P.op("vector", lambda e: e.tensor_sub(nmax[:, :], top8[:, 0:1], top8[:, 1:2]), sem=s_rv)
            P.wait("scalar", s_rv, s_rv.n)
            P.op("scalar", lambda e: e.activation(nw1[:, :], nmax[:, :], AF.Sigmoid), sem=s_ra)
            P.wait("vector", s_ra, s_ra.n)
            P.op("vector", lambda e: e.tensor_scalar(nw2[:, :], nw1[:, :], -1.0, 1.0,
                                                     ALU.mult, ALU.add))
            P.op("vector", lambda e: e.tensor_tensor(
                oh1f[:, :], iota8[:, :], i1f[:, :].to_broadcast([128, 8]), ALU.is_equal))
            P.op("vector", lambda e: e.tensor_tensor(
                oh2f[:, :], iota8[:, :], i2f[:, :].to_broadcast([128, 8]), ALU.is_equal))
            otok = P.opt("vector", lambda e: e.tensor_add(ohcb[:, :], oh1f[:, :], oh2f[:, :]))
            P.waits(("tensor",), [otok])
            pRank.mm_group([(pRank.ap[:, 0:8], trilow[:, :], ohcb[:, :])])
            pCnt.mm_group([(pCnt.ap[:, 0:8], fullon[:, :], ohcb[:, :])])
            pRank.consume("vector", lambda e, p: e.tensor_copy(rk[:, :], p[:, 0:8]))
            P.op("vector", lambda e: e.tensor_add(base8[:, :], rk[:, :], offs[:, :]))
            pCnt.consume("vector", lambda e, p: e.tensor_copy(cnt8[:, :], p[:, 0:8]))
            P.op("vector", lambda e: e.tensor_add(offs[:, :], offs[:, :], cnt8[:, :]))
            P.op("vector", lambda e, _s=s: e.tensor_scalar(
                tval[:, :], pidx[:, :], 1.0, float(128 * _s), ALU.mult, ALU.add))
            for sl, (ohf, nw, pos, pri, val) in enumerate(
                    [(oh1f, nw1, pos1, pri1, val1), (oh2f, nw2, pos2, pri2, val2)]):
                def slot(ohf=ohf, nw=nw, pos=pos, pri=pri, val=val, sl=sl):
                    if_ = i1f if sl == 0 else i2f
                    P.op("vector", lambda e: e.tensor_mul(t8[:, :], ohf[:, :], base8[:, :]))
                    P.op("vector", lambda e: e.reduce_sum(pos[:, :], t8[:, :], axis=AXX))
                    P.op("vector", lambda e: e.scalar_tensor_tensor(
                        pos[:, :], if_[:, :], float(CE), pos[:, :], ALU.mult, ALU.add))
                    P.op("vector", lambda e: e.scalar_tensor_tensor(
                        lim1[:, :], if_[:, :], float(CE), cemo[:, :], ALU.mult, ALU.add))
                    P.op("vector", lambda e: e.tensor_tensor(
                        pos[:, :], pos[:, :], lim1[:, :], ALU.min))
                    if sc_guard[sl] is not None:
                        P.waits(("vector",), [sc_guard[sl]])
                    P.op("vector", lambda e: e.tensor_copy(pri[:, :], pos[:, :]))
                    P.op("vector", lambda e: e.tensor_copy(val[:, 0:1], tval[:, :]))
                    vtok = P.opt("vector", lambda e: e.tensor_copy(val[:, 1:2], nw[:, :]))
                    P.waits(("gpsimd",), [vtok])
                    if s == 0 and sl == 0:
                        P.waits(("gpsimd",), [(s_tixi, v_tixi)])
                    sc_guard[sl] = sct1.issue_scatter(a["tix_d"], pri[:, :], val[:, :], None)
                slot()
            if pt_guard[0] is not None:
                P.waits(("vector",), [pt_guard[0]])
            P.op("vector", lambda e: e.tensor_copy(ptrow[:, 0:1], pos1[:, :]))
            P.op("vector", lambda e: e.tensor_copy(ptrow[:, 1:2], pos2[:, :]))
            P.op("vector", lambda e: e.tensor_copy(ptrow[:, 2:3], nw1[:, :]))
            ptok = P.opt("vector", lambda e: e.tensor_copy(ptrow[:, 3:4], nw2[:, :]))
            P.waits(("gpsimd",), [ptok])
            pt_guard[0] = st_pt.issue(a["ptd"][ts(s, 128), :], ptrow[:, :])
        rstrip(s)
    P.waits(("sync", "gpsimd", "vector", "tensor", "scalar"),
            [sc_guard[0], sc_guard[1], pt_guard[0]])
    P.waits(("gpsimd", "sync"), ln_toks[2:])
    lnscope.close()

    exsc = ExitStack()
    sbE = lambda n, s, d: exsc.enter_context(nc.sbuf_tensor(n, s, d))
    wg_sb = sbE("wg_sb", [128, 8, 2048], BF16)
    wu_sb = sbE("wu_sb", [128, 8, 2048], BF16)
    wd_sb = sbE("wd_sb", [128, 16, DM], BF16)
    tixs = [sbE(f"tixs{i}", [128, 2], F32) for i in range(2)]
    ixq = [sbE(f"ixq{i}", [128, 1], I32) for i in range(2)]
    xg = [sbE(f"xg{i}", [128, DM], BF16) for i in range(2)]
    xgT = sbE("xgT", [128, 8, 512], BF16)
    sg2 = [sbE(f"sg2_{i}", [128, 512], BF16) for i in range(2)]
    fuseT = sbE("fuseT", [128, 16, 512], BF16)
    ob = [sbE(f"ob{i}", [128, DM], F32) for i in range(2)]

    pg = PBank(P, exsc, nc, "pg", [128, 512])
    pu = PBank(P, exsc, nc, "pu", [128, 512])
    pg2 = PBank(P, exsc, nc, "pg2", [128, 512])
    pu2 = PBank(P, exsc, nc, "pu2", [128, 512])
    pd1 = PBank(P, exsc, nc, "pd1", [128, 512])
    pd2 = PBank(P, exsc, nc, "pd2", [128, 512])
    pTx = [PBank(P, exsc, nc, f"pTx{i}", [128, 128], BF16) for i in range(2)]

    s_wle = P.S("wle", sw=True)
    ld_tix = DStream(P, "ldtix", "sync", 2)
    gthx = DStream(P, "gthx", "gpsimd", 2)
    st_out = DStream(P, "stout", "gpsimd", 2)
    wfree_tok = [None]
    tix_free = [None, None]
    xg_free = [None, None]
    sg_free = [None, None]
    ob_free = [None, None]
    xgT_guard = [None]
    fuse_guard = [None]
    prev_fuse = [None]
    NTT = CE // 512
    for e_ in range(8):
        def expert(e_=e_):
            if wfree_tok[0] is not None:
                P.waits(("gpsimd",), [wfree_tok[0]])
            P.dma("gpsimd", s_wle, wg_sb[:, :, :], a["wg"][e_].rearrange("(c p) f -> p c f", p=128))
            P.dma("gpsimd", s_wle, wu_sb[:, :, :], a["wu"][e_].rearrange("(c p) f -> p c f", p=128))
            P.dma("gpsimd", s_wle, wd_sb[:, :, :], a["wd"][e_].rearrange("(c p) f -> p c f", p=128))
            vw = s_wle.n
            P.waits(("tensor",), [(s_wle, vw)])
            for tt in range(NTT):
                def ttile(tt=tt):
                    if xgT_guard[0] is not None:
                        P.waits(("vector",), [xgT_guard[0]])
                    for q in range(4):
                        def chunk(q=q):
                            bi = q % 2
                            row0 = e_ * CE + tt * 512 + q * 128
                            if tix_free[bi] is not None:
                                P.waits(("sync",), [tix_free[bi]])
                            ttok = ld_tix.issue(tixs[bi][:, :], a["tix_d"][row0:row0 + 128, :])
                            P.waits(("vector",), [ttok])
                            ctok = P.opt("vector", lambda e: e.tensor_copy(
                                ixq[bi][:, :], tixs[bi][:, 0:1]))
                            tix_free[bi] = ctok
                            P.waits(("gpsimd",), [ctok])
                            if xg_free[bi] is not None:
                                P.waits(("gpsimd",), [xg_free[bi]])
                            gtok = gthx.issue_gather(xg[bi][:, :], a["xnat_d"],
                                                     ixq[bi][:, :], None)
                            P.waits(("tensor",), [gtok])
                            for c in range(8):
                                pt_ = pTx[c % 2]
                                pt_.tp_into(xg[bi][:, ts(c, 128)], ident[:, :])
                                pt_.consume("vector", lambda e, p, _c=c, _q=q: e.tensor_copy(
                                    xgT[:, _c, ts(_q, 128)], p[:, 0:128]))
                            xg_free[bi] = (pTx[1].full, pTx[1].fills)
                        chunk(q)
                    xv = P.opt("vector", lambda e: e.nop(), drain=False)
                    P.waits(("tensor",), [xv])
                    if fuse_guard[0] is not None:
                        P.waits(("vector", "scalar"), [fuse_guard[0]])
                    for fc in range(16):
                        def ftile(fc=fc):
                            bg = pg if fc % 2 == 0 else pg2
                            bu = pu if fc % 2 == 0 else pu2
                            sgb = sg2[fc % 2]
                            bg.mm_group([(None, wg_sb[:, cc, ts(fc, 128)], xgT[:, cc, :])
                                         for cc in range(8)])
                            bu.mm_group([(None, wu_sb[:, cc, ts(fc, 128)], xgT[:, cc, :])
                                         for cc in range(8)])
                            if sg_free[fc % 2] is not None:
                                P.waits(("scalar",), [sg_free[fc % 2]])
                            stok = bg.consume("scalar", lambda e, p, _b=sgb: e.activation(
                                _b[:, :], p[:, :], AF.Silu))
                            P.waits(("vector",), [stok])
                            mtok = bu.consume("vector", lambda e, p, _f=fc, _b=sgb: e.tensor_mul(
                                fuseT[:, _f, :], p[:, :], _b[:, :]))
                            sg_free[fc % 2] = mtok
                            prev_fuse[0] = mtok
                        ftile(fc)
                    P.waits(("tensor",), [prev_fuse[0]])
                    for q2 in range(4):
                        def dchunk(q2=q2):
                            bi = q2 % 2
                            row0 = e_ * CE + tt * 512 + q2 * 128
                            if ob_free[bi] is not None:
                                P.waits(("vector",), [ob_free[bi]])
                            for dh in range(2):
                                bk = pd1 if dh == 0 else pd2
                                bk.mm_group([(None, fuseT[:, ft2, ts(q2, 128)],
                                              wd_sb[:, ft2, ts(dh, 512)]) for ft2 in range(16)])
                                bk.consume("vector", lambda e, p, _d=dh, _o=ob[bi]:
                                           e.tensor_copy(_o[:, ts(_d, 512)], p[:, :]))
                            ov = P.opt("vector", lambda e: e.nop(), drain=False)
                            P.waits(("gpsimd",), [ov])
                            ob_free[bi] = st_out.issue(a["out_d"][row0:row0 + 128, :],
                                                       ob[bi][:, :])
                        dchunk(q2)
                    fuse_guard[0] = (pd2.free, pd2.freed_target)
                    xgT_guard[0] = (pu2.full, pu2.fills)
                ttile(tt)
            wfree_tok[0] = (pd2.free, pd2.freed_target)
        expert(e_)
    P.waits(("sync", "vector", "gpsimd"), st_out.tokens())
    exsc.close()

    fsc = ExitStack()
    sbF = lambda n, s, d: fsc.enter_context(nc.sbuf_tensor(n, s, d))
    xs2 = [sbF(f"xs2_{i}", [128, DM], F32) for i in range(2)]
    g1b = [sbF(f"g1b{i}", [128, DM], F32) for i in range(2)]
    g2b = [sbF(f"g2b{i}", [128, DM], F32) for i in range(2)]
    ptl = [sbF(f"ptl{i}", [128, 4], F32) for i in range(2)]
    pi1 = [sbF(f"pi1_{i}", [128, 1], I32) for i in range(2)]
    pi2 = [sbF(f"pi2_{i}", [128, 1], I32) for i in range(2)]
    ld_x2 = DStream(P, "ldx2", "sync", 2)
    ld_pt = DStream(P, "ldpt", "sync", 2)
    gth1 = DStream(P, "gth1", "gpsimd", 2)
    gth2f = DStream(P, "gth2f", "gpsimd", 2)
    st_y = DStream(P, "sty", "gpsimd", 2)
    y_tok = [None, None]
    for s in range(NSTRIP):
        def fstrip(s=s):
            bi = s % 2
            if y_tok[bi] is not None:
                P.waits(("sync", "vector", "gpsimd"), [y_tok[bi]])
            xtok = ld_x2.issue(xs2[bi][:, :], a["x"][ts(s, 128), :])
            ptok = ld_pt.issue(ptl[bi][:, :], a["ptd"][ts(s, 128), :])
            P.waits(("vector",), [ptok])
            c1 = P.opt("vector", lambda e: e.tensor_copy(pi1[bi][:, :], ptl[bi][:, 0:1]))
            c2 = P.opt("vector", lambda e: e.tensor_copy(pi2[bi][:, :], ptl[bi][:, 1:2]))
            P.waits(("gpsimd",), [c1, c2])
            g1t = gth1.issue_gather(g1b[bi][:, :], a["out_d"], pi1[bi][:, :], None)
            g2t = gth2f.issue_gather(g2b[bi][:, :], a["out_d"], pi2[bi][:, :], None)
            P.waits(("vector",), [xtok, g1t, g2t])
            P.op("vector", lambda e: e.tensor_scalar_mul(
                g1b[bi][:, :], g1b[bi][:, :], ptl[bi][:, 2:3]))
            P.op("vector", lambda e: e.scalar_tensor_tensor(
                g1b[bi][:, :], g2b[bi][:, :], ptl[bi][:, 3:4], g1b[bi][:, :],
                ALU.mult, ALU.add))
            ytok = P.opt("vector", lambda e: e.scalar_tensor_tensor(
                g1b[bi][:, :], g1b[bi][:, :], a["scale"], xs2[bi][:, :],
                ALU.mult, ALU.add))
            P.waits(("gpsimd",), [ytok])
            y_tok[bi] = st_y.issue(a["y"][ts(s, 128), :], g1b[bi][:, :])
        fstrip(s)
    P.waits(("gpsimd",), st_y.tokens())
    fsc.close()


LAT_SLOTS = [(J, K) for K in range(8) for J in range(4 * K, 32)]
LAT_SLOT = {jk: i for i, jk in enumerate(LAT_SLOTS)}


def build_l0attn():
    nc = bass.Bass("TRN2", target_bir_lowering=False)
    dr = lambda n, s, d, k="ExternalInput": nc.dram_tensor(n, s, d, kind=k).ap()
    a = dict(
        x=dr("x", [T, DM], F32),
        y=dr("y", [T, DM], F32, "ExternalOutput"),
        wqkvh=dr("wqkvh", [8, DM, 320], F32),
        wvc=dr("wvc", [8, DM, 64], F32),
        woutT=dr("woutT", [512, DM], F32),
        identbf=dr("identbf", [128, 128], BF16),
        t1t_mask=dr("t1t_mask", [128, 2, 256], BF16),
        lat_mask=dr("lat_mask", [128, 4, 512], BF16),
        otd=nc.dram_tensor("otd", [4, 128, T], BF16, kind="Internal").ap(),
        causal_mask=dr("causal_mask", [128, 4, 512], BF16),
        lat_d=dr("lat_d", [len(LAT_SLOTS) * 128, 512], BF16, "Internal"),
        scale=1.0,
    )
    ctx = ExitStack()
    P = Prog(nc, ctx)
    emit_l0attn(nc, ctx, P, a)
    P.build()
    ctx.close()
    return nc


def emit_l0attn(nc, ctx, P, a):
    sb = lambda n, s, d: ctx.enter_context(nc.sbuf_tensor(n, s, d))
    ident = sb("ident_sb", [128, 128], BF16)
    t1m = sb("t1m_sb", [128, 2, 256], BF16)
    latm = sb("latm_sb", [128, 4, 512], BF16)
    caum = sb("caum_sb", [128, 4, 512], BF16)
    s_cst = P.S("cst")
    P.dma("sync", s_cst, ident[:, :], a["identbf"])
    P.dma("sync", s_cst, t1m[:, :, :], a["t1t_mask"])
    P.dma("sync", s_cst, latm[:, :, :], a["lat_mask"])
    P.dma("sync", s_cst, caum[:, :, :], a["causal_mask"])
    P.waits(("tensor", "vector", "scalar"), [(s_cst, s_cst.n)])

    xlnT = sb("xlnT", [128, 8, T], BF16)
    lnscope = ExitStack()
    sbL = lambda n, s, d: lnscope.enter_context(nc.sbuf_tensor(n, s, d))
    lnk = LNKit(P, sbL)
    ln_toks = emit_ln_xlnT(nc, lnscope, P, sbL, a, xlnT, ident, lnk, tag="A")
    P.waits(("tensor",), ln_toks[0:2])
    lnscope.close()

    vuT = sb("vuT", [64, T], BF16)
    qcsT = sb("qcsT", [64, T], BF16)
    kcT = sb("kcT", [64, T], BF16)
    vc_sb = sb("vc_sb", [128, 32, 64], BF16)
    otsb = [sb(f"otsb{i}", [64, 128], BF16) for i in range(2)]
    t1tb = sb("t1tb", [128, 32, 256], BF16)
    scoreb = [sb(f"scoreb{i}", [128, T], F32) for i in range(2)]
    pbuf = sb("pbuf", [128, T], BF16)
    latc = [sb(f"latc{i}", [128, 4, 512], BF16) for i in range(2)]
    latw = sb("latw", [128, 512], BF16)
    silu_s = [sb(f"silu_s{i}", [128, 512], F32) for i in range(2)]
    negmax = sb("negmaxA", [128, 1], F32)
    rl = sb("rlA", [128, 1], F32)
    lsum = sb("lsumA", [128, 1], F32)
    osb = sb("osbA", [128, 64], BF16)
    ptb = [sb(f"ptbA{i}", [128, 128], BF16) for i in range(2)]

    pA = PBank(P, ctx, nc, "pA", [128, 512])
    pB = PBank(P, ctx, nc, "pB", [128, 512])
    pS0 = PBank(P, ctx, nc, "pS0", [128, 512])
    pS1 = PBank(P, ctx, nc, "pS1", [128, 512])
    pT0 = PBank(P, ctx, nc, "pT0", [128, 128], BF16)
    pT1 = PBank(P, ctx, nc, "pT1", [128, 128], BF16)
    pO0 = PBank(P, ctx, nc, "pO0", [128, 64])
    pO1 = PBank(P, ctx, nc, "pO1", [128, 64])

    ld_lat2 = [DStream(P, "ldlatA", "sync", 4), DStream(P, "ldlatB", "sync", 4)]
    chunk_ctr = [0]
    chunk_guard = [None, None]
    st_lat = DStream(P, "stlat", "gpsimd", 2)
    ld_w = DStream(P, "ldw", "gpsimd", 2)
    st_ot = DStream(P, "stot", "gpsimd", 2)
    ot_ctr = [0]
    ot_st_tok = [None, None]
    s_latuse = P.S("latuse")

    sc_free = [None, None]
    p_free = [None]
    silu_free = [None, None]
    osb_free = [None]
    t1_guard = [None]
    lat_st_toks = {}

    for h in range(8):
        def head(h=h):
            hs = ExitStack()
            sbH = lambda n, s, d: hs.enter_context(nc.sbuf_tensor(n, s, d))
            qusT = sbH(f"qusT{h}", [64, T], BF16)
            kuT = sbH(f"kuT{h}", [64, T], BF16)
            wq = sbH(f"wq{h}", [128, 8, 320], BF16)
            wv = sbH(f"wv{h}", [128, 8, 64], BF16)
            if t1_guard[0] is not None:
                P.waits(("gpsimd",), [t1_guard[0]])
                P.waits(("gpsimd",), ld_lat2[0].tokens() + ld_lat2[1].tokens())
            tokw = ld_w.issue(wq[:, :, :], a["wqkvh"][h].rearrange("(c p) m -> p c m", p=128))
            tokv = ld_w.issue(wv[:, :, :], a["wvc"][h].rearrange("(c p) m -> p c m", p=128))
            P.waits(("tensor",), [tokw, tokv])
            for tt in range(8):
                def qkvtile(tt=tt):
                    for mt in range(3):
                        bk = pA if mt % 2 == 0 else pB
                        mwid = 128 if mt < 2 else 64
                        bk.mm_group([(bk.ap[0:mwid, :], wq[:, cc, mt*128:mt*128+mwid],
                                      xlnT[:, cc, ts(tt, 512)]) for cc in range(8)])
                        if mt == 0:
                            bk.consume("vector", lambda e, p, _t=tt: e.tensor_scalar_mul(
                                qusT[:, ts(_t, 512)], p[0:64, :], 0.125))
                            bk.consume("vector", lambda e, p, _t=tt: e.tensor_copy(
                                kuT[:, ts(_t, 512)], p[64:128, :]))
                        elif mt == 1:
                            bk.consume("vector", lambda e, p, _t=tt: e.tensor_copy(
                                vuT[:, ts(_t, 512)], p[0:64, :]))
                            bk.consume("vector", lambda e, p, _t=tt: e.tensor_scalar_mul(
                                qcsT[:, ts(_t, 512)], p[64:128, :], 0.125))
                        else:
                            bk.consume("vector", lambda e, p, _t=tt: e.tensor_copy(
                                kcT[:, ts(_t, 512)], p[0:64, :]))
                    for q in range(4):
                        st_i = tt * 4 + q
                        bk = pO0 if q % 2 == 0 else pO1
                        bk.mm_group([(None, xlnT[:, cc, ts(st_i, 128)], wv[:, cc, :])
                                     for cc in range(8)])
                        bk.consume("vector", lambda e, p, _s=st_i: e.tensor_copy(
                            vc_sb[:, _s, :], p[:, :]))
                qkvtile(tt)
            for J in range(32):
                for K in range(min(J // 4, 7) + 1):
                    def lat_tile(J=J, K=K):
                        slot = LAT_SLOT[(J, K)]
                        bk = pA if (J + K) % 2 == 0 else pB
                        prev = lat_st_toks.get("last")
                        if prev is not None:
                            P.waits(("scalar", "vector"), [prev])
                        if h > 0:
                            pass
                        bk.mm_group([(None, kuT[:, ts(J, 128)], qusT[:, ts(K, 512)])])
                        if J < 4 * K + 4:
                            stok = bk.consume("scalar", lambda e, p: e.activation(
                                silu_s[0][:, :], p[:, :], AF.Sigmoid))
                            P.waits(("vector",), [stok])
                            vtok = P.opt("vector", lambda e, _m=J-4*K: e.tensor_tensor(
                                latw[:, :], silu_s[0][:, :], latm[:, _m, :], ALU.mult))
                            P.waits(("gpsimd",), [vtok])
                            silu_free[0] = vtok
                        else:
                            ctok = bk.consume("scalar", lambda e, p: e.activation(
                                latw[:, :], p[:, :], AF.Sigmoid))
                            P.waits(("gpsimd",), [ctok])
                        t = st_lat.issue(a["lat_d"][slot*128:(slot+1)*128, :], latw[:, :])
                        lat_st_toks[slot] = t
                        lat_st_toks["last"] = t
                    lat_tile(J, K)
            hs.close()
            for g in range(16):
                def group(g=g):
                    s0, s1 = 2 * g, 2 * g + 1
                    for J in range(s1 + 1):
                        def t1tile(J=J):
                            bk = pA if J % 2 == 0 else pB
                            bk.mm_group([(bk.ap[:, 0:256], vuT[:, ts(J, 128)],
                                          qcsT[:, g*256:g*256+256])])
                            if t1_guard[0] is not None and J <= 1:
                                P.waits(("vector",), [t1_guard[0]])
                            if J >= s0:
                                bk.consume("vector", lambda e, p, _J=J, _m=J-s0: e.tensor_tensor(
                                    t1tb[:, _J, :], p[:, 0:256], t1m[:, _m, :], ALU.mult))
                            else:
                                bk.consume("vector", lambda e, p, _J=J: e.tensor_copy(
                                    t1tb[:, _J, :], p[:, 0:256]))
                        t1tile(J)
                    P.waits(("tensor",), [(pA.free, pA.freed_target),
                                          (pB.free, pB.freed_target)])
                    km0, km1 = s0 // 4, s1 // 4
                    for K in range(km1 + 1):
                        def ktile(K=K):
                            do0 = K <= km0
                            if do0:
                                pS0.acc_start()
                            pS1.acc_start()
                            Js = list(range(4 * K, s1 + 1))
                            chunks = [Js[i:i+4] for i in range(0, len(Js), 4)]
                            for ci, ch in enumerate(chunks):
                                def chunk(ci=ci, ch=ch):
                                    par = chunk_ctr[0] % 2
                                    cb = latc[par]
                                    strm = ld_lat2[par]
                                    g2 = chunk_guard[par]
                                    if g2 is not None:
                                        P.waits(("sync",), [g2])
                                    chunk_ctr[0] += 1
                                    toks = []
                                    for q, J in enumerate(ch):
                                        slot = LAT_SLOT[(J, K)]
                                        P.waits(("sync",), [lat_st_toks[slot]])
                                        toks.append(strm.issue(
                                            cb[:, q, :], a["lat_d"][slot*128:(slot+1)*128, :]))
                                    P.waits(("tensor",), toks)
                                    lastc = (ci == len(chunks) - 1)
                                    for q, J in enumerate(ch):
                                        first = (ci == 0 and q == 0)
                                        last = (lastc and q == len(ch) - 1)
                                        if do0:
                                            pS0.acc_mm(None, t1tb[:, J, 0:128], cb[:, q, :],
                                                       first, last)
                                        if last:
                                            pS1.acc_mm(None, t1tb[:, J, 128:256], cb[:, q, :],
                                                       first, True)
                                            chunk_guard[(chunk_ctr[0] - 1) % 2] = (pS1.full, pS1.fills)
                                        elif q == len(ch) - 1:
                                            v = P.mm(pS1.ap[:, :], t1tb[:, J, 128:256], cb[:, q, :],
                                                     first, False, sem=s_latuse)
                                            chunk_guard[(chunk_ctr[0] - 1) % 2] = (s_latuse, v)
                                        else:
                                            pS1.acc_mm(None, t1tb[:, J, 128:256], cb[:, q, :],
                                                       first, False)
                                chunk(ci, ch)
                            for sidx, bk, s_ in ((0, pS0, s0), (1, pS1, s1)):
                                if sidx == 0 and not do0:
                                    continue
                                def evac(bk=bk, s_=s_, sidx=sidx):
                                    if silu_free[sidx] is not None:
                                        P.waits(("scalar",), [silu_free[sidx]])
                                    stok = bk.consume("scalar", lambda e, p, _i=sidx: e.activation(
                                        silu_s[_i][:, :], p[:, :], AF.Silu))
                                    ck = pA if sidx == 0 else pB
                                    ck.mm_group([(None, qcsT[:, ts(s_, 128)], kcT[:, ts(K, 512)])])
                                    P.waits(("vector",), [stok])
                                    if K == 0 and sc_free[s_ % 2] is not None:
                                        P.waits(("vector",), [sc_free[s_ % 2]])
                                    silu_free[sidx] = ck.consume(
                                        "vector", lambda e, p, _s=s_, _K=K, _i=sidx:
                                        e.scalar_tensor_tensor(
                                            scoreb[_s % 2][:, ts(_K, 512)], p[:, :], 1.0,
                                            silu_s[_i][:, :], ALU.mult, ALU.subtract))
                                    if K == s_ // 4:
                                        P.op("vector", lambda e, _s=s_, _K=K: e.tensor_add(
                                            scoreb[_s % 2][:, ts(_K, 512)],
                                            scoreb[_s % 2][:, ts(_K, 512)],
                                            caum[:, _s % 4, :]))
                                evac()
                        ktile(K)
                    for s_ in (s0, s1):
                        def smax(s_=s_):
                            scb = scoreb[s_ % 2]
                            Ls = 512 * (s_ // 4 + 1)
                            P.op("vector", lambda e, _L=Ls, _b=scb: e.reduce_max(
                                negmax[:, :], _b[:, 0:_L], axis=AXX))
                            ntok = P.opt("vector", lambda e: e.tensor_scalar_mul(
                                negmax[:, :], negmax[:, :], -1.0))
                            P.waits(("scalar",), [ntok])
                            if p_free[0] is not None:
                                P.waits(("scalar",), [p_free[0]])
                            etok = P.opt("scalar", lambda e, _L=Ls, _b=scb: e.activation(
                                pbuf[:, 0:_L], _b[:, 0:_L], AF.Exp, bias=negmax[:, :],
                                accum_out=lsum[:, :]))
                            sc_free[s_ % 2] = etok
                            P.waits(("vector",), [etok])
                            P.op("vector", lambda e: e.reciprocal(rl[:, :], lsum[:, :]))
                            pstok = P.opt("vector", lambda e, _L=Ls: e.tensor_scalar_mul(
                                pbuf[:, 0:_L], pbuf[:, 0:_L], rl[:, :]))
                            P.waits(("tensor",), [pstok])
                            ob = pO0 if s_ % 2 == 0 else pO1
                            ob.acc_start()
                            nk = Ls // 128
                            for kt in range(nk):
                                def pv(kt=kt):
                                    pt = pT0 if kt % 2 == 0 else pT1
                                    pt.tp_into(pbuf[:, ts(kt, 128)], ident[:, :])
                                    pt.consume("vector", lambda e, p, _k=kt: e.tensor_copy(
                                        ptb[_k % 2][:, :], p[:, 0:128]))
                                    P.wait("tensor", pt.free, pt.freed_target)
                                    ob.acc_mm(None if False else ob.ap[:, 0:64],
                                              ptb[kt % 2][:, :], vc_sb[:, kt, :],
                                              kt == 0, kt == nk - 1)
                                pv(kt)
                            if osb_free[0] is not None:
                                P.waits(("scalar",), [osb_free[0]])
                            ob.consume("scalar", lambda e, p: e.copy(osb[:, :], p[:, 0:64]))
                            P.wait("tensor", ob.free, ob.freed_target)
                            pt = pT0
                            pt.tp_into(osb[:, 0:64], ident[:, :])
                            osb_free[0] = (pt.full, pt.fills)
                            bi = ot_ctr[0] % 2
                            ot_ctr[0] += 1
                            if ot_st_tok[bi] is not None:
                                P.waits(("vector",), [ot_st_tok[bi]])
                            etok = pt.consume("vector", lambda e, p, _b=bi: e.tensor_copy(
                                otsb[_b][:, :], p[0:64, 0:128]))
                            P.waits(("gpsimd",), [etok])
                            ot_st_tok[bi] = st_ot.issue(
                                a["otd"][h // 2, 64*(h % 2):64*(h % 2)+64, ts(s_, 128)],
                                otsb[bi][:, :])
                            p_free[0] = (pt.full, pt.fills)
                        smax(s_)
                    t1_guard[0] = (s_latuse, s_latuse.n)
                group(g)
        head(h)

    P.waits(("sync",), st_ot.tokens())
    woutT_sb = sb("woutT_sb", [128, 4, DM], BF16)
    otl = [sb(f"otl{i}", [128, 4, 128], BF16) for i in range(2)]
    s_cstg = P.S("cstg", sw=True)
    P.dma("gpsimd", s_cstg, woutT_sb[:, :, :], a["woutT"].rearrange("(c p) h -> p c h", p=128))
    P.waits(("tensor",), [(s_cstg, s_cstg.n)])
    ld_x2 = DStream(P, "ldx2", "sync", 2)
    ld_ot = DStream(P, "ldot", "sync", 2)
    st_y2 = DStream(P, "sty2", "gpsimd", 2)
    xs2 = [sb(f"xs2_{i}", [128, DM], F32) for i in range(2)]
    ys2 = [sb(f"ys2_{i}", [128, DM], F32) for i in range(2)]
    y_tok2 = [None, None]
    x_free2 = [None, None]
    ot_free2 = [None, None]
    for s in range(NSTRIP):
        def ostrip(s=s):
            xb, yb, ob_ = xs2[s % 2], ys2[s % 2], otl[s % 2]
            if x_free2[s % 2] is not None:
                P.waits(("sync",), [x_free2[s % 2]])
            tok = ld_x2.issue(xb[:, :], a["x"][ts(s, 128), :])
            if ot_free2[s % 2] is not None:
                P.waits(("sync",), [ot_free2[s % 2]])
            tok2 = ld_ot.issue(ob_[:, :, :],
                               a["otd"][:, :, ts(s, 128)].rearrange("c p t -> p c t"))
            P.waits(("tensor",), [tok2])
            if y_tok2[s % 2] is not None:
                P.waits(("vector",), [y_tok2[s % 2]])
            last_v = None
            for dh in range(2):
                bk = pA if dh == 0 else pB
                bk.mm_group([(None, ob_[:, cc, :], woutT_sb[:, cc, ts(dh, 512)])
                             for cc in range(4)])
                if dh == 0:
                    P.waits(("vector",), [tok])
                last_v = bk.consume("vector", lambda e, p, _d=dh, _y=yb, _x=xb:
                                    e.scalar_tensor_tensor(
                                        _y[:, ts(_d, 512)], p[:, :], a["scale"],
                                        _x[:, ts(_d, 512)], ALU.mult, ALU.add))
            x_free2[s % 2] = last_v
            ot_free2[s % 2] = (pB.full, pB.fills)
            P.waits(("gpsimd",), [last_v])
            y_tok2[s % 2] = st_y2.issue(a["y"][ts(s, 128), :], yb[:, :])
        ostrip(s)
    P.waits(("gpsimd",), st_y2.tokens())


def build_l1attn():
    nc = bass.Bass("TRN2", target_bir_lowering=False)
    dr = lambda n, s, d, k="ExternalInput": nc.dram_tensor(n, s, d, kind=k).ap()
    a = dict(
        x=dr("x", [T, DM], F32),
        y=dr("y", [T, DM], F32, "ExternalOutput"),
        wdkvkr=dr("wdkvkr", [DM, 288], F32),
        wdq=dr("wdq", [DM, 512], F32),
        wqr=dr("wqr", [512, 256], F32),
        widxq=dr("widxq", [DM, 256], F32),
        widxk=dr("widxk", [DM, 64], F32),
        wuqT=dr("wuqT", [512, 512], F32),
        wukT=dr("wukT", [512, 256], F32),
        wuv=dr("wuv", [256, 512], F32),
        wout=dr("wout", [512, DM], F32),
        idxw=dr("idxw", [1, 4], F32),
        rawdelta=dr("rawdelta", [1, 16], F32),
        cosT=dr("cosT", [128, 32, 16], F32), sinT=dr("sinT", [128, 32, 16], F32),
        cosB=dr("cosB", [128, 32, 16], F32), sinB=dr("sinB", [128, 32, 16], F32),
        cosS=dr("cosS", [128, 32, 16], F32), sinS=dr("sinS", [128, 32, 16], F32),
        cosR=dr("cosR", [128, 3, 32, 16], F32), sinR=dr("sinR", [128, 3, 32, 16], F32),
        identbf=dr("identbf", [128, 128], BF16),
        onesf=dr("onesf", [1, 128], F32),
        bandmask=dr("bandmask", [128, 256], BF16),
        imaskd=dr("imaskd", [128, 4, 512], BF16),
        imaskp=dr("imaskp", [128, 512], BF16),
        s0add=dr("s0add", [128, 192], F32),
        qiT_d=dr("qiT_d", [NSTRIP, 128, 2, 128], BF16, "Internal"),
        cqd=dr("cqd", [NSTRIP, 128, 4, 128], BF16, "Internal"),
        ckd=dr("ckd", [T, 288], BF16, "Internal"),
        otd=dr("otd", [4, 128, T], BF16, "Internal"),
        scale=1.0,
    )
    ctx = ExitStack()
    P = Prog(nc, ctx)
    emit_l1attn(nc, ctx, P, a)
    P.build()
    ctx.close()
    return nc


def _rot(P, out_re, out_im, re, im, cos, sin, tmp1, tmp2):
    P.op("vector", lambda e: e.tensor_tensor(tmp1, re, cos, ALU.mult))
    P.op("vector", lambda e: e.tensor_tensor(tmp2, im, sin, ALU.mult))
    P.op("vector", lambda e: e.tensor_sub(out_re, tmp1, tmp2))
    P.op("vector", lambda e: e.tensor_tensor(tmp1, re, sin, ALU.mult))
    P.op("vector", lambda e: e.tensor_tensor(tmp2, im, cos, ALU.mult))
    return P.opt("vector", lambda e: e.tensor_add(out_im, tmp1, tmp2))


def _softplus(P, dst, src, tmp):
    P.op("scalar", lambda e: e.activation(tmp, src, AF.Abs))
    P.op("scalar", lambda e: e.activation(tmp, tmp, AF.Exp, scale=-1.0))
    P.op("scalar", lambda e: e.activation(tmp, tmp, AF.Ln, bias=1.0))
    tok = P.opt("scalar", lambda e: e.activation(dst, src, AF.Relu))
    P.waits(("vector",), [tok])
    return P.opt("vector", lambda e: e.tensor_add(dst, dst, tmp))


def emit_l1attn(nc, ctx, P, a):
    sb = lambda n, s, d: ctx.enter_context(nc.sbuf_tensor(n, s, d))
    SCL = float((64 + 32) ** -0.5)

    ident = sb("ident_sb", [128, 128], BF16)
    onesf = sb("onesf_sb", [1, 128], F32)
    bandm = sb("bandm_sb", [128, 256], BF16)
    imaskd = sb("imaskd_sb", [128, 4, 512], BF16)
    imaskp = sb("imaskp_sb", [128, 512], BF16)
    tabs = {}
    s_cst = P.S("cst")
    P.dma("sync", s_cst, ident[:, :], a["identbf"])
    P.dma("sync", s_cst, onesf[:, :], a["onesf"])
    P.dma("sync", s_cst, bandm[:, :], a["bandmask"])
    P.dma("sync", s_cst, imaskd[:, :, :], a["imaskd"])
    P.dma("sync", s_cst, imaskp[:, :], a["imaskp"])
    for nm in ("cosT", "sinT", "cosB", "sinB", "cosS", "sinS"):
        tabs[nm] = sb(nm + "_sb", [128, 32, 16], F32)
        P.dma("sync", s_cst, tabs[nm][:, :, :], a[nm])
    tabRc = sb("tabRc", [128, 3, 32, 16], F32)
    tabRs = sb("tabRs", [128, 3, 32, 16], F32)
    P.dma("sync", s_cst, tabRc[:, :, :, :], a["cosR"])
    P.dma("sync", s_cst, tabRs[:, :, :, :], a["sinR"])
    idxw_sb = sb("idxw_sb", [1, 4], F32)
    rdel = sb("rdel_sb", [1, 16], F32)
    s0add = sb("s0add_sb", [128, 192], F32)
    P.dma("sync", s_cst, idxw_sb[:, :], a["idxw"])
    P.dma("sync", s_cst, rdel[:, :], a["rawdelta"])
    P.dma("sync", s_cst, s0add[:, :], a["s0add"])
    v_cst = s_cst.n
    P.waits(("tensor", "vector", "scalar"), [(s_cst, v_cst)])

    pA = PBank(P, ctx, nc, "pA", [128, 512])
    pB = PBank(P, ctx, nc, "pB", [128, 512])
    pC = PBank(P, ctx, nc, "pC", [128, 512])
    pD = PBank(P, ctx, nc, "pD", [128, 512])
    pT0 = PBank(P, ctx, nc, "pT0", [128, 128], BF16)
    pT1 = PBank(P, ctx, nc, "pT1", [128, 128], BF16)
    pO0 = PBank(P, ctx, nc, "pO0", [128, 64])
    pRep = PBank(P, ctx, nc, "pRep", [128, 64])

    dl = sb("dl", [1, 16], F32)
    shh = sb("shh", [1, 16], F32)
    chh = sb("chh", [1, 16], F32)
    cosd = sb("cosd", [1, 16], F32)
    sind = sb("sind", [1, 16], F32)
    cosd_r = sb("cosd_r", [128, 16], F32)
    sind_r = sb("sind_r", [128, 16], F32)
    wrep4 = sb("wrep4", [128, 4], F32)
    dsc = ExitStack()
    sbD = lambda n, s, d: dsc.enter_context(nc.sbuf_tensor(n, s, d))
    tt1 = sbD("tt1", [128, 32, 16], F32)
    tt2 = sbD("tt2", [128, 32, 16], F32)
    tt3 = sbD("tt3", [128, 32, 16], F32)

    t1 = P.opt("scalar", lambda e: e.activation(dl[:, :], rdel[:, :], AF.Sigmoid))
    P.waits(("vector",), [t1])
    t2 = P.opt("vector", lambda e: e.tensor_scalar_mul(dl[:, :], dl[:, :], -2.0 * float(np.pi)))
    P.waits(("scalar",), [t2])
    P.op("scalar", lambda e: e.activation(shh[:, :], dl[:, :], AF.Sin, scale=0.5))
    t2b = P.opt("scalar", lambda e: e.mul(chh[:, :], dl[:, :], 0.5))
    P.waits(("vector",), [t2b])
    t2c = P.opt("vector", lambda e: e.tensor_scalar_add(chh[:, :], chh[:, :],
                                                        float(np.pi / 2)))
    P.waits(("scalar",), [t2c])
    t3 = P.opt("scalar", lambda e: e.activation(chh[:, :], chh[:, :], AF.Sin))
    P.waits(("vector",), [t3])
    P.op("vector", lambda e: e.scalar_tensor_tensor(cosd[:, :], shh[:, :], -2.0, shh[:, :],
                                                    ALU.mult, ALU.mult))
    P.op("vector", lambda e: e.tensor_scalar_add(cosd[:, :], cosd[:, :], 1.0))
    t4 = P.opt("vector", lambda e: e.scalar_tensor_tensor(sind[:, :], shh[:, :], 2.0,
                                                          chh[:, :], ALU.mult, ALU.mult))
    P.waits(("tensor",), [t4])
    pRep.mm_group([(pRep.ap[0:128, 0:16], onesf[:, :], cosd[:, :])])
    pRep.consume("vector", lambda e, p: e.tensor_copy(cosd_r[:, :], p[:, 0:16]))
    pRep.mm_group([(pRep.ap[0:128, 0:16], onesf[:, :], sind[:, :])])
    pRep.consume("vector", lambda e, p: e.tensor_copy(sind_r[:, :], p[:, 0:16]))
    pRep.mm_group([(pRep.ap[0:128, 0:4], onesf[:, :], idxw_sb[:, :])])
    pRep.consume("vector", lambda e, p: e.tensor_copy(wrep4[:, :], p[:, 0:4]))

    cb = cosd_r[:, :].unsqueeze(1).to_broadcast([128, 32, 16])
    sb_ = sind_r[:, :].unsqueeze(1).to_broadcast([128, 32, 16])
    s_tb = P.S("tb")

    def delta_combine2(ct, st):
        P.op("vector", lambda e: e.tensor_tensor(tt1[:, :, :], ct, cb, ALU.mult))
        P.op("vector", lambda e: e.tensor_tensor(tt2[:, :, :], st, sb_, ALU.mult))
        P.op("vector", lambda e: e.tensor_sub(tt3[:, :, :], tt1[:, :, :], tt2[:, :, :]))
        P.op("vector", lambda e: e.tensor_tensor(tt1[:, :, :], st, cb, ALU.mult))
        P.op("vector", lambda e: e.tensor_tensor(tt2[:, :, :], ct, sb_, ALU.mult))
        P.op("vector", lambda e: e.tensor_add(st, tt1[:, :, :], tt2[:, :, :]))
        P.op("vector", lambda e: e.tensor_copy(ct, tt3[:, :, :]), sem=s_tb)

    delta_combine2(tabs["cosT"][:, :, :], tabs["sinT"][:, :, :])
    for b in range(3):
        delta_combine2(tabRc[:, b, :, :], tabRs[:, b, :, :])
    v_tb = s_tb.n
    dsc.close()

    ckvT = sb("ckvT", [128, 2, T], BF16)
    cknat = sb("cknat", [128, 32, 288], BF16)
    vT = sb("vT", [32, T], BF16)
    kiT = sb("kiT", [128, T], BF16)
    wth = sb("wth", [128, 32, 256], BF16)
    wuv_sb = sb("wuv_sb", [128, 2, 512], BF16)
    wqr_sb = sb("wqr_sb", [128, 4, 256], BF16)

    ln1 = ExitStack()
    sbL = lambda n, s, d: ln1.enter_context(nc.sbuf_tensor(n, s, d))
    xlnT = sbL("xlnT", [128, 8, T], BF16)
    lnk = LNKit(P, sbL)
    ln_toks = emit_ln_xlnT(nc, ln1, P, sbL, a, xlnT, ident, lnk, tag="M", banks=(pT0, pT1))
    P.waits(("tensor",), ln_toks[0:2])

    wsc = ExitStack()
    sbW = lambda n, s, d: wsc.enter_context(nc.sbuf_tensor(n, s, d))
    wdkv_sb = sbW("wdkv_sb", [128, 8, 288], BF16)
    wdq_sb = sbW("wdq_sb", [128, 8, 512], BF16)
    widxq_sb = sbW("widxq_sb", [128, 8, 256], BF16)
    widxk_sb = sbW("widxk_sb", [128, 8, 64], BF16)
    wuqT_sb = sbW("wuqT_sb", [128, 4, 512], BF16)
    wukT_sb = sbW("wukT_sb", [128, 4, 256], BF16)
    lnsq = sbW("lnsq", [128, 288], F32)
    sp1t = sbW("sp1t", [128, 32], F32)
    vband = sbW("vband", [128, 32], BF16)
    qi_sb2 = [sbW(f"qi_sb{i}", [128, 2, 128], BF16) for i in range(2)]
    cqb2 = [sbW(f"cqb{i}", [128, 4, 128], BF16) for i in range(2)]
    s_ldp = P.S("ldp", sw=True)
    P.dma("gpsimd", s_ldp, wdkv_sb[:, :, :], a["wdkvkr"].rearrange("(c p) h -> p c h", p=128))
    P.dma("gpsimd", s_ldp, wdq_sb[:, :, :], a["wdq"].rearrange("(c p) h -> p c h", p=128))
    P.dma("gpsimd", s_ldp, widxq_sb[:, :, :], a["widxq"].rearrange("(c p) h -> p c h", p=128))
    P.dma("gpsimd", s_ldp, widxk_sb[:, :, :], a["widxk"].rearrange("(c p) h -> p c h", p=128))
    P.dma("gpsimd", s_ldp, wuqT_sb[:, :, :], a["wuqT"].rearrange("(c p) h -> p c h", p=128))
    P.dma("gpsimd", s_ldp, wukT_sb[:, :, :], a["wukT"].rearrange("(c p) h -> p c h", p=128))
    P.dma("gpsimd", s_ldp, wuv_sb[:, :, :], a["wuv"].rearrange("(c p) h -> p c h", p=128))
    P.dma("gpsimd", s_ldp, wqr_sb[:, :, :], a["wqr"].rearrange("(c p) h -> p c h", p=128))
    v_ldp = s_ldp.n
    P.waits(("tensor",), [(s_ldp, v_ldp), (s_tb, v_tb)])

    for h in range(8):
        for mt in range(4):
            def wt_tile(h=h, mt=mt):
                bk = pA if mt % 2 == 0 else pB
                hb = (h % 2) * 64
                bk.mm_group([(bk.ap[:, 0:256],
                              wuqT_sb[hb:hb + 64, h // 2, ts(mt, 128)],
                              wukT_sb[hb:hb + 64, h // 2, :])])
                bk.consume("vector", lambda e, p, _h=h, _m=mt: e.tensor_scalar_mul(
                    wth[:, _h * 4 + _m, :], p[:, 0:256], SCL))
            wt_tile(h, mt)

    st_ck = DStream(P, "stck", "gpsimd", 2)
    st_qi = DStream(P, "stqi", "gpsimd", 2)
    st_cq = DStream(P, "stcq", "gpsimd", 2)
    cq_sb = wsc_cq = None
    ck_tok = None
    qi_tok = [None, None]
    cq_tok = [None, None]
    for s in range(NSTRIP):
        def pstrip(s=s):
            nonlocal ck_tok
            pA.mm_group([(pA.ap[:, 0:288], xlnT[:, cc, ts(s, 128)], wdkv_sb[:, cc, :])
                         for cc in range(8)])
            etok = pA.consume("vector", lambda e, p: e.tensor_copy(lnsq[:, :], p[:, 0:288]))
            P.waits(("scalar",), [etok])
            sp_tok = _softplus(P, lnsq[:, 256:288], lnsq[:, 256:288], sp1t[:, :])
            if ck_tok is not None:
                P.waits(("vector",), [ck_tok])
            c1 = P.opt("vector", lambda e, _s=s: e.tensor_copy(cknat[:, _s, :], lnsq[:, :]))
            P.waits(("gpsimd",), [c1])
            ck_tok = st_ck.issue(a["ckd"][ts(s, 128), :], cknat[:, s, :])
            P.waits(("tensor",), [c1])
            rtok = _rot(P, vband[:, 0:16], vband[:, 16:32],
                        lnsq[:, 256:272], lnsq[:, 272:288],
                        tabs["cosS"][:, s, :], tabs["sinS"][:, s, :],
                        sp1t[:, 0:16], sp1t[:, 16:32])
            P.waits(("tensor",), [rtok])
            pT0.tp_into(vband[:, 0:32], ident[:, :])
            pT0.consume("vector", lambda e, p, _s=s: e.tensor_copy(
                vT[:, ts(_s, 128)], p[0:32, 0:128]))
            for c2 in range(2):
                pt = pT1 if c2 == 0 else pT0
                pt.tp_into(cknat[:, s, ts(c2, 128)], ident[:, :])
                pt.consume("vector", lambda e, p, _c=c2, _s=s: e.tensor_copy(
                    ckvT[:, _c, ts(_s, 128)], p[:, 0:128]))
            for mt in range(4):
                bk = pB if mt % 2 == 0 else pC
                bk.mm_group([(bk.ap[:, 0:128], wdq_sb[:, cc, ts(mt, 128)],
                              xlnT[:, cc, ts(s, 128)]) for cc in range(8)])
                if mt == 0 and cq_tok[s % 2] is not None:
                    P.waits(("vector",), [cq_tok[s % 2]])
                bk.consume("vector", lambda e, p, _m=mt, _s=s: e.tensor_copy(
                    cqb2[_s % 2][:, _m, :], p[:, 0:128]))
            vcq = P.opt("vector", lambda e: e.nop(), drain=False)
            P.waits(("gpsimd",), [vcq])
            cq_tok[s % 2] = st_cq.issue(a["cqd"][s], cqb2[s % 2][:, :, :])
            pD.mm_group([(pD.ap[0:64, 0:128], widxk_sb[:, cc, :],
                          xlnT[:, cc, ts(s, 128)]) for cc in range(8)])
            pD.consume("vector", lambda e, p, _s=s: e.tensor_copy(
                kiT[0:64, ts(_s, 128)], p[0:64, 0:128]))
            pD.consume("vector", lambda e, p, _s=s: e.tensor_copy(
                kiT[64:128, ts(_s, 128)], p[0:64, 0:128]))
            for mt in range(2):
                bk = pB if mt == 0 else pC
                bk.mm_group([(bk.ap[:, 0:128], widxq_sb[:, cc, ts(mt, 128)],
                              xlnT[:, cc, ts(s, 128)]) for cc in range(8)])
                if mt == 0 and qi_tok[s % 2] is not None:
                    P.waits(("vector",), [qi_tok[s % 2]])
                bk.consume("vector", lambda e, p, _m=mt, _s=s: e.tensor_copy(
                    qi_sb2[_s % 2][:, _m, :], p[:, 0:128]))
            vq = P.opt("vector", lambda e: e.nop(), drain=False)
            P.waits(("gpsimd",), [vq])
            qi_tok[s % 2] = st_qi.issue(a["qiT_d"][s], qi_sb2[s % 2][:, :, :])
        pstrip(s)
    proj_toks = [t for t in [ck_tok, qi_tok[0], qi_tok[1], cq_tok[0], cq_tok[1]]
                 if t is not None]
    P.waits(("sync", "gpsimd"), proj_toks)
    wsc.close()
    ln1.close()

    at = ExitStack()
    sbA_ = lambda n, s, d: at.enter_context(nc.sbuf_tensor(n, s, d))
    ksel = sbA_("ksel", [128, 32, 288], BF16)
    rotb = sbA_("rotb", [128, 32, 16], F32)
    rotb2 = sbA_("rotb2", [128, 32, 16], F32)
    Ibuf = sbA_("Ibuf", [128, T], F32)
    relu_t = sbA_("relu_t", [128, 512], F32)
    mx8 = sbA_("mx8", [128, 8], F32)
    tidx = sbA_("tidx", [128, 96], U32)
    tidxi = sbA_("tidxi", [128, 96], I32)
    qi_l = sbA_("qi_l", [128, 2, 128], BF16)
    cq_l = sbA_("cq_l", [128, 4, 128], BF16)
    qr_raw = sbA_("qr_raw", [128, 256], F32)
    sptmp = sbA_("sptmp", [128, 256], F32)
    usel = sbA_("usel", [128, 8, 32], BF16)
    uband = sbA_("uband", [128, 8, 32], BF16)
    qhat = sbA_("qhat", [128, 256], BF16)
    qext = sbA_("qext", [128, 288], BF16)
    qhatT = sbA_("qhatT", [128, 2, 128], BF16)
    ubT = sbA_("ubT", [32, 128], BF16)
    sbband = sbA_("sbband", [128, 256], F32)
    selS = sbA_("selS", [128, 32], F32)
    ttrj = sbA_("ttrj", [128, 288], BF16)
    ttp = sbA_("ttp", [128, 32, 288], BF16)
    ttq = sbA_("ttq", [128, 32, 256], BF16)
    negm = sbA_("negm", [128, 1], F32)
    m2t = sbA_("m2t", [128, 1], F32)
    l1t = sbA_("l1t", [128, 1], F32)
    l2t = sbA_("l2t", [128, 1], F32)
    rlt = sbA_("rlt", [128, 1], F32)
    Pb = sbA_("Pb", [128, 256], BF16)
    Ps = sbA_("Ps", [128, 32], F32)
    ptb = [sbA_(f"ptb{i}", [128, 128], BF16) for i in range(2)]
    asel = sbA_("asel", [128, 256], F32)
    atot = sbA_("atot", [128, 256], F32)
    atbf = sbA_("atbf", [128, 256], BF16)
    osb = sbA_("osb", [128, 64], BF16)
    otsb = [sbA_(f"otsb{i}", [64, 128], BF16) for i in range(2)]
    s96 = sbA_("s96", [128, 8, 96], F32)
    P96 = sbA_("P96", [128, 8, 96], F32)
    a8 = sbA_("a8", [128, 8, 256], F32)
    sent = sbA_("sent", [128, 192], F32)

    gsel = DStream(P, "gsel", "gpsimd", 4)
    ld_qi = DStream(P, "ldqi", "sync", 2)
    st_ot = DStream(P, "stot", "gpsimd", 2)
    ot_ctr = [0]
    ot_tok = [None, None]
    ksel_guard = [None]
    ksel_guard2 = [None]

    ri_tok = [None]

    def emit_I(s, nK):
        for K in range(nK):
            for hh in range(4):
                def ih(K=K, hh=hh):
                    bk = pC if hh % 2 == 0 else pD
                    hb = (hh % 2) * 64
                    bk.mm_group([(None, qi_l[hb:hb + 64, hh // 2, :],
                                  kiT[hb:hb + 64, ts(K, 512)])])
                    if ri_tok[0] is not None:
                        P.waits(("scalar",), [ri_tok[0]])
                    stok = bk.consume("scalar", lambda e, p: e.activation(
                        relu_t[:, :], p[:, :], AF.Relu))
                    P.waits(("vector",), [stok])
                    if hh == 0:
                        ri_tok[0] = P.opt("vector", lambda e, _K=K: e.tensor_scalar_mul(
                            Ibuf[:, ts(_K, 512)], relu_t[:, :], wrep4[:, 0:1]))
                    else:
                        ri_tok[0] = P.opt("vector", lambda e, _K=K, _h=hh: e.scalar_tensor_tensor(
                            Ibuf[:, ts(_K, 512)], relu_t[:, :], wrep4[:, _h:_h + 1],
                            Ibuf[:, ts(_K, 512)], ALU.mult, ALU.add))
                ih(K, hh)
        m = s % 4
        Kd = s // 4
        P.op("vector", lambda e, _m=m, _K=Kd: e.tensor_add(
            Ibuf[:, ts(_K, 512)], Ibuf[:, ts(_K, 512)], imaskd[:, _m, :]))
        if m == 0 and Kd >= 1:
            P.op("vector", lambda e, _K=Kd - 1: e.tensor_add(
                Ibuf[:, ts(_K, 512)], Ibuf[:, ts(_K, 512)], imaskp[:, :]))

    def emit_topk(region, rounds, out_off):
        for r in range(rounds):
            P.op("vector", lambda e, _r=r: e.max(out=mx8[:, :], in_=region))
            P.op("vector", lambda e, _r=r: e.max_index(
                tidx[:, out_off + 8 * _r: out_off + 8 * _r + 8], mx8[:, :], region))
            P.op("vector", lambda e: e.match_replace(
                out=region, in_to_replace=mx8[:, :], in_values=region, imm_value=-2e38))

    def emit_qside(s, with_band):
        pD.mm_group([(pD.ap[:, 0:256], cq_l[:, cc, :], wqr_sb[:, cc, :])
                     for cc in range(4)])
        etok = pD.consume("vector", lambda e, p: e.tensor_copy(qr_raw[:, :], p[:, 0:256]))
        P.waits(("scalar",), [etok])
        _softplus(P, qr_raw[:, :], qr_raw[:, :], sptmp[:, :])
        wqv = qr_raw[:, :].rearrange("p (h c) -> p h c", h=8)
        re = wqv[:, :, 0:16]
        im = wqv[:, :, 16:32]
        cT = tabs["cosT"][:, s, :].unsqueeze(1).to_broadcast([128, 8, 16])
        sT = tabs["sinT"][:, s, :].unsqueeze(1).to_broadcast([128, 8, 16])
        _rot(P, usel[:, :, 0:16], usel[:, :, 16:32], re, im, cT, sT,
             rotb[:, 0:8, :], rotb2[:, 0:8, :])
        if with_band:
            cB = tabs["cosB"][:, s, :].unsqueeze(1).to_broadcast([128, 8, 16])
            sB = tabs["sinB"][:, s, :].unsqueeze(1).to_broadcast([128, 8, 16])
            return _rot(P, uband[:, :, 0:16], uband[:, :, 16:32], re, im, cB, sB,
                        rotb[:, 0:8, :], rotb2[:, 0:8, :])
        return P.opt("vector", lambda e: e.nop(), drain=False)

    def emit_gather_rotate(idx_col_base, nslots, rank_block):
        toks = []
        for j in range(nslots):
            toks.append(gsel.issue_gather(
                ksel[:, j, :], a["ckd"], tidxi[:, idx_col_base + j:idx_col_base + j + 1],
                None))
        P.waits(("vector",), toks)
        kv = ksel[:, :, 256:288]
        re = kv[:, :, 0:16]
        im = kv[:, :, 16:32]
        cR = tabRc[:, rank_block, :, :]
        sR = tabRs[:, rank_block, :, :]
        P.op("vector", lambda e: e.tensor_tensor(rotb[:, :, :], re, cR, ALU.mult))
        P.op("vector", lambda e: e.tensor_tensor(rotb2[:, :, :], im, sR, ALU.mult))
        P.op("vector", lambda e: e.tensor_sub(rotb[:, :, :], rotb[:, :, :], rotb2[:, :, :]))
        P.op("vector", lambda e: e.tensor_tensor(rotb2[:, :, :], re, sR, ALU.mult))
        P.op("vector", lambda e: e.tensor_tensor(sptmp_g[:, :, :], im, cR, ALU.mult))
        P.op("vector", lambda e: e.tensor_add(im, rotb2[:, :, :], sptmp_g[:, :, :]))
        return P.opt("vector", lambda e: e.tensor_copy(re, rotb[:, :, :]))

    sptmp_g = sbA_("sptmp_g", [128, 32, 16], F32)

    def emit_headsel_scores(h, out_ap, nslots):
        qb = qext[:, :].unsqueeze(1).to_broadcast([128, nslots, 288])
        P.op("vector", lambda e, _n=nslots, _q=qb: e.tensor_tensor(
            ttp[:, 0:_n, :], ksel[:, 0:_n, :], _q, ALU.mult))
        return P.opt("vector", lambda e, _n=nslots, _o=out_ap: e.reduce_sum(
            _o, ttp[:, 0:_n, :], axis=AXX))

    qx_guard = [None]

    def emit_qext(h, utab):
        P.op("vector", lambda e: e.tensor_copy(qext[:, 0:256], qhat[:, :]))
        tok = P.opt("vector", lambda e, _h=h, _u=utab: e.tensor_copy(
            qext[:, 256:288], _u[:, _h, :]))
        qx_guard[0] = tok
        return tok

    def emit_qhat(s, h):
        bk = pA if h % 2 == 0 else pB
        bk.mm_group([(bk.ap[:, 0:256], cq_l[:, cc, :], wth[:, h * 4 + cc, :])
                     for cc in range(4)])
        if qx_guard[0] is not None:
            P.waits(("scalar",), [qx_guard[0]])
        return bk.consume("scalar", lambda e, p: e.copy(qhat[:, :], p[:, 0:256]))

    def emit_opath(h, s_, src_f32, in_tok=None):
        if in_tok is not None:
            P.waits(("scalar",), [in_tok])
        ctok = P.opt("scalar", lambda e: e.copy(atbf[:, :], src_f32))
        P.waits(("tensor",), [ctok])
        for c2 in range(2):
            pt = pT0 if c2 == 0 else pT1
            pt.tp_into(atbf[:, ts(c2, 128)], ident[:, :])
            pt.consume("scalar", lambda e, p, _c=c2: e.copy(
                qhatT[:, _c, :], p[:, 0:128]))
        P.waits(("tensor",), [(pT0.free, pT0.freed_target), (pT1.free, pT1.freed_target)])
        pO0.mm_group([(None, qhatT[:, c2, :], wuv_sb[:, c2, 64 * h:64 * h + 64])
                      for c2 in range(2)])
        otok = pO0.consume("scalar", lambda e, p: e.copy(osb[:, :], p[:, 0:64]))
        P.waits(("tensor",), [otok])
        pT0.tp_into(osb[:, 0:64], ident[:, :])
        bi = ot_ctr[0] % 2
        ot_ctr[0] += 1
        if ot_tok[bi] is not None:
            P.waits(("vector",), [ot_tok[bi]])
        ev = pT0.consume("vector", lambda e, p, _b=bi: e.tensor_copy(
            otsb[_b][:, :], p[0:64, 0:128]))
        P.waits(("gpsimd",), [ev])
        ot_tok[bi] = st_ot.issue(a["otd"][h // 2, 64 * (h % 2):64 * (h % 2) + 64, ts(s_, 128)],
                                 otsb[bi][:, :])

    prev_e1 = [None]
    prev_e2 = [None]
    prev_atok = [None]
    qi_guard = [None]
    for s in range(1, NSTRIP):
        def strip(s=s):
            nK = s // 4 + 1
            if qi_guard[0] is not None:
                P.waits(("sync",), [qi_guard[0]])
            tok = ld_qi.issue(qi_l[:, :, :], a["qiT_d"][s])
            tokc = ld_qi.issue(cq_l[:, :, :], a["cqd"][s])
            P.waits(("tensor",), [tok, tokc])
            emit_I(s, nK)
            emit_topk(Ibuf[:, 0:nK * 512], 4, 0)
            cvt = P.opt("vector", lambda e: e.tensor_copy(tidxi[:, 0:32], tidx[:, 0:32]))
            if ksel_guard[0] is not None:
                P.waits(("gpsimd",), [ksel_guard[0], ksel_guard2[0]])
            P.waits(("gpsimd",), [cvt])
            emit_gather_rotate(0, 32, 2)
            qtok = emit_qside(s, with_band=True)
            for h in range(8):
                def headp(h=h):
                    htok = emit_qhat(s, h)
                    P.waits(("vector",), [htok])
                    qx = emit_qext(h, usel)
                    ct2 = P.opt("scalar", lambda e: e.copy(atbf[:, :], qhat[:, :]))
                    P.waits(("tensor",), [ct2])
                    for c2 in range(2):
                        pt = pT0 if c2 == 0 else pT1
                        pt.tp_into(atbf[:, ts(c2, 128)], ident[:, :])
                        pt.consume("scalar", lambda e, p, _c=c2: e.copy(
                            qhatT[:, _c, :], p[:, 0:128]))
                    P.waits(("tensor",), [(pT0.free, pT0.freed_target),
                                          (pT1.free, pT1.freed_target)])
                    ub = P.opt("vector", lambda e, _h=h: e.tensor_copy(
                        ttrj[:, 0:32], uband[:, _h, :]))
                    P.waits(("tensor",), [ub])
                    pT0.tp_into(ttrj[:, 0:32], ident[:, :])
                    ubtok = pT0.consume("vector", lambda e, p: e.tensor_copy(
                        ubT[:, :], p[0:32, 0:128]))
                    P.waits(("tensor",), [ubtok])
                    win0 = (s - 1) * 128
                    pC.mm_group([(pC.ap[:, 0:256], qhatT[:, 0, :],
                                  ckvT[:, 0, win0:win0 + 256]),
                                 (pC.ap[:, 0:256], qhatT[:, 1, :],
                                  ckvT[:, 1, win0:win0 + 256]),
                                 (pC.ap[:, 0:256], ubT[:, :], vT[:, win0:win0 + 256])])
                    if prev_e1[0] is not None:
                        P.waits(("vector",), [prev_e1[0], prev_e2[0]])
                    btok = pC.consume("vector", lambda e, p: e.tensor_add(
                        sbband[:, :], p[:, 0:256], bandm[:, :]))
                    sstok = emit_headsel_scores(h, selS[:, :], 32)
                    P.op("vector", lambda e: e.reduce_max(negm[:, :], sbband[:, :], axis=AXX))
                    P.waits(("vector",), [sstok])
                    P.op("vector", lambda e: e.reduce_max(m2t[:, :], selS[:, :], axis=AXX))
                    P.op("vector", lambda e: e.tensor_max(negm[:, :], negm[:, :], m2t[:, :]))
                    mtok = P.opt("vector", lambda e: e.tensor_scalar_mul(
                        negm[:, :], negm[:, :], -1.0))
                    P.waits(("scalar",), [mtok])
                    e1 = P.opt("scalar", lambda e: e.activation(
                        Pb[:, :], sbband[:, :], AF.Exp, bias=negm[:, :],
                        accum_out=l1t[:, :]))
                    e2 = P.opt("scalar", lambda e: e.activation(
                        Ps[:, :], selS[:, :], AF.Exp, bias=negm[:, :],
                        accum_out=l2t[:, :]))
                    prev_e1[0], prev_e2[0] = e1, e2
                    P.waits(("vector",), [e1, e2])
                    P.op("vector", lambda e: e.tensor_add(l1t[:, :], l1t[:, :], l2t[:, :]))
                    P.op("vector", lambda e: e.reciprocal(rlt[:, :], l1t[:, :]))
                    P.op("vector", lambda e: e.tensor_scalar_mul(Pb[:, :], Pb[:, :], rlt[:, :]))
                    pstok = P.opt("vector", lambda e: e.tensor_scalar_mul(
                        Ps[:, :], Ps[:, :], rlt[:, :]))
                    P.waits(("tensor",), [pstok])
                    for c2 in range(2):
                        pt = pT0 if c2 == 0 else pT1
                        pt.tp_into(Pb[:, ts(c2, 128)], ident[:, :])
                        pt.consume("scalar", lambda e, p, _c=c2: e.copy(
                            ptb[_c][:, :], p[:, 0:128]))
                    P.waits(("tensor",), [(pT0.free, pT0.freed_target),
                                          (pT1.free, pT1.freed_target)])
                    pD.mm_group([(pD.ap[:, 0:256], ptb[0][:, :], cknat[:, s - 1, 0:256]),
                                 (pD.ap[:, 0:256], ptb[1][:, :], cknat[:, s, 0:256])])
                    psb = Ps[:, :].unsqueeze(2).to_broadcast([128, 32, 256])
                    P.op("vector", lambda e, _p=psb: e.tensor_tensor(
                        ttq[:, :, :], ksel[:, :, 0:256], _p, ALU.mult))
                    tfl = ttq[:, :, :].rearrange("p j c -> p (j c)")
                    for w in (4096, 2048, 1024, 512):
                        P.op("vector", lambda e, _w=w, _f=tfl: e.tensor_add(
                            _f[:, 0:_w], _f[:, 0:_w], _f[:, _w:2 * _w]))
                    P.op("vector", lambda e, _f=tfl: e.tensor_add(
                        asel[:, :], _f[:, 0:256], _f[:, 256:512]))
                    atok = pD.consume("vector", lambda e, p: e.tensor_add(
                        atot[:, :], p[:, 0:256], asel[:, :]))
                    P.waits(("vector",), [atok])
                    emit_opath(h, s, atot[:, :], in_tok=atok)
                headp(h)
            ksel_guard[0] = P.opt("vector", lambda e: e.nop(), drain=False)
            ksel_guard2[0] = ksel_guard[0]
            qi_guard[0] = (pB.full, pB.fills)
        strip(s)

    def strip0():
        s96_tok = [None]
        if qi_guard[0] is not None:
            P.waits(("sync",), [qi_guard[0]])
        tok = ld_qi.issue(qi_l[:, :, :], a["qiT_d"][0])
        tokc = ld_qi.issue(cq_l[:, :, :], a["cqd"][0])
        P.waits(("tensor",), [tok, tokc])
        for hh in range(4):
            def ih(hh=hh):
                bk = pC if hh % 2 == 0 else pD
                hb = (hh % 2) * 64
                bk.mm_group([(None, qi_l[hb:hb + 64, hh // 2, :], kiT[hb:hb + 64, 0:512])])
                if ri_tok[0] is not None:
                    P.waits(("scalar",), [ri_tok[0]])
                stok = bk.consume("scalar", lambda e, p: e.activation(
                    relu_t[:, :], p[:, :], AF.Relu))
                P.waits(("vector",), [stok])
                if hh == 0:
                    ri_tok[0] = P.opt("vector", lambda e: e.tensor_scalar_mul(
                        Ibuf[:, 0:512], relu_t[:, :], wrep4[:, 0:1]))
                else:
                    ri_tok[0] = P.opt("vector", lambda e, _h=hh: e.scalar_tensor_tensor(
                        Ibuf[:, 0:512], relu_t[:, :], wrep4[:, _h:_h + 1],
                        Ibuf[:, 0:512], ALU.mult, ALU.add))
            ih(hh)
        P.op("vector", lambda e: e.tensor_add(sent[:, :], Ibuf[:, 0:192],
                                              s0add[:, :]))
        emit_topk(sent[:, :], 12, 0)
        cvt = P.opt("vector", lambda e: e.tensor_copy(tidxi[:, 0:96], tidx[:, 0:96]))
        if ksel_guard[0] is not None:
            P.waits(("gpsimd",), [ksel_guard[0], ksel_guard2[0]])
        P.waits(("gpsimd",), [cvt])
        qtok = emit_qside(0, with_band=False)
        for b in range(3):
            def chunkb(b=b):
                if b > 0 and ksel_guard[0] is not None:
                    P.waits(("gpsimd",), [ksel_guard[0]])
                emit_gather_rotate(32 * b, 32, b)
                for h in range(8):
                    def hsc(h=h, b=b):
                        htok = emit_qhat(0, h)
                        P.waits(("vector",), [htok, qtok])
                        qx = emit_qext(h, usel)
                        s96_tok[0] = emit_headsel_scores(
                            h, s96[:, h, 32 * b:32 * b + 32], 32)
                    hsc(h, b)
                gd = P.opt("vector", lambda e: e.nop(), drain=False)
                ksel_guard[0] = gd
                ksel_guard2[0] = gd
            chunkb(b)
        P.waits(("vector",), [s96_tok[0]])
        p96_tok = [None]
        for h in range(8):
            def hsm(h=h):
                P.op("vector", lambda e, _h=h: e.reduce_max(negm[:, :], s96[:, _h, :],
                                                            axis=AXX))
                mtok = P.opt("vector", lambda e: e.tensor_scalar_mul(
                    negm[:, :], negm[:, :], -1.0))
                P.waits(("scalar",), [mtok])
                e1 = P.opt("scalar", lambda e, _h=h: e.activation(
                    P96[:, _h, :], s96[:, _h, :], AF.Exp, bias=negm[:, :],
                    accum_out=l1t[:, :]))
                P.waits(("vector",), [e1])
                P.op("vector", lambda e: e.reciprocal(rlt[:, :], l1t[:, :]))
                p96_tok[0] = P.opt("vector", lambda e, _h=h: e.tensor_scalar_mul(
                    P96[:, _h, :], P96[:, _h, :], rlt[:, :]))
            hsm(h)
        P.waits(("gpsimd",), [p96_tok[0]])
        first = [True] * 8
        for b in range(3):
            def avb(b=b):
                P.waits(("gpsimd",), [ksel_guard[0], ksel_guard2[0]])
                toks = []
                for j in range(32):
                    toks.append(gsel.issue_gather(
                        ksel[:, j, :], a["ckd"], tidxi[:, 32 * b + j:32 * b + j + 1],
                        None))
                P.waits(("vector", "gpsimd"), toks)
                for h in range(8):
                    def hav(h=h, b=b):
                        pvb = P96[:, h, 32 * b:32 * b + 32].unsqueeze(2).to_broadcast(
                            [128, 32, 256])
                        P.op("vector", lambda e, _p=pvb: e.tensor_tensor(
                            ttq[:, :, :], ksel[:, :, 0:256], _p, ALU.mult))
                        tfl = ttq[:, :, :].rearrange("p j c -> p (j c)")
                        for w in (4096, 2048, 1024, 512):
                            P.op("vector", lambda e, _w=w, _f=tfl: e.tensor_add(
                                _f[:, 0:_w], _f[:, 0:_w], _f[:, _w:2 * _w]))
                        if first[h]:
                            P.op("vector", lambda e, _h=h, _f=tfl: e.tensor_add(
                                a8[:, _h, :], _f[:, 0:256], _f[:, 256:512]))
                            first[h] = False
                        else:
                            P.op("vector", lambda e, _f=tfl: e.tensor_add(
                                asel[:, :], _f[:, 0:256], _f[:, 256:512]))
                            P.op("vector", lambda e, _h=h: e.tensor_add(
                                a8[:, _h, :], a8[:, _h, :], asel[:, :]))
                    hav(h, b)
                g1 = P.opt("vector", lambda e: e.nop(), drain=False)
                ksel_guard[0] = g1
                ksel_guard2[0] = g1
            avb(b)
        P.waits(("vector",), [ksel_guard2[0]])
        for h in range(8):
            emit_opath(h, 0, a8[:, h, :], in_tok=ksel_guard2[0])
    strip0()

    P.waits(("sync",), st_ot.tokens())
    at.close()
    op_sc = ExitStack()
    sbO = lambda n, s, d: op_sc.enter_context(nc.sbuf_tensor(n, s, d))
    woutsb = sbO("woutsb", [128, 4, DM], BF16)
    otl = [sbO(f"otl{i}", [128, 4, 128], BF16) for i in range(2)]
    xs2 = [sbO(f"xs2_{i}", [128, DM], F32) for i in range(2)]
    ys2 = [sbO(f"ys2_{i}", [128, DM], F32) for i in range(2)]
    s_cstg = P.S("cstg", sw=True)
    P.dma("gpsimd", s_cstg, woutsb[:, :, :], a["wout"].rearrange("(c p) h -> p c h", p=128))
    P.waits(("tensor",), [(s_cstg, s_cstg.n)])
    ld_x2 = DStream(P, "ldx2", "sync", 2)
    ld_ot = DStream(P, "ldot", "sync", 2)
    st_y2 = DStream(P, "sty2", "gpsimd", 2)
    y_tok2 = [None, None]
    x_free2 = [None, None]
    ot_free2 = [None, None]
    for s in range(NSTRIP):
        def ostrip(s=s):
            xb, yb, ob_ = xs2[s % 2], ys2[s % 2], otl[s % 2]
            if x_free2[s % 2] is not None:
                P.waits(("sync",), [x_free2[s % 2]])
            tok = ld_x2.issue(xb[:, :], a["x"][ts(s, 128), :])
            if ot_free2[s % 2] is not None:
                P.waits(("sync",), [ot_free2[s % 2]])
            tok2 = ld_ot.issue(ob_[:, :, :],
                               a["otd"][:, :, ts(s, 128)].rearrange("c p t -> p c t"))
            P.waits(("tensor",), [tok2])
            if y_tok2[s % 2] is not None:
                P.waits(("vector",), [y_tok2[s % 2]])
            last_v = None
            for dh in range(2):
                bk = pA if dh == 0 else pB
                bk.mm_group([(None, ob_[:, cc, :], woutsb[:, cc, ts(dh, 512)])
                             for cc in range(4)])
                if dh == 0:
                    P.waits(("vector",), [tok])
                last_v = bk.consume("vector", lambda e, p, _d=dh, _y=yb, _x=xb:
                                    e.scalar_tensor_tensor(
                                        _y[:, ts(_d, 512)], p[:, :], a["scale"],
                                        _x[:, ts(_d, 512)], ALU.mult, ALU.add))
            x_free2[s % 2] = last_v
            ot_free2[s % 2] = (pB.full, pB.fills)
            P.waits(("gpsimd",), [last_v])
            y_tok2[s % 2] = st_y2.issue(a["y"][ts(s, 128), :], yb[:, :])
        ostrip(s)
    P.waits(("gpsimd",), st_y2.tokens())
    op_sc.close()


class NCProxy:
    def __init__(self, nc, tag):
        self._nc, self._tag = nc, tag

    def sbuf_tensor(self, name, shape, dtype, **kw):
        return self._nc.sbuf_tensor(self._tag + name, shape, dtype, **kw)

    def dram_tensor(self, name, shape, dtype, **kw):
        return self._nc.dram_tensor(self._tag + name, shape, dtype, **kw)

    def __getattr__(self, k):
        return getattr(self._nc, k)


def build_fused(C, stages=("f0a", "a0", "f0b", "f1a", "a1", "f1b")):
    nc = bass.Bass("TRN2", target_bir_lowering=False)
    dr = lambda n, s, d, k="ExternalInput": nc.dram_tensor(n, s, d, kind=k).ap()
    ext = dict(
        x=dr("x", [T, DM], F32),
        y=dr("y", [T, DM], F32, "ExternalOutput"),
        onesf=dr("onesf", [1, 128], F32), identbf=dr("identbf", [128, 128], BF16),
        iota8=dr("iota8", [128, 8], F32),
        onesbf=dr("onesbf", [1, 128], BF16), identf=dr("identf", [128, 128], F32),
        sw1T=dr("sw1T", [DM, 512], F32), sw2T=dr("sw2T", [DM, 512], F32),
        sw3T=dr("sw3T", [512, DM], F32),
        sb1=dr("sb1", [1, 512], F32), sb2=dr("sb2", [1, 512], F32), sb3=dr("sb3", [1, DM], F32),
        slng=dr("slng", [1, 512], F32), slnb=dr("slnb", [128, 4], F32),
        sgeg=dr("sgeg", [DM, 16], F32), sgegb=dr("sgegb", [1, 16], F32),
        expb=dr("expb", [1, 8], F32),
        ew1T=dr("ew1T", [8, DM, 512], F32), ew2T=dr("ew2T", [8, DM, 512], F32),
        ew3T=dr("ew3T", [8, 512, DM], F32),
        eb1=dr("eb1", [8, 1, 512], F32), eb2=dr("eb2", [8, 1, 512], F32),
        eb3=dr("eb3", [8, 1, DM], F32),
        selg=dr("selg", [1, 512], F32), selb=dr("selb", [128, 4], F32),
        midx=dr("midx", [128, C // 128], I32),
        mmask=dr("mmask", [128, C // 128], F32),
        wqkvh=dr("wqkvh", [8, DM, 320], F32), wvc=dr("wvc", [8, DM, 64], F32),
        woutT=dr("woutT", [512, DM], F32),
        t1t_mask=dr("t1t_mask", [128, 2, 256], BF16),
        lat_mask=dr("lat_mask", [128, 4, 512], BF16),
        causal_mask=dr("causal_mask", [128, 4, 512], BF16),
        wr=dr("wr", [DM, 8], F32), wg=dr("wg", [8, DM, 2048], F32),
        wu=dr("wu", [8, DM, 2048], F32), wd=dr("wd", [8, 2048, DM], F32),
        trilow=dr("trilow", [128, 128], BF16),
        pidx=dr("pidx", [128, 1], F32),
        wdkvkr=dr("wdkvkr", [DM, 288], F32), wdq=dr("wdq", [DM, 512], F32),
        wqr=dr("wqr", [512, 256], F32),
        widxq=dr("widxq", [DM, 256], F32), widxk=dr("widxk", [DM, 64], F32),
        wuqT=dr("wuqT", [512, 512], F32), wukT=dr("wukT", [512, 256], F32),
        wuv=dr("wuv", [256, 512], F32), wout=dr("wout", [512, DM], F32),
        idxw=dr("idxw", [1, 4], F32), rawdelta=dr("rawdelta", [1, 16], F32),
        cosT=dr("cosT", [128, 32, 16], F32), sinT=dr("sinT", [128, 32, 16], F32),
        cosB=dr("cosB", [128, 32, 16], F32), sinB=dr("sinB", [128, 32, 16], F32),
        cosS=dr("cosS", [128, 32, 16], F32), sinS=dr("sinS", [128, 32, 16], F32),
        cosR=dr("cosR", [128, 3, 32, 16], F32), sinR=dr("sinR", [128, 3, 32, 16], F32),
        bandmask=dr("bandmask", [128, 256], BF16),
        imaskd=dr("imaskd", [128, 4, 512], BF16),
        imaskp=dr("imaskp", [128, 512], BF16),
        s0add=dr("s0add", [128, 192], F32),
    )
    scr = dict(
        f=dict(xnat_d=dr("s_xnat", [T, DM], BF16, "Internal"),
               shout_d=dr("s_shout", [T, DM], BF16, "Internal")),
        g=dict(xnat_d=dr("s_xnat1", [T, DM], BF16, "Internal"),
               tix_d=dr("s_tix", [8 * CE, 2], F32, "Internal"),
               out_d=dr("s_outd", [8 * CE, DM], F32, "Internal"),
               ptd=dr("s_ptd", [T, 4], F32, "Internal")),
        a0=dict(lat_d=dr("s_lat", [len(LAT_SLOTS) * 128, 512], BF16, "Internal"),
                otd=dr("s_otd0", [4, 128, T], BF16, "Internal")),
        a1=dict(qiT_d=dr("s_qiT", [NSTRIP, 128, 2, 128], BF16, "Internal"),
                cqd=dr("s_cqd", [NSTRIP, 128, 4, 128], BF16, "Internal"),
                ckd=dr("s_ckd", [T, 288], BF16, "Internal"),
                otd=dr("s_otd1", [4, 128, T], BF16, "Internal")),
    )
    n = len(stages)
    acts = [ext["x"]] + [dr(f"act{i}", [T, DM], F32, "Internal")
                         for i in range(1, n)] + [ext["y"]]
    octx = ExitStack()
    P = Prog(nc, octx)
    P.barsc = octx.enter_context(nc.sbuf_tensor("barsc", [1, 8], F32))
    cs = P._pool("_binit")
    P.op("vector", lambda e: e.memset(P.barsc[:, :], 0.0), sem=cs, drain=False)
    P.waits(("scalar",), [(cs, cs.n)])
    for i, tag in enumerate(stages):
        P.new_stage()
        P.stage_label = tag
        sctx = ExitStack()
        ncp = NCProxy(nc, tag + "_")
        a = dict(ext)
        a["x"], a["y"] = acts[i], acts[i + 1]
        if tag.startswith("f0"):
            a.update(scr["f"]); a.update(C=C, scale=0.5)
            emit_l0ffn(ncp, sctx, P, a)
        elif tag == "a0":
            a.update(scr["a0"]); a["scale"] = 1.0
            emit_l0attn(ncp, sctx, P, a)
        elif tag.startswith("f1"):
            a.update(scr["g"]); a["scale"] = 0.5
            emit_l1ffn_sparse(ncp, sctx, P, a)
        else:
            a.update(scr["a1"]); a["scale"] = 1.0
            emit_l1attn(ncp, sctx, P, a)
        sctx.close()
        if i < n - 1:
            P.barrier()
    P.build()
    octx.close()
    return nc


def _f32(x):
    return np.ascontiguousarray(x, dtype=np.float32)


def l0ffn_host_inputs(ww):
    sgeg = np.zeros((DM, 16), np.float32)
    sgeg[:, 0] = _f32(ww["L0_ffn_sg_w"])[0]
    sgeg[:, 1:9] = _f32(ww["L0_ffn_eg_w"]).T
    sgegb = np.zeros((1, 16), np.float32)
    sgegb[0, 0] = _f32(ww["L0_ffn_sg_b"])[0]
    sgegb[0, 1:9] = _f32(ww["L0_ffn_eg_b"])
    return dict(
        sw1T=_f32(np.asarray(ww["L0_ffn_sw1"]).T), sw2T=_f32(np.asarray(ww["L0_ffn_sw2"]).T),
        sw3T=_f32(np.asarray(ww["L0_ffn_sw3"]).T),
        sb1=_f32(ww["L0_ffn_sb1"]).reshape(1, 512), sb2=_f32(ww["L0_ffn_sb2"]).reshape(1, 512),
        sb3=_f32(ww["L0_ffn_sb3"]).reshape(1, DM),
        slng=_f32(ww["L0_ffn_s_ln_g"]).reshape(1, 512),
        slnb=_f32(np.asarray(ww["L0_ffn_s_ln_b"]).reshape(4, 128).T),
        sgeg=sgeg, sgegb=sgegb,
        expb=_f32(ww["L0_ffn_exp_bias"]).reshape(1, 8),
        ew1T=_f32(np.transpose(np.asarray(ww["L0_ffn_ew1"]), (0, 2, 1))),
        ew2T=_f32(np.transpose(np.asarray(ww["L0_ffn_ew2"]), (0, 2, 1))),
        ew3T=_f32(np.transpose(np.asarray(ww["L0_ffn_ew3"]), (0, 2, 1))),
        eb1=_f32(ww["L0_ffn_eb1"]).reshape(8, 1, 512),
        eb2=_f32(ww["L0_ffn_eb2"]).reshape(8, 1, 512),
        eb3=_f32(ww["L0_ffn_eb3"]).reshape(8, 1, DM),
        selg=_f32(ww["L0_ffn_sel_ln_g"]).reshape(1, 512),
        selb=_f32(np.asarray(ww["L0_ffn_sel_ln_b"]).reshape(4, 128).T),
        onesf=np.ones((1, 128), np.float32),
        identbf=np.eye(128, dtype=np.float32).astype(BF),
        iota8=np.tile(np.arange(8, dtype=np.float32), (128, 1)),
    )


def l0ffn_moe_tables(index_b, index_t):
    idx_per_core = []
    for c in range(NCORES):
        own = np.where(np.asarray(index_b) == c)[0]
        idx_per_core.append(np.asarray(index_t)[own].astype(np.int64))
    mx = max(max((len(v) for v in idx_per_core)), 1)
    C = int(np.ceil(mx / 128.0)) * 128
    tables, masks = [], []
    for v in idx_per_core:
        used = set(int(i) for i in v)
        safe = next(i for i in range(T) if i not in used)
        t = np.full((C,), safe, np.int64)
        t[: len(v)] = v
        m = np.zeros((C,), np.float32)
        m[: len(v)] = 1.0
        tables.append(np.ascontiguousarray(t.reshape(C // 128, 128).T).astype(np.int32))
        masks.append(np.ascontiguousarray(m.reshape(C // 128, 128).T).astype(np.float32))
    return C, tables, masks


def run_l0ffn(x_full, ww, nc_prog=None):
    C, tables, masks = l0ffn_moe_tables(ww["L0_ffn_index_b"], ww["L0_ffn_index_t"])
    nc = nc_prog if nc_prog is not None else build_l0ffn(C)
    base = l0ffn_host_inputs(ww)
    in_maps = []
    for c in range(NCORES):
        m = dict(base)
        m["x"] = _f32(x_full[c])
        m["midx"] = tables[c]
        m["mmask"] = masks[c]
        in_maps.append(m)
    res = run_bass_kernel_spmd(nc, in_maps, core_ids=list(range(NCORES)))
    return np.stack([r["y"] for r in res.results], axis=0)


def l0attn_host_inputs(ww):
    W_qkv = _f32(ww["L0_attn_W_qkv"])
    W_out = _f32(ww["L0_attn_W_out"])
    wqkvh = np.zeros((8, DM, 320), np.float32)
    wvc = np.zeros((8, DM, 64), np.float32)
    for h in range(8):
        for j in range(320):
            s_, d_ = j // 64, j % 64
            wqkvh[h, :, j] = W_qkv[s_ * 512 + h * 64 + d_, :]
        for d_ in range(64):
            wvc[h, :, d_] = W_qkv[5 * 512 + h * 64 + d_, :]
    iota_i = np.arange(128)
    diag_keep = (iota_i[None, :] >= iota_i[:, None]).astype(np.float32)
    t1m = np.zeros((128, 2, 256), np.float32)
    t1m[:, 0, 0:128] = diag_keep
    t1m[:, 0, 128:256] = 1.0
    t1m[:, 1, 128:256] = diag_keep
    latm = np.zeros((128, 4, 512), np.float32)
    kk = np.arange(512)
    for m in range(4):
        latm[:, m, :] = ((128 * m + iota_i)[:, None] > kk[None, :]).astype(np.float32)
    caum = np.zeros((128, 4, 512), np.float32)
    for m in range(4):
        caum[:, m, :] = np.where(kk[None, :] <= (128 * m + iota_i)[:, None], 0.0, -1e38)
    return dict(
        wqkvh=wqkvh, wvc=wvc, woutT=np.ascontiguousarray(W_out.T),
        identbf=np.eye(128, dtype=np.float32).astype(BF),
        t1t_mask=t1m.astype(BF),
        lat_mask=latm.astype(BF),
        causal_mask=caum.astype(BF),
    )


def l1ffn_host_inputs(ww):
    return dict(
        wr=_f32(ww["L1_ffn_router_kernel_DE"]),
        wg=_f32(ww["L1_ffn_kernel_gating_EDF"]),
        wu=_f32(ww["L1_ffn_kernel_up_EDF"]),
        wd=_f32(ww["L1_ffn_kernel_down_EFD"]),
        onesf=np.ones((1, 128), np.float32),
        identbf=np.eye(128, dtype=np.float32).astype(BF),
        onesbf=np.ones((1, 128), np.float32).astype(BF),
        identf=np.eye(128, dtype=np.float32),
        iota8=np.tile(np.arange(8, dtype=np.float32), (128, 1)),
        trilow=np.triu(np.ones((128, 128), np.float32), 1).astype(BF),
        pidx=np.arange(128, dtype=np.float32).reshape(128, 1),
    )


def l1attn_host_inputs(ww):
    NH, DH, DR, ROPE = 8, 64, 32, 10000.0
    SCL = (DH + DR) ** -0.5
    theta = 1.0 / ROPE ** (2.0 * np.arange(16) / 32.0)
    tpos = np.arange(T)

    def tab(ang, scl=1.0):
        return np.ascontiguousarray(
            (ang * scl).reshape(32, 128, 16).transpose(1, 0, 2)).astype(np.float32)

    angT = tpos[:, None] * theta[None, :]
    angB = (2 * tpos - 63)[:, None] * theta[None, :]
    cosT, sinT = tab(np.cos(angT), SCL), tab(np.sin(angT), SCL)
    cosB, sinB = tab(np.cos(angB), SCL), tab(np.sin(angB), SCL)
    cosS, sinS = tab(np.cos(angT)), tab(np.sin(angT))
    ranks = np.arange(96)
    angR = ranks[:, None] * theta[None, :]
    cosR = np.tile(np.cos(angR).reshape(1, 3, 32, 16), (128, 1, 1, 1)).astype(np.float32)
    sinR = np.tile(np.sin(angR).reshape(1, 3, 32, 16), (128, 1, 1, 1)).astype(np.float32)
    p = np.arange(128)
    wv = np.arange(256)
    bandmask = np.where((wv[None, :] >= p[:, None] + 65) & (wv[None, :] <= p[:, None] + 128),
                        0.0, -1e38).astype(np.float32)
    kf = np.arange(512)
    imaskd = np.zeros((128, 4, 512), np.float32)
    for m in range(4):
        thr = np.maximum(0, 128 * m + p - 63)
        imaskd[:, m, :] = np.where(kf[None, :] >= thr[:, None], -1e38, 0.0)
    imaskp = np.where((p[:, None] < 63) & (kf[None, :] >= 449 + p[:, None]), -1e38, 0.0
                      ).astype(np.float32)
    sv = np.arange(192)
    s0loc = (((sv[None, :] >= np.maximum(0, p - 63)[:, None]) & (sv[None, :] <= p[:, None]))
             ).astype(np.float32)
    s0cau = (sv[None, :] > p[:, None]).astype(np.float32)
    s0add = (1e38 * s0loc - 1e38 * s0cau).astype(np.float32)
    return dict(
        wdkvkr=np.ascontiguousarray(np.concatenate(
            [_f32(ww["L1_attn_W_dkv"]), _f32(ww["L1_attn_W_kr"])], 1)),
        wdq=_f32(ww["L1_attn_W_dq"]), wqr=_f32(ww["L1_attn_W_qr"]),
        widxq=_f32(ww["L1_attn_idx_wq"]), widxk=_f32(ww["L1_attn_idx_wk"]),
        wuqT=np.ascontiguousarray(_f32(ww["L1_attn_W_uq"]).T),
        wukT=np.ascontiguousarray(_f32(ww["L1_attn_W_uk"]).T),
        wuv=_f32(ww["L1_attn_W_uv"]), wout=_f32(ww["L1_attn_W_out"]),
        idxw=_f32(ww["L1_attn_idx_w"]).reshape(1, 4),
        rawdelta=_f32(ww["L1_attn_raw_delta"]).reshape(1, 16),
        cosT=cosT, sinT=sinT, cosB=cosB, sinB=sinB, cosS=cosS, sinS=sinS,
        cosR=cosR, sinR=sinR,
        identbf=np.eye(128, dtype=np.float32).astype(BF),
        onesf=np.ones((1, 128), np.float32),
        bandmask=bandmask.astype(BF),
        imaskd=imaskd.astype(BF),
        imaskp=imaskp.astype(BF),
        s0add=s0add,
    )


def _launch(nc, base, x_full, extra=None):
    in_maps = []
    for c in range(NCORES):
        m = dict(base)
        m["x"] = np.ascontiguousarray(x_full[c], dtype=np.float32)
        if extra is not None:
            m.update(extra[c])
        in_maps.append(m)
    res = run_bass_kernel_spmd(nc, in_maps, core_ids=list(range(NCORES)))
    return np.stack([r["y"] for r in res.results], axis=0)


def kernel(**inputs):
    x = np.asarray(inputs["L0_attn_x"], dtype=np.float32)
    C, tables, masks = l0ffn_moe_tables(np.asarray(inputs["L0_ffn_index_b"]),
                                        np.asarray(inputs["L0_ffn_index_t"]))
    base = {}
    base.update(l0ffn_host_inputs(inputs))
    base.update(l0attn_host_inputs(inputs))
    base.update(l1ffn_host_inputs(inputs))
    base.update(l1attn_host_inputs(inputs))
    nc = build_fused(C)
    in_maps = []
    for c in range(NCORES):
        m = dict(base)
        m["x"] = np.ascontiguousarray(x[c], dtype=np.float32)
        m["midx"] = tables[c]
        m["mmask"] = masks[c]
        in_maps.append(m)
    res = run_bass_kernel_spmd(nc, in_maps, core_ids=list(range(NCORES)))
    y = np.stack([r["y"] for r in res.results], axis=0)
    return np.ascontiguousarray(y, dtype=np.float32)
